# version 01
# baseline (speedup 1.0000x reference)
import numpy as np
import ml_dtypes
import concourse.bass as bass
import concourse.mybir as mybir
from concourse.bass_utils import run_bass_kernel_spmd

F32 = mybir.dt.float32
BF16 = mybir.dt.bfloat16
I32 = mybir.dt.int32
AF = mybir.ActivationFunctionType
ALU = mybir.AluOpType
bfloat16 = ml_dtypes.bfloat16

NCORES = 8
D = 1024
S = 4096
KB = D // 128
EPS = 1e-5

SAME_ENGINE_EDGES = False
SAME_ENGINE_DRAIN = True


class Prog:

    def __init__(self, nc):
        self.nc = nc
        self.steps = []
        self.last_w = {}
        self.readers = {}

    def step(self, eng, fn, reads=(), writes=(), dma=False, tag=""):
        deps = set()
        for r in reads:
            w = self.last_w.get(r)
            if w is not None:
                deps.add(w)
        for r in writes:
            w = self.last_w.get(r)
            if w is not None:
                deps.add(w)
            deps.update(self.readers.get(r, ()))
        i = len(self.steps)
        self.steps.append(
            dict(i=i, eng=eng, fn=fn, deps=deps, dma=bool(dma), tag=tag, inc=False)
        )
        for r in reads:
            self.readers.setdefault(r, set()).add(i)
        for r in writes:
            self.last_w[r] = i
            self.readers[r] = set()
        return i

    def emit(self):
        nc = self.nc
        for st in self.steps:
            st["stream"] = st["eng"] + (".d" if st["dma"] else "")
        for st in self.steps:
            if st["dma"]:
                st["inc"] = True
            for d in st["deps"]:
                ds = self.steps[d]
                if ds["stream"] == st["stream"] and not ds["dma"]:
                    if st["eng"] == "tensor" or not SAME_ENGINE_EDGES:
                        continue
                ds["inc"] = True
        NLANES = 8
        streams = {}
        for st in self.steps:
            streams.setdefault(st["stream"], []).append(st)
        for s, lst in streams.items():
            if s.endswith(".d"):
                lane_rank = [0] * NLANES
                i = 0
                for st in lst:
                    st["lane"] = i % NLANES
                    lane_rank[st["lane"]] += 1
                    st["rank"] = lane_rank[st["lane"]]
                    i += 1
            else:
                r = 0
                for st in lst:
                    if st["inc"]:
                        r += 1
                    st["rank"] = r
        import contextlib

        with contextlib.ExitStack() as ex:
            sems = {}
            for s, lst in streams.items():
                if s.endswith(".d"):
                    sems[s] = (
                        [ex.enter_context(nc.semaphore(f"sem_{s.replace('.', '_')}_{l}"))
                         for l in range(NLANES)],
                        16,
                    )
                elif any(st["inc"] for st in lst):
                    sems[s] = (
                        ex.enter_context(nc.semaphore("sem_" + s.replace(".", "_"))),
                        1,
                    )
            by_eng = {}
            for st in self.steps:
                by_eng.setdefault(st["eng"], []).append(st)
            steps_all = self.steps
            with nc.Block() as block:
                for eng_name in list(by_eng.keys()):
                    lst = by_eng[eng_name]

                    def make_body(lst):
                        def body(eng):
                            watermark = {}
                            last_drain = -1
                            for st in lst:
                                if SAME_ENGINE_DRAIN and st["eng"] != "tensor" and not st["dma"]:
                                    need = any(
                                        steps_all[d]["stream"] == st["stream"]
                                        and not steps_all[d]["dma"]
                                        and d > last_drain
                                        for d in st["deps"]
                                    )
                                    if need:
                                        eng.drain(fusable=False)
                                        last_drain = st["i"]
                                for d in sorted(st["deps"]):
                                    ds = steps_all[d]
                                    if ds["stream"] == st["stream"] and not ds["dma"] and (st["eng"] == "tensor" or not SAME_ENGINE_EDGES):
                                        continue
                                    sem, unit = sems[ds["stream"]]
                                    if ds["dma"]:
                                        sem = sem[ds["lane"]]
                                        wkey = (ds["stream"], ds["lane"])
                                    else:
                                        wkey = ds["stream"]
                                    val = ds["rank"] * unit
                                    if watermark.get(wkey, -1) >= val:
                                        continue
                                    eng.wait_ge(sem, val)
                                    watermark[wkey] = val
                                if st["dma"] and st["rank"] > 1:
                                    sem, unit = sems[st["stream"]]
                                    sem = sem[st["lane"]]
                                    wkey = (st["stream"], st["lane"])
                                    val = (st["rank"] - 1) * unit
                                    if watermark.get(wkey, -1) < val:
                                        eng.wait_ge(sem, val)
                                        watermark[wkey] = val
                                ins = st["fn"](eng)
                                if st["inc"]:
                                    sem, unit = sems[st["stream"]]
                                    if st["dma"]:
                                        sem = sem[st["lane"]]
                                    ins.then_inc(sem, unit)
                            return None

                        return body

                    getattr(block, eng_name)(make_body(lst))


class Ctx:
    def __init__(self, name, bf16_banks=()):
        nc = bass.Bass("TRN2", target_bir_lowering=False)
        self.nc = nc
        self.p = Prog(nc)
        self.name = name
        self.pb = [nc.alloc_psum_tensor(f"pb{i}", [128, 1024] if i in bf16_banks else [128, 512],
                                        BF16 if i in bf16_banks else F32) for i in range(8)]
        self.ones_bf = nc.alloc_sbuf_tensor("ones_bf", [1, 128], BF16)
        self.ones_f = nc.alloc_sbuf_tensor("ones_f", [1, 128], F32)
        self.onescol_bf = nc.alloc_sbuf_tensor("onescol_bf", [128, 1], BF16)
        self.eps_col = nc.alloc_sbuf_tensor("eps_col", [128, 1], F32)
        self.p.step("vector", lambda e: e.memset(self.ones_bf[:, :], 1.0), writes=("ones_bf",))
        self.p.step("vector", lambda e: e.memset(self.ones_f[:, :], 1.0), writes=("ones_f",))
        self.p.step("vector", lambda e: e.memset(self.onescol_bf[:, :], 1.0), writes=("onescol_bf",))
        self.p.step("vector", lambda e: e.memset(self.eps_col[:, :], EPS), writes=("eps_col",))
        self._uid = 0

    def uid(self):
        self._uid += 1
        return self._uid

    def sb(self, name, shape, dtype):
        return self.nc.alloc_sbuf_tensor(name, shape, dtype)

    def din(self, name, shape, dtype=F32):
        return self.nc.dram_tensor(name, shape, dtype, kind="ExternalInput").ap()

    def dout(self, name, shape, dtype=F32):
        return self.nc.dram_tensor(name, shape, dtype, kind="ExternalOutput").ap()

    def dscratch(self, name, shape, dtype=F32):
        return self.nc.dram_tensor(name, shape, dtype, kind="Internal").ap()

    def dma(self, queue, out_ap, in_ap, reads=(), writes=(), tag=""):
        return self.p.step(
            queue, lambda e: e.dma_start(out_ap, in_ap), reads=reads, writes=writes,
            dma=True, tag=tag,
        )

    def mm_group(self, psum_ap, psum_key, pairs, tag=""):
        n = len(pairs)
        for j, (l, r, rk) in enumerate(pairs):
            def fn(e, l=l, r=r, j=j, n=n):
                return e.matmul(psum_ap, l, r, start=(j == 0), stop=(j == n - 1))
            self.p.step("tensor", fn, reads=tuple(rk), writes=(psum_key,), tag=tag)

    def ln_fm(self, T, nb, x3, xkey, xb3, xbk, sq3, sqk, t13, t1k, out3, outkey,
              rows, rowsk, bsA, bsAk, bsB, bsBk, bbR=None, bbRk=None, bbM=None, bbMk=None,
              cast_engine="gpsimd"):
        if bbR is None:
            bbR, bbRk, bbM, bbMk = bsA, bsAk, bsB, bsBk
        """LayerNorm over the feature axis, feature-major. x3/xb3/sq3/t13/out3:
        [128, nb, T] views (out3 may be bf16 or f32). Uses 2 stats banks (row 0)
        and 2 broadcast banks (full). T <= 512."""
        assert T <= 512
        p = self.p
        p.step(cast_engine, lambda e: e.tensor_copy(xb3, x3), reads=(xkey,), writes=(xbk,))
        p.step("scalar", lambda e: e.activation(sq3, xb3, AF.Square), reads=(xbk,), writes=(sqk,))
        self.mm_group(bsA[0:1, 0:T], bsAk,
                      [(self.onescol_bf[:, :], xb3[:, i, :], (xbk, "onescol_bf")) for i in range(nb)])
        self.mm_group(bsB[0:1, 0:T], bsBk,
                      [(self.onescol_bf[:, :], sq3[:, i, :], (sqk, "onescol_bf")) for i in range(nb)])
        Dn = float(nb * 128)
        m_r = rows[0:1, 0:T]
        var_r = rows[0:1, T:2 * T]
        rstd_r = rows[0:1, 2 * T:3 * T]
        mistd_r = rows[0:1, 3 * T:4 * T]
        p.step("vector", lambda e: e.tensor_scalar(m_r, bsA[0:1, 0:T], 1.0 / Dn, None, ALU.mult), reads=(bsAk,), writes=(rowsk,))
        p.step("vector", lambda e: e.tensor_mul(var_r, m_r, m_r), reads=(rowsk,), writes=(rowsk,))
        p.step("vector", lambda e: e.scalar_tensor_tensor(var_r, bsB[0:1, 0:T], 1.0 / Dn, var_r, ALU.mult, ALU.subtract), reads=(bsBk, rowsk), writes=(rowsk,))
        p.step("scalar", lambda e: e.activation(rstd_r, var_r, AF.Sqrt, bias=self.eps_col[0:1, 0:1]), reads=(rowsk, "eps_col"), writes=(rowsk,))
        p.step("vector", lambda e: e.reciprocal(rstd_r, rstd_r), reads=(rowsk,), writes=(rowsk,))
        p.step("vector", lambda e: e.tensor_mul(mistd_r, m_r, rstd_r), reads=(rowsk,), writes=(rowsk,))
        self.mm_group(bbR[:, 0:T], bbRk, [(self.ones_f[0:1, :], rstd_r, (rowsk, "ones_f"))])
        self.mm_group(bbM[:, 0:T], bbMk, [(self.ones_f[0:1, :], mistd_r, (rowsk, "ones_f"))])
        rstd_b = bbR[:, 0:T].unsqueeze(1).broadcast_to([128, nb, T])
        mistd_b = bbM[:, 0:T].unsqueeze(1).broadcast_to([128, nb, T])
        p.step("vector", lambda e: e.tensor_mul(t13, x3, rstd_b), reads=(xkey, bbRk), writes=(t1k,))
        p.step("vector", lambda e: e.tensor_sub(out3, t13, mistd_b), reads=(t1k, bbMk), writes=(outkey,))


def fmvec(v):
    return np.ascontiguousarray(np.asarray(v, dtype=np.float32).reshape(-1, 128).T)


def build_l1_ffn(Ttot=S):
    ctx = Ctx("l1ffn")
    nc, p = ctx.nc, ctx.p
    TC = 256
    nch = Ttot // TC
    H = 4096
    HB = H // 128

    xin = ctx.din("xin_fm", [D, Ttot])
    w1 = ctx.din("w1", [D, H])
    w2 = ctx.din("w2", [H, D])
    b1f = ctx.din("b1f", [128, HB])
    b2f = ctx.din("b2f", [128, KB])
    xout = ctx.dout("xout_fm", [D, Ttot])

    stage = ctx.sb("stage", [128, 4096], F32)
    w1b = ctx.sb("w1b", [128, KB, H], BF16)
    w2b = ctx.sb("w2b", [128, HB, D], BF16)
    b1s = ctx.sb("b1s", [128, HB], F32)
    b2s = ctx.sb("b2s", [128, KB], F32)
    ctx.dma("sync", b1s[:, :], b1f, writes=("b1s",))
    ctx.dma("sync", b2s[:, :], b2f, writes=("b2s",))
    for kc in range(KB):
        ctx.dma("sync", stage[:, :], w1[kc * 128:(kc + 1) * 128, :], writes=("stage",))
        p.step("gpsimd", lambda e, kc=kc: e.tensor_copy(w1b[:, kc, :], stage[:, :]), reads=("stage",), writes=("w1b",))
    for hc in range(HB):
        ctx.dma("sync", stage[:, 0:D], w2[hc * 128:(hc + 1) * 128, :], writes=("stage",))
        p.step("gpsimd", lambda e, hc=hc: e.tensor_copy(w2b[:, hc, :], stage[:, 0:D]), reads=("stage",), writes=("w2b",))

    xch = ctx.sb("xch", [128, KB, TC], F32)
    xb = ctx.sb("xb", [128, KB, TC], BF16)
    sq = ctx.sb("sq", [128, KB, TC], BF16)
    t1 = ctx.sb("t1", [128, KB, TC], F32)
    xnb = ctx.sb("xnb", [128, KB, TC], BF16)
    hb = ctx.sb("hb", [128, HB, TC], BF16)
    yf = ctx.sb("yf", [128, KB, TC], F32)
    rows = ctx.sb("rows", [1, 2048], F32)

    xin_r = xin.rearrange("(a p) t -> p a t", p=128)
    xout_r = xout.rearrange("(a p) t -> p a t", p=128)

    for c in range(nch):
        t0 = c * TC
        ctx.dma("sync", xch.ap(), xin_r[:, :, t0:t0 + TC], writes=("xch",), tag=f"xld{c}")
        ctx.ln_fm(TC, KB, xch.ap(), "xch", xb.ap(), "xb", sq.ap(), "sq",
                  t1.ap(), "t1", xnb.ap(), "xnb", rows, "rows",
                  ctx.pb[6], "pb6", ctx.pb[7], "pb7")
        for mb in range(HB):
            bank, bk = ctx.pb[mb % 2], f"pb{mb % 2}"
            ctx.mm_group(bank[:, 0:TC], bk,
                         [(w1b[:, kc, mb * 128:(mb + 1) * 128], xnb[:, kc, :], ("w1b", "xnb")) for kc in range(KB)],
                         tag=f"fc{c}_{mb}")
            p.step("scalar", lambda e, mb=mb, bank=bank: e.activation(hb[:, mb, :], bank[:, 0:TC], AF.Gelu, bias=b1s[:, mb:mb + 1]),
                   reads=(bk, "b1s"), writes=("hb",), tag=f"gelu{c}_{mb}")
        for ob in range(KB):
            bank, bk = ctx.pb[2 + ob % 2], f"pb{2 + ob % 2}"
            ctx.mm_group(bank[:, 0:TC], bk,
                         [(w2b[:, hc, ob * 128:(ob + 1) * 128], hb[:, hc, :], ("w2b", "hb")) for hc in range(HB)],
                         tag=f"pj{c}_{ob}")
            p.step("scalar", lambda e, ob=ob, bank=bank: e.activation(yf[:, ob, :], bank[:, 0:TC], AF.Identity, bias=b2s[:, ob:ob + 1]),
                   reads=(bk, "b2s"), writes=("yf",), tag=f"bias{c}_{ob}")
        ctx.ln_fm(TC, KB, yf.ap(), "yf", xb.ap(), "xb", sq.ap(), "sq",
                  t1.ap(), "t1", t1.ap(), "t1", rows, "rows",
                  ctx.pb[6], "pb6", ctx.pb[7], "pb7")
        p.step("vector", lambda e: e.tensor_add(xch.ap(), xch.ap(), t1.ap()), reads=("xch", "t1"), writes=("xch",), tag=f"resadd{c}")
        ctx.dma("sync", xout_r[:, :, t0:t0 + TC], xch.ap(), reads=("xch",), writes=(f"xout{c}",), tag=f"xst{c}")

    p.step("gpsimd", lambda e: e.nop(), reads=tuple(f"xout{c}" for c in range(nch)), tag="final_wait")
    ctx.p.emit()
    return ctx


def l1_ffn_inmap(x_fm, w1, b1, w2, b2):
    return {
        "xin_fm": np.ascontiguousarray(x_fm),
        "w1": np.ascontiguousarray(w1), "w2": np.ascontiguousarray(w2),
        "b1f": fmvec(b1), "b2f": fmvec(b2),
    }


def build_l0_ffn(Ttot=S):
    ctx = Ctx("l0ffn", bf16_banks=(3,))
    nc, p = ctx.nc, ctx.p
    TC = 128
    nch = Ttot // TC
    H = 4096
    HB = H // 128

    xin = ctx.din("xin_fm", [D, Ttot])
    w1t = ctx.din("w1t", [D, H])
    w2t = ctx.din("w2t", [H, D])
    b1f = ctx.din("b1f", [128, HB])
    b2f = ctx.din("b2f", [128, KB])
    scale1 = ctx.din("scale1", [1, 1])
    identb = ctx.din("ident_bf", [128, 128], BF16)
    xout = ctx.dout("xout_fm", [D, Ttot])

    stage = ctx.sb("stage", [128, 1024], F32)
    w1b = ctx.sb("w1b", [128, KB, H], BF16)
    w2b = ctx.sb("w2b", [128, HB, D], BF16)
    b1s = ctx.sb("b1s", [128, HB], F32)
    b2s = ctx.sb("b2s", [128, KB], F32)
    idb = ctx.sb("idb", [128, 128], BF16)
    scs = ctx.sb("scs", [1, 1], F32)
    ctx.dma("sync", b1s[:, :], b1f, writes=("b1s",))
    ctx.dma("sync", b2s[:, :], b2f, writes=("b2s",))
    ctx.dma("sync", idb[:, :], identb, writes=("idb",))
    ctx.dma("sync", scs[:, :], scale1, writes=("scs",))
    for kc in range(KB):
        for hh in range(4):
            ctx.dma("sync", stage[:, :], w1t[kc * 128:(kc + 1) * 128, hh * 1024:(hh + 1) * 1024], writes=("stage",))
            p.step("gpsimd", lambda e, kc=kc, hh=hh: e.tensor_copy(w1b[:, kc, hh * 1024:(hh + 1) * 1024], stage[:, :]), reads=("stage",), writes=("w1b",))
    for hc in range(HB):
        ctx.dma("sync", stage[:, 0:D], w2t[hc * 128:(hc + 1) * 128, :], writes=("stage",), tag=f"w2dma{hc}")
        p.step("gpsimd", lambda e, hc=hc: e.tensor_copy(w2b[:, hc, :], stage[:, 0:D]), reads=("stage",), writes=("w2b",))

    xch = ctx.sb("xch", [128, KB, TC], F32)
    xb = ctx.sb("xb", [128, KB, TC], BF16)
    sq = ctx.sb("sq", [128, KB, TC], BF16)
    t1 = ctx.sb("t1", [128, KB, TC], F32)
    xnb = ctx.sb("xnb", [128, KB, TC], BF16)
    hb = ctx.sb("hb", [128, HB, TC], BF16)
    yf = ctx.sb("yf", [128, KB, TC], F32)
    rows = ctx.sb("rows", [1, 4 * TC], F32)
    odd1f = ctx.sb("odd1f", [128, KB, 4], BF16)
    even1T = ctx.sb("even1T", [1, 4 * D], BF16)
    even2T = ctx.sb("even2T", [1, 4 * D], BF16)
    odd2f = ctx.sb("odd2f", [128, KB, 4], BF16)
    tib = ctx.sb("tib", [1, 4 * TC], BF16)
    sub = ctx.sb("sub", [1, 16 * TC], BF16)

    xin_r = xin.rearrange("(a p) t -> p a t", p=128)
    xout_r = xout.rearrange("(a p) t -> p a t", p=128)

    trb = ctx.pb[3]

    for c in range(nch):
        t0 = c * TC
        ctx.dma("sync", xch.ap(), xin_r[:, :, t0:t0 + TC], writes=("xch",), tag=f"xld{c}")
        ctx.ln_fm(TC, KB, xch.ap(), "xch", xb.ap(), "xb", sq.ap(), "sq",
                  t1.ap(), "t1", xnb.ap(), "xnb", rows, "rows",
                  ctx.pb[6], "pb6", ctx.pb[7], "pb7")
        if c == 0:
            p.step("vector", lambda e: e.tensor_copy(odd1f.ap(), xnb.ap()[:, :, 2:9:2]), reads=("xnb",), writes=("odd1f",))
            p.step("vector", lambda e: e.tensor_copy(odd2f.ap(), xnb.ap()[:, :, 10:17:2]), reads=("xnb",), writes=("odd2f",))
            for i in range(4):
                for kc in range(KB):
                    tok_e1 = 1 + 2 * i
                    tok_e2 = 9 + 2 * i
                    p.step("tensor", lambda e, kc=kc, tok=tok_e1: e.transpose(trb[0:1, 0:128], xnb[:, kc, tok:tok + 1], idb[:, :]),
                           reads=("xnb", "idb"), writes=("pb3",))
                    p.step("vector", lambda e, i=i, kc=kc: e.tensor_copy(even1T[0:1, i * D + kc * 128: i * D + (kc + 1) * 128], trb[0:1, 0:128]),
                           reads=("pb3",), writes=("even1T",))
                    p.step("tensor", lambda e, kc=kc, tok=tok_e2: e.transpose(trb[0:1, 0:128], xnb[:, kc, tok:tok + 1], idb[:, :]),
                           reads=("xnb", "idb"), writes=("pb3",))
                    p.step("vector", lambda e, i=i, kc=kc: e.tensor_copy(even2T[0:1, i * D + kc * 128: i * D + (kc + 1) * 128], trb[0:1, 0:128]),
                           reads=("pb3",), writes=("even2T",))
        for i in range(4):
            bank, bk = ctx.pb[2], "pb2"
            ctx.mm_group(bank[0:1, 0:TC], bk,
                         [(odd1f[:, kc, i:i + 1], xnb[:, kc, :], ("odd1f", "xnb")) for kc in range(KB)],
                         tag=f"t{c}_{i}")
            p.step("vector", lambda e, i=i, bank=bank: e.tensor_scalar(tib[0:1, i * TC:(i + 1) * TC], bank[0:1, 0:TC], scs[0:1, 0:1], None, ALU.mult),
                   reads=(bk, "scs"), writes=("tib",))
        for mb in range(HB):
            bank, bk = ctx.pb[mb % 2], f"pb{mb % 2}"
            i = mb // 8
            pairs = [(w1b[:, kc, mb * 128:(mb + 1) * 128], xnb[:, kc, :], ("w1b", "xnb")) for kc in range(KB)]
            pairs.append((even1T[0:1, i * D + (mb % 8) * 128: i * D + (mb % 8 + 1) * 128], tib[0:1, i * TC:(i + 1) * TC], ("even1T", "tib")))
            ctx.mm_group(bank[:, 0:TC], bk, pairs, tag=f"fc{c}_{mb}")
            p.step("scalar", lambda e, mb=mb, bank=bank: e.activation(hb[:, mb, :], bank[:, 0:TC], AF.Gelu, bias=b1s[:, mb:mb + 1]),
                   reads=(bk, "b1s"), writes=("hb",), tag=f"gelu{c}_{mb}")
        for i in range(4):
            for j in range(4):
                bank, bk = ctx.pb[2], "pb2"
                ctx.mm_group(bank[0:1, 0:TC], bk,
                             [(odd2f[:, kc, i:i + 1], hb[:, j * 8 + kc, :], ("odd2f", "hb")) for kc in range(KB)],
                             tag=f"u{c}_{i}{j}")
                p.step("vector", lambda e, i=i, j=j, bank=bank: e.tensor_scalar(sub[0:1, (i * 4 + j) * TC:(i * 4 + j + 1) * TC], bank[0:1, 0:TC], scs[0:1, 0:1], None, ALU.mult),
                       reads=(bk, "scs"), writes=("sub",))
        for ob in range(KB):
            bank, bk = ctx.pb[2 + ob % 2] if False else ctx.pb[ob % 2], f"pb{ob % 2}"
            i = ob // 2
            r0 = (ob % 2) * 128
            pairs = [(w2b[:, hc, ob * 128:(ob + 1) * 128], hb[:, hc, :], ("w2b", "hb")) for hc in range(HB)]
            for j in range(4):
                st0 = i * D + 4 * r0 + j
                eap = even2T[0:1, st0: st0 + 4 * 127 + 1: 4]
                pairs.append((eap, sub[0:1, (i * 4 + j) * TC:(i * 4 + j + 1) * TC], ("even2T", "sub")))
            ctx.mm_group(bank[:, 0:TC], bk, pairs, tag=f"pj{c}_{ob}")
            p.step("scalar", lambda e, ob=ob, bank=bank: e.activation(yf[:, ob, :], bank[:, 0:TC], AF.Identity, bias=b2s[:, ob:ob + 1]),
                   reads=(bk, "b2s"), writes=("yf",), tag=f"bias{c}_{ob}")
        ctx.ln_fm(TC, KB, yf.ap(), "yf", xb.ap(), "xb", sq.ap(), "sq",
                  t1.ap(), "t1", t1.ap(), "t1", rows, "rows",
                  ctx.pb[6], "pb6", ctx.pb[7], "pb7")
        p.step("vector", lambda e: e.tensor_add(xch.ap(), xch.ap(), t1.ap()), reads=("xch", "t1"), writes=("xch",), tag=f"resadd{c}")
        ctx.dma("sync", xout_r[:, :, t0:t0 + TC], xch.ap(), reads=("xch",), writes=(f"xout{c}",), tag=f"xst{c}")

    p.step("gpsimd", lambda e: e.nop(), reads=tuple(f"xout{c}" for c in range(nch)), tag="final_wait")
    ctx.p.emit()
    return ctx


def l0_ffn_inmap(x_fm, W1, b1, W2, b2, scale):
    return {
        "xin_fm": np.ascontiguousarray(x_fm),
        "w1t": np.ascontiguousarray(W1.T), "w2t": np.ascontiguousarray(W2.T),
        "b1f": fmvec(b1), "b2f": fmvec(b2),
        "scale1": np.asarray(scale, np.float32).reshape(1, 1),
        "ident_bf": np.eye(128, dtype=bfloat16),
    }




def build_ln_pre(Ttot=S):
    ctx = Ctx("lnpre")
    p = ctx.p
    TC = 256
    nch = Ttot // TC
    xin = ctx.din("xin_fm", [D, Ttot])
    xout = ctx.dout("xi_fm", [D, Ttot])
    xch = ctx.sb("xch", [128, KB, TC], F32)
    xb = ctx.sb("xb", [128, KB, TC], BF16)
    sq = ctx.sb("sq", [128, KB, TC], BF16)
    t1 = ctx.sb("t1", [128, KB, TC], F32)
    rows = ctx.sb("rows", [1, 4 * TC], F32)
    xin_r = xin.rearrange("(a p) t -> p a t", p=128)
    xout_r = xout.rearrange("(a p) t -> p a t", p=128)
    for c in range(nch):
        t0 = c * TC
        ctx.dma("sync", xch.ap(), xin_r[:, :, t0:t0 + TC], writes=("xch",))
        ctx.ln_fm(TC, KB, xch.ap(), "xch", xb.ap(), "xb", sq.ap(), "sq",
                  t1.ap(), "t1", t1.ap(), "t1", rows, "rows",
                  ctx.pb[6], "pb6", ctx.pb[7], "pb7")
        ctx.dma("sync", xout_r[:, :, t0:t0 + TC], t1.ap(), reads=("t1",), writes=(f"xo{c}",))
    p.step("gpsimd", lambda e: e.nop(), reads=tuple(f"xo{c}" for c in range(nch)), tag="final_wait")
    p.emit()
    return ctx


def build_addln(Ttot=S):
    ctx = Ctx("addln")
    p = ctx.p
    TC = 256
    nch = Ttot // TC
    xa = ctx.din("xa_fm", [D, Ttot])
    xbin = ctx.din("xb_fm", [D, Ttot])
    xout = ctx.dout("xout_fm", [D, Ttot])
    ach = ctx.sb("ach", [128, KB, TC], F32)
    xch = ctx.sb("xch", [128, KB, TC], F32)
    xb = ctx.sb("xb", [128, KB, TC], BF16)
    sq = ctx.sb("sq", [128, KB, TC], BF16)
    t1 = ctx.sb("t1", [128, KB, TC], F32)
    rows = ctx.sb("rows", [1, 4 * TC], F32)
    xa_r = xa.rearrange("(a p) t -> p a t", p=128)
    xb_r = xbin.rearrange("(a p) t -> p a t", p=128)
    xout_r = xout.rearrange("(a p) t -> p a t", p=128)
    for c in range(nch):
        t0 = c * TC
        ctx.dma("sync", ach.ap(), xa_r[:, :, t0:t0 + TC], writes=("ach",))
        ctx.dma("sync", xch.ap(), xb_r[:, :, t0:t0 + TC], writes=("xch",))
        ctx.ln_fm(TC, KB, xch.ap(), "xch", xb.ap(), "xb", sq.ap(), "sq",
                  t1.ap(), "t1", t1.ap(), "t1", rows, "rows",
                  ctx.pb[6], "pb6", ctx.pb[7], "pb7")
        p.step("vector", lambda e: e.tensor_add(ach.ap(), ach.ap(), t1.ap()), reads=("ach", "t1"), writes=("ach",))
        ctx.dma("sync", xout_r[:, :, t0:t0 + TC], ach.ap(), reads=("ach",), writes=(f"xo{c}",))
    p.step("gpsimd", lambda e: e.nop(), reads=tuple(f"xo{c}" for c in range(nch)), tag="final_wait")
    p.emit()
    return ctx


def _load_scaled_weight(ctx, wname, w_hbm, Kin, Nout, stage, stage_cols, wb, wbk,
                        lnwb=None, lnbb=None, biasf=None, beff=None, beffk=None,
                        bank=None, bankk=None):
    p = ctx.p
    nk = Kin // 128
    nm = Nout // 128
    for kc in range(nk):
        for h0 in range(0, Nout, stage_cols):
            hw = min(stage_cols, Nout - h0)
            ctx.dma("sync", stage[:, 0:hw], w_hbm[kc * 128:(kc + 1) * 128, h0:h0 + hw], writes=("stage",))
            p.step("gpsimd", lambda e, kc=kc, h0=h0, hw=hw: e.tensor_copy(wb[:, kc, h0:h0 + hw], stage[:, 0:hw]), reads=("stage",), writes=(wbk,))
    if lnbb is not None:
        for mb in range(nm):
            ctx.mm_group(bank[:, 0:1], bankk,
                         [(wb[:, kc, mb * 128:(mb + 1) * 128], lnbb[:, kc:kc + 1], (wbk, "lnbb")) for kc in range(nk)])
            p.step("vector", lambda e, mb=mb: e.scalar_tensor_tensor(beff[:, mb:mb + 1], bank[:, 0:1], 1.0, biasf[:, mb:mb + 1], ALU.mult, ALU.add),
                   reads=(bankk, "biasf"), writes=(beffk,))
    if lnwb is not None:
        for kc in range(nk):
            p.step("vector", lambda e, kc=kc: e.tensor_scalar(wb[:, kc, :], wb[:, kc, :], lnwb[:, kc:kc + 1], None, ALU.mult),
                   reads=(wbk, "lnwb"), writes=(wbk,))


def build_l0a_qkv(Ttot=S):
    ctx = Ctx("l0aqkv")
    nc, p = ctx.nc, ctx.p
    TC = 128
    nch = Ttot // TC
    NQ = 24

    xi = ctx.din("xi_fm", [D, Ttot])
    aw = ctx.din("attn_w", [D, 3072])
    abf = ctx.din("attn_bf", [128, NQ])
    lnwf = ctx.din("ln1wf", [128, KB])
    lnbf = ctx.din("ln1bf", [128, KB])
    wout = ctx.dout("w_fm", [D, Ttot], BF16)

    stage = ctx.sb("stage", [128, 1536], F32)
    awb = ctx.sb("awb", [128, KB, 3072], BF16)
    lnw = ctx.sb("lnw", [128, KB], F32)
    lnb = ctx.sb("lnb", [128, KB], F32)
    lnbb = ctx.sb("lnbb", [128, KB], BF16)
    abs_ = ctx.sb("abs", [128, NQ], F32)
    beff = ctx.sb("beff", [128, NQ], F32)
    ctx.dma("sync", lnw[:, :], lnwf, writes=("lnwb",))
    ctx.dma("sync", lnb[:, :], lnbf, writes=("lnbq",))
    ctx.dma("sync", abs_[:, :], abf, writes=("biasf",))
    p.step("vector", lambda e: e.tensor_copy(lnbb[:, :], lnb[:, :]), reads=("lnbq",), writes=("lnbb",))
    _load_scaled_weight(ctx, "aw", aw, D, 3072, stage, 1536, awb, "awb",
                        lnwb=lnw, lnbb=lnbb, biasf=abs_, beff=beff, beffk="beff",
                        bank=ctx.pb[2], bankk="pb2")

    xch = ctx.sb("xch", [128, KB, TC], F32)
    xb = ctx.sb("xb", [128, KB, TC], BF16)
    sq = ctx.sb("sq", [128, KB, TC], BF16)
    t1 = ctx.sb("t1", [128, KB, TC], F32)
    xnb = ctx.sb("xnb", [128, KB, TC], BF16)
    rows = ctx.sb("rows", [1, 4 * TC], F32)
    qkv = ctx.sb("qkv", [128, NQ, TC], F32)
    qprev = ctx.sb("qprev", [128, NQ, 1], F32)
    dqkv = ctx.sb("dqkv", [128, NQ, TC], F32)
    rqkv = ctx.sb("rqkv", [128, NQ, TC], F32)
    cqkv = ctx.sb("cqkv", [128, NQ, TC], F32)
    s1 = ctx.sb("s1", [128, NQ, TC], F32)
    si = ctx.sb("si", [128, NQ, TC], I32)
    qk0 = ctx.sb("qk0", [128, KB, TC], F32)
    qk1 = ctx.sb("qk1", [128, KB, TC], F32)
    qk2 = ctx.sb("qk2", [128, KB, TC], F32)
    qk3 = ctx.sb("qk3", [128, KB, TC], F32)
    scr = ctx.sb("scr", [128, KB, TC], F32)
    wtmp = ctx.sb("wtmp", [128, KB, TC], F32)
    wb16 = ctx.sb("wb16", [128, KB, TC], BF16)

    xi_r = xi.rearrange("(a p) t -> p a t", p=128)
    wout_r = wout.rearrange("(a p) t -> p a t", p=128)

    INV2PI = float(1.0 / (2.0 * np.pi))
    TWOPI = float(2.0 * np.pi)

    for c in range(nch):
        t0 = c * TC
        ctx.dma("sync", xch.ap(), xi_r[:, :, t0:t0 + TC], writes=("xch",))
        ctx.ln_fm(TC, KB, xch.ap(), "xch", xb.ap(), "xb", sq.ap(), "sq",
                  t1.ap(), "t1", xnb.ap(), "xnb", rows, "rows",
                  ctx.pb[6], "pb6", ctx.pb[7], "pb7")
        for mb in range(NQ):
            bank, bk = ctx.pb[mb % 2], f"pb{mb % 2}"
            ctx.mm_group(bank[:, 0:TC], bk,
                         [(awb[:, kc, mb * 128:(mb + 1) * 128], xnb[:, kc, :], ("awb", "xnb")) for kc in range(KB)])
            p.step("scalar", lambda e, mb=mb, bank=bank: e.activation(qkv[:, mb, :], bank[:, 0:TC], AF.Identity, bias=beff[:, mb:mb + 1]),
                   reads=(bk, "beff"), writes=("qkv",))
        p.step("vector", lambda e: e.tensor_sub(dqkv.ap()[:, :, 1:TC], qkv.ap()[:, :, 1:TC], qkv.ap()[:, :, 0:TC - 1]), reads=("qkv",), writes=("dqkv",))
        if c == 0:
            p.step("vector", lambda e: e.memset(dqkv.ap()[:, :, 0:1], 0.0), writes=("dqkv",))
        else:
            p.step("vector", lambda e: e.tensor_sub(dqkv.ap()[:, :, 0:1], qkv.ap()[:, :, 0:1], qprev.ap()), reads=("qkv", "qprev"), writes=("dqkv",))
        p.step("vector", lambda e: e.tensor_copy(qprev.ap(), qkv.ap()[:, :, TC - 1:TC]), reads=("qkv",), writes=("qprev",))
        p.step("vector", lambda e: e.tensor_copy(rqkv.ap()[64:128, :, :], qkv.ap()[0:64, :, :]), reads=("qkv",), writes=("rqkv",))
        for sec in range(3):
            b0 = sec * 8
            p.step("vector", lambda e, b0=b0: e.tensor_copy(rqkv.ap()[0:64, b0 + 1:b0 + 8, :], qkv.ap()[64:128, b0:b0 + 7, :]), reads=("qkv",), writes=("rqkv",))
            p.step("vector", lambda e, b0=b0: e.tensor_copy(rqkv.ap()[0:64, b0:b0 + 1, :], qkv.ap()[64:128, b0 + 7:b0 + 8, :]), reads=("qkv",), writes=("rqkv",))
        p.step("vector", lambda e: e.tensor_scalar(s1.ap(), qkv.ap(), -INV2PI, 0.25, ALU.mult, ALU.add), reads=("qkv",), writes=("s1",))
        p.step("vector", lambda e: e.tensor_copy(si.ap(), s1.ap()), reads=("s1",), writes=("si",))
        p.step("vector", lambda e: e.tensor_sub(s1.ap(), s1.ap(), si.ap()), reads=("s1", "si"), writes=("s1",))
        p.step("scalar", lambda e: e.activation(cqkv.ap(), s1.ap(), AF.Sin, scale=TWOPI), reads=("s1",), writes=("cqkv",))
        q, k, v = qkv[:, 0:8, :], qkv[:, 8:16, :], qkv[:, 16:24, :]
        dq, dk, dv = dqkv[:, 0:8, :], dqkv[:, 8:16, :], dqkv[:, 16:24, :]
        rq, rk, rv = rqkv[:, 0:8, :], rqkv[:, 8:16, :], rqkv[:, 16:24, :]
        cq, ck, cv = cqkv[:, 0:8, :], cqkv[:, 8:16, :], cqkv[:, 16:24, :]
        def acc4(out, outk, terms):
            (s0, a0, a0k, b0, b0k) = terms[0]
            assert s0 == 1
            p.step("vector", lambda e: e.tensor_mul(out, a0, b0), reads=(a0k, b0k), writes=(outk,))
            for (sg, a, akk, b, bkk) in terms[1:]:
                p.step("vector", lambda e, a=a, b=b: e.tensor_mul(scr.ap(), a, b), reads=(akk, bkk), writes=("scr",))
                op = ALU.add if sg > 0 else ALU.subtract
                p.step("vector", lambda e, op=op: e.tensor_tensor(out, out, scr.ap(), op), reads=(outk, "scr"), writes=(outk,))
        Q, K_, V, DQ, DK, DV, RQ, RK, RV, CQ, CK, CV = ("qkv",) * 3 + ("dqkv",) * 3 + ("rqkv",) * 3 + ("cqkv",) * 3
        acc4(qk0.ap(), "qk0", [(1, q, Q, k, K_), (-1, dq, DQ, dk, DK), (-1, rq, RQ, rk, RK), (-1, cq, CQ, ck, CK)])
        acc4(qk1.ap(), "qk1", [(1, q, Q, dk, DK), (1, dq, DQ, k, K_), (1, rq, RQ, ck, CK), (-1, cq, CQ, rk, RK)])
        acc4(qk2.ap(), "qk2", [(1, q, Q, rk, RK), (-1, dq, DQ, ck, CK), (1, rq, RQ, k, K_), (1, cq, CQ, dk, DK)])
        acc4(qk3.ap(), "qk3", [(1, q, Q, ck, CK), (1, dq, DQ, rk, RK), (-1, rq, RQ, dk, DK), (1, cq, CQ, k, K_)])
        p.step("vector", lambda e: e.tensor_mul(wtmp.ap(), qk0.ap(), v), reads=("qk0", V), writes=("wtmp",))
        for (qq, qqk, bb, bbk) in [(qk1, "qk1", dv, DV), (qk2, "qk2", rv, RV)]:
            p.step("vector", lambda e, qq=qq, bb=bb: e.tensor_mul(scr.ap(), qq.ap(), bb), reads=(qqk, bbk), writes=("scr",))
            p.step("vector", lambda e: e.tensor_sub(wtmp.ap(), wtmp.ap(), scr.ap()), reads=("wtmp", "scr"), writes=("wtmp",))
        p.step("vector", lambda e: e.tensor_mul(scr.ap(), qk3.ap(), cv), reads=("qk3", CV), writes=("scr",))
        p.step("vector", lambda e: e.tensor_sub(wb16.ap(), wtmp.ap(), scr.ap()), reads=("wtmp", "scr"), writes=("wb16",))
        ctx.dma("sync", wout_r[:, :, t0:t0 + TC], wb16.ap(), reads=("wb16",), writes=(f"wo{c}",))

    p.step("gpsimd", lambda e: e.nop(), reads=tuple(f"wo{c}" for c in range(nch)), tag="final_wait")
    p.emit()
    return ctx


def build_l0a_spec(Ttot=S):
    ctx = Ctx("l0aspec")
    nc, p = ctx.nc, ctx.p
    TC = 512
    nch = Ttot // TC

    xi = ctx.din("xi_fm", [D, Ttot])
    wfm = ctx.din("w_fm", [D, Ttot], BF16)
    mfwd = ctx.din("mfwd", [D, 2048], BF16)
    minv = ctx.din("minv", [2048, D], BF16)
    ffm = ctx.din("ffm", [2048, Ttot])
    apw = ctx.din("aproj_w", [D, D])
    apbf = ctx.din("aprojbf", [128, KB])
    xo = ctx.dout("xo_fm", [D, Ttot])

    mfwdb = ctx.sb("mfwdb", [128, KB, 2048], BF16)
    minvb = ctx.sb("minvb", [128, 16, D], BF16)
    apb16 = ctx.sb("apb16", [128, KB, D], BF16)
    apb = ctx.sb("apb", [128, KB], F32)
    stage = ctx.sb("stage", [128, 1024], F32)
    ctx.dma("sync", apb[:, :], apbf, writes=("apb",))
    mfwd_r = mfwd.rearrange("(a p) g -> p a g", p=128)
    minv_r = minv.rearrange("(a p) f -> p a f", p=128)
    ctx.dma("sync", mfwdb.ap(), mfwd_r, writes=("mfwdb",))
    ctx.dma("sync", minvb.ap(), minv_r, writes=("minvb",))
    _load_scaled_weight(ctx, "apw", apw, D, D, stage, 1024, apb16, "apb16")

    wch = ctx.sb("wch", [128, KB, TC], BF16)
    fch = ctx.sb("fch", [128, 16, TC], F32)
    wfb = ctx.sb("wfb", [128, 16, TC], BF16)
    out0b = ctx.sb("out0b", [128, KB, TC], BF16)
    tA = ctx.sb("tA", [128, TC], F32)
    tB = ctx.sb("tB", [128, TC], F32)
    xich = ctx.sb("xich", [128, KB, TC], F32)

    xi_r = xi.rearrange("(a p) t -> p a t", p=128)
    w_r = wfm.rearrange("(a p) t -> p a t", p=128)
    f_r = ffm.rearrange("(a p) t -> p a t", p=128)
    xo_r = xo.rearrange("(a p) t -> p a t", p=128)

    for c in range(nch):
        t0 = c * TC
        ctx.dma("sync", wch.ap(), w_r[:, :, t0:t0 + TC], writes=("wch",))
        ctx.dma("sync", fch.ap(), f_r[:, :, t0:t0 + TC], writes=("fch",))
        ctx.dma("sync", xich.ap(), xi_r[:, :, t0:t0 + TC], writes=("xich",))
        for gp in range(8):
            bA, bAk = ctx.pb[0], "pb0"
            bB, bBk = ctx.pb[1], "pb1"
            ctx.mm_group(bA[:, 0:TC], bAk,
                         [(mfwdb[:, kc, gp * 128:(gp + 1) * 128], wch[:, kc, :], ("mfwdb", "wch")) for kc in range(KB)])
            ctx.mm_group(bB[:, 0:TC], bBk,
                         [(mfwdb[:, kc, 1024 + gp * 128:1024 + (gp + 1) * 128], wch[:, kc, :], ("mfwdb", "wch")) for kc in range(KB)])
            Fr = fch[:, gp, :]
            Fi = fch[:, 8 + gp, :]
            p.step("vector", lambda e, bA=bA, Fr=Fr: e.tensor_mul(tA.ap(), bA[:, 0:TC], Fr), reads=(bAk, "fch"), writes=("tA",))
            p.step("vector", lambda e, bB=bB, Fi=Fi: e.tensor_mul(tB.ap(), bB[:, 0:TC], Fi), reads=(bBk, "fch"), writes=("tB",))
            p.step("vector", lambda e, gp=gp: e.tensor_sub(wfb[:, gp, :], tA.ap(), tB.ap()), reads=("tA", "tB"), writes=("wfb",))
            p.step("vector", lambda e, bA=bA, Fi=Fi: e.tensor_mul(tA.ap(), bA[:, 0:TC], Fi), reads=(bAk, "fch"), writes=("tA",))
            p.step("vector", lambda e, bB=bB, Fr=Fr: e.tensor_mul(tB.ap(), bB[:, 0:TC], Fr), reads=(bBk, "fch"), writes=("tB",))
            p.step("vector", lambda e, gp=gp: e.tensor_add(wfb[:, 8 + gp, :], tA.ap(), tB.ap()), reads=("tA", "tB"), writes=("wfb",))
        for fb in range(KB):
            bank, bk = ctx.pb[2 + fb % 2], f"pb{2 + fb % 2}"
            ctx.mm_group(bank[:, 0:TC], bk,
                         [(minvb[:, gb, fb * 128:(fb + 1) * 128], wfb[:, gb, :], ("minvb", "wfb")) for gb in range(16)])
            p.step("scalar", lambda e, fb=fb, bank=bank: e.copy(out0b[:, fb, :], bank[:, 0:TC]), reads=(bk,), writes=("out0b",))
        for ob in range(KB):
            bank, bk = ctx.pb[ob % 2], f"pb{ob % 2}"
            ctx.mm_group(bank[:, 0:TC], bk,
                         [(apb16[:, fb, ob * 128:(ob + 1) * 128], out0b[:, fb, :], ("apb16", "out0b")) for fb in range(KB)])
            p.step("vector", lambda e, ob=ob, bank=bank: e.scalar_tensor_tensor(xich[:, ob, :], bank[:, 0:TC], apb[:, ob:ob + 1], xich[:, ob, :], ALU.add, ALU.add),
                   reads=(bk, "apb", "xich"), writes=("xich",))
        ctx.dma("sync", xo_r[:, :, t0:t0 + TC], xich.ap(), reads=("xich",), writes=(f"xo{c}",))

    p.step("gpsimd", lambda e: e.nop(), reads=tuple(f"xo{c}" for c in range(nch)), tag="final_wait")
    p.emit()
    return ctx


def build_l0a_ffn(Ttot=S):
    ctx = Ctx("l0affn")
    nc, p = ctx.nc, ctx.p
    TC = 256
    nch = Ttot // TC
    H = 4096
    HB = 32

    xi = ctx.din("xi_fm", [D, Ttot])
    fcw = ctx.din("fc_w", [D, H])
    fcbf = ctx.din("fcbf", [128, HB])
    lnwf = ctx.din("ln2wf", [128, KB])
    lnbf = ctx.din("ln2bf", [128, KB])
    mpw = ctx.din("mproj_w", [H, D])
    mpbf = ctx.din("mprojbf", [128, KB])
    xo = ctx.dout("xo_fm", [D, Ttot])

    stage = ctx.sb("stage", [128, 1024], F32)
    fcb16 = ctx.sb("fcb16", [128, KB, H], BF16)
    mpb16 = ctx.sb("mpb16", [128, HB, D], BF16)
    lnw = ctx.sb("lnw", [128, KB], F32)
    lnb = ctx.sb("lnb", [128, KB], F32)
    lnbb = ctx.sb("lnbb", [128, KB], BF16)
    fcb = ctx.sb("fcb", [128, HB], F32)
    beff = ctx.sb("beff", [128, HB], F32)
    mpb = ctx.sb("mpb", [128, KB], F32)
    ctx.dma("sync", lnw[:, :], lnwf, writes=("lnwb",))
    ctx.dma("sync", lnb[:, :], lnbf, writes=("lnbq",))
    ctx.dma("sync", fcb[:, :], fcbf, writes=("biasf",))
    ctx.dma("sync", mpb[:, :], mpbf, writes=("mpb",))
    p.step("vector", lambda e: e.tensor_copy(lnbb[:, :], lnb[:, :]), reads=("lnbq",), writes=("lnbb",))
    _load_scaled_weight(ctx, "fcw", fcw, D, H, stage, 1024, fcb16, "fcb16",
                        lnwb=lnw, lnbb=lnbb, biasf=fcb, beff=beff, beffk="beff",
                        bank=ctx.pb[2], bankk="pb2")
    _load_scaled_weight(ctx, "mpw", mpw, H, D, stage, 1024, mpb16, "mpb16")

    xch = ctx.sb("xch", [128, KB, TC], F32)
    xb = ctx.sb("xb", [128, KB, TC], BF16)
    sq = ctx.sb("sq", [128, KB, TC], BF16)
    t1 = ctx.sb("t1", [128, KB, TC], F32)
    xnb = ctx.sb("xnb", [128, KB, TC], BF16)
    hb = ctx.sb("hb", [128, HB, TC], BF16)
    rows = ctx.sb("rows", [1, 4 * TC], F32)

    xi_r = xi.rearrange("(a p) t -> p a t", p=128)
    xo_r = xo.rearrange("(a p) t -> p a t", p=128)

    for c in range(nch):
        t0 = c * TC
        ctx.dma("sync", xch.ap(), xi_r[:, :, t0:t0 + TC], writes=("xch",))
        ctx.ln_fm(TC, KB, xch.ap(), "xch", xb.ap(), "xb", sq.ap(), "sq",
                  t1.ap(), "t1", xnb.ap(), "xnb", rows, "rows",
                  ctx.pb[6], "pb6", ctx.pb[7], "pb7")
        for mb in range(HB):
            bank, bk = ctx.pb[mb % 2], f"pb{mb % 2}"
            ctx.mm_group(bank[:, 0:TC], bk,
                         [(fcb16[:, kc, mb * 128:(mb + 1) * 128], xnb[:, kc, :], ("fcb16", "xnb")) for kc in range(KB)])
            p.step("scalar", lambda e, mb=mb, bank=bank: e.activation(hb[:, mb, :], bank[:, 0:TC], AF.Gelu, bias=beff[:, mb:mb + 1]),
                   reads=(bk, "beff"), writes=("hb",))
        for ob in range(KB):
            bank, bk = ctx.pb[2 + ob % 2], f"pb{2 + ob % 2}"
            ctx.mm_group(bank[:, 0:TC], bk,
                         [(mpb16[:, hc, ob * 128:(ob + 1) * 128], hb[:, hc, :], ("mpb16", "hb")) for hc in range(HB)])
            p.step("vector", lambda e, ob=ob, bank=bank: e.scalar_tensor_tensor(xch[:, ob, :], bank[:, 0:TC], mpb[:, ob:ob + 1], xch[:, ob, :], ALU.add, ALU.add),
                   reads=(bk, "mpb", "xch"), writes=("xch",))
        ctx.dma("sync", xo_r[:, :, t0:t0 + TC], xch.ap(), reads=("xch",), writes=(f"xo{c}",))

    p.step("gpsimd", lambda e: e.nop(), reads=tuple(f"xo{c}" for c in range(nch)), tag="final_wait")
    p.emit()
    return ctx


H16, HD64 = 16, 64


def l0_consts():
    h = np.arange(H16)
    hd = np.arange(HD64)
    ph = np.exp(-2j * np.pi * np.outer(h, np.arange(H16)) / H16)
    pd = np.exp(-2j * np.pi * np.outer(hd, np.arange(HD64)) / HD64)
    M2D = (ph[:, None, :, None] * pd[None, :, None, :]).reshape(D, D)
    Minv = np.conj(M2D).T / D
    mfwd = np.concatenate([M2D.real, M2D.imag], axis=1).astype(bfloat16)
    minv_stack = np.concatenate([Minv.real, -Minv.imag], axis=0).astype(bfloat16)
    ks = np.fft.fftfreq(S).astype(np.float32)[:, None, None]
    khf = np.fft.fftfreq(H16).astype(np.float32)[None, :, None]
    kdf = np.fft.fftfreq(HD64).astype(np.float32)[None, None, :]
    km = np.sqrt(ks**2 + khf**2 + kdf**2 + 1e-10)
    F = np.exp(1j * np.arctan(np.log(km + 1e-10)))
    lam = (np.arange(S) / S).astype(np.float64)
    t = np.arange(S).astype(np.float64) * 0.1
    mod = np.sin(t + lam) * np.cos(t - 2.0 * lam + 0.5 * lam**2)
    msc = 1.0 + 0.1 * mod
    Fsc = F.reshape(S, D) * msc[:, None]
    ffm = np.ascontiguousarray(np.concatenate([Fsc.real.T, Fsc.imag.T], axis=0).astype(np.float32))
    return mfwd, minv_stack, ffm


def perm_qkv(attn_w, attn_b):
    w = attn_w.reshape(D, H16, 3, HD64)
    wq = w[:, :, 0, :].reshape(D, D)
    wk = w[:, :, 1, :].reshape(D, D)
    wv = w[:, :, 2, :].reshape(D, D)
    wp = np.concatenate([wq, wk, wv], axis=1)
    b = attn_b.reshape(H16, 3, HD64)
    bp = np.concatenate([b[:, 0, :].ravel(), b[:, 1, :].ravel(), b[:, 2, :].ravel()])
    return np.ascontiguousarray(wp), bp


def build_l1_attn(Ttot=S, debug=False, part="all", crange=None):
    ctx = Ctx("l1attn", bf16_banks=())
    nc, p = ctx.nc, ctx.p
    TC = 128
    TCDBG = 128
    DBGC = 1
    nch = Ttot // TC
    NH, HD = 16, 64

    xin = ctx.din("xin_fm", [D, Ttot])
    wq = ctx.din("wq", [D, D]); wk = ctx.din("wk", [D, D])
    wv = ctx.din("wv", [D, D]); wo = ctx.din("wo", [D, D])
    aq = ctx.din("aq", [D, 64]); bq = ctx.din("bq", [64, D])
    av = ctx.din("av", [D, 64]); bv = ctx.din("bv", [64, D])
    ao = ctx.din("ao", [D, 64]); bo = ctx.din("bo", [64, D])
    costab = ctx.din("costab", [Ttot, HD])
    sintab = ctx.din("sintab", [Ttot, HD])
    cdmah = ctx.din("cdmah", [128, HD])
    cdma1 = ctx.din("cdma1", [128, HD])
    cdma2 = ctx.din("cdma2", [128, HD])
    meqt = ctx.din("meqt", [128, 32])
    mkeept = ctx.din("mkeept", [128, 32])
    mfutnt = ctx.din("mfutnt", [128, 32])
    t4t = ctx.din("t4t", [128, 128], BF16)
    s4t = ctx.din("s4t", [128, 32], BF16)
    e4t = ctx.din("e4t", [128, 4, 128], BF16)
    identb = ctx.din("ident_bf", [128, 128], BF16)
    nch0 = Ttot // 128
    if part == "kv":
        fks_out = ctx.dout("fks_hbm", [128, KB * D], BF16)
        fvs_out = ctx.dout("fvs_hbm", [128, KB * D], BF16)
        fkfm_out = ctx.dout("fkfm_hbm", [128, KB * D], BF16)
        xout = None
    else:
        xout = ctx.dout("xout_fm", [D, Ttot])
        if part != "all":
            fks_in = ctx.din("fks_hbm", [128, KB * D], BF16)
            fvs_in = ctx.din("fvs_hbm", [128, KB * D], BF16)
            fkfm_in = ctx.din("fkfm_hbm", [128, KB * D], BF16)
    dbg = {}
    if debug:
        for nm, shp, dt_ in [("d_kf", [128, D], F32), ("d_ksh", [128, D], BF16),
                             ("d_fKs", [128, KB * D], BF16), ("d_fKfm", [128, KB * D], BF16),
                             ("d_aqf", [128, D], F32), ("d_cs", [128, NH], F32),
                             ("d_wexp", [128, 1024], BF16), ("d_ssum", [128, NH], F32),
                             ("d_outk", [128, D], F32), ("d_t1", [128, KB * TCDBG], F32),
                             ("d_outk0", [128, D], F32), ("d_fve", [128, D], F32),
                             ("d_fVs", [128, KB * D], BF16),
                             ("d_cpv", [128, D], F32), ("d_wT", [128, KB * TCDBG], BF16),
                             ("d_vf", [128, D], F32), ("d_xnb", [128, KB * TCDBG], BF16)]:
            dbg[nm] = ctx.dout(nm, shp, dt_)

    stage = ctx.sb("stage", [128, 1024], F32)
    wqb = ctx.sb("wqb", [128, KB, D], BF16)
    wkb = ctx.sb("wkb", [128, KB, D], BF16)
    wvb = ctx.sb("wvb", [128, KB, D], BF16)
    wob = ctx.sb("wob", [128, KB, D], BF16)
    aqb_w = ctx.sb("aqb_w", [128, KB, 64], BF16)
    avb_w = ctx.sb("avb_w", [128, KB, 64], BF16)
    aob_w = ctx.sb("aob_w", [128, KB, 64], BF16)
    bqb_w = ctx.sb("bqb_w", [64, D], BF16)
    bvb_w = ctx.sb("bvb_w", [64, D], BF16)
    bob_w = ctx.sb("bob_w", [64, D], BF16)
    for nm, hbmw, sb_ in [("wq", wq, wqb), ("wk", wk, wkb), ("wv", wv, wvb), ("wo", wo, wob)]:
        _load_scaled_weight(ctx, nm, hbmw, D, D, stage, 1024, sb_, nm + "b")
    for nm, hbmw, sb_ in [("aq", aq, aqb_w), ("av", av, avb_w), ("ao", ao, aob_w)]:
        for kc in range(KB):
            ctx.dma("sync", stage[:, 0:64], hbmw[kc * 128:(kc + 1) * 128, :], writes=("stage",))
            p.step("gpsimd", lambda e, kc=kc, sb_=sb_: e.tensor_copy(sb_[:, kc, :], stage[:, 0:64]), reads=("stage",), writes=(nm + "b",))
    for nm, hbmw, sb_ in [("bq", bq, bqb_w), ("bv", bv, bvb_w), ("bo", bo, bob_w)]:
        ctx.dma("sync", stage[0:64, 0:D], hbmw, writes=("stage",))
        p.step("gpsimd", lambda e, sb_=sb_: e.tensor_copy(sb_[:, :], stage[0:64, 0:D]), reads=("stage",), writes=(nm + "b",))

    cdh = ctx.sb("cdh", [128, HD], F32); ctx.dma("sync", cdh[:, :], cdmah, writes=("cdh",))
    cd1 = ctx.sb("cd1", [128, HD], F32); ctx.dma("sync", cd1[:, :], cdma1, writes=("cd1",))
    cd2 = ctx.sb("cd2", [128, HD], F32); ctx.dma("sync", cd2[:, :], cdma2, writes=("cd2",))
    meq = ctx.sb("meq", [128, 32], F32); ctx.dma("sync", meq[:, :], meqt, writes=("meq",))
    mkp = ctx.sb("mkp", [128, 32], F32); ctx.dma("sync", mkp[:, :], mkeept, writes=("mkp",))
    mfn = ctx.sb("mfn", [128, 32], F32); ctx.dma("sync", mfn[:, :], mfutnt, writes=("mfn",))
    t4 = ctx.sb("t4", [128, 128], BF16); ctx.dma("sync", t4[:, :], t4t, writes=("t4",))
    s4 = ctx.sb("s4", [128, 32], BF16); ctx.dma("sync", s4[:, :], s4t, writes=("s4",))
    e4 = ctx.sb("e4", [128, 4, 128], BF16); ctx.dma("sync", e4.ap(), e4t, writes=("e4",))
    idb = ctx.sb("idb", [128, 128], BF16); ctx.dma("sync", idb[:, :], identb, writes=("idb",))

    fKs = ctx.sb("fKs", [128, KB, D], BF16)
    fVs = ctx.sb("fVs", [128, KB, D], BF16)
    fKfm = ctx.sb("fKfm", [128, KB, D], BF16)
    if part == "kv":
        p.step("vector", lambda e: e.memset(fKs.ap(), 0.0), writes=("fKs",))
        p.step("vector", lambda e: e.memset(fVs.ap(), 0.0), writes=("fVs",))
        p.step("vector", lambda e: e.memset(fKfm.ap(), 0.0), writes=("fKfm",))

    xch = ctx.sb("xch", [128, KB, TC], F32)
    xb = ctx.sb("xb", [128, KB, TC], BF16)
    sq = ctx.sb("sq", [128, KB, TC], BF16)
    t1 = ctx.sb("t1", [128, KB, TC], F32)
    xnb = ctx.sb("xnb", [128, KB, TC], BF16)
    rows = ctx.sb("rows", [1, 4 * TC], F32)
    kf = ctx.sb("kf", [128, D], F32)
    vf = ctx.sb("vf", [128, D], F32)
    qf = ctx.sb("qf", [128, D], F32)
    s1 = ctx.sb("s1", [128, D], F32)
    s2 = ctx.sb("s2", [128, D], F32)
    s3 = ctx.sb("s3", [128, D], F32)
    coss = ctx.sb("coss", [128, HD], F32)
    sins = ctx.sb("sins", [128, HD], F32)
    kshb = ctx.sb("kshb", [128, D], BF16)
    vshb = ctx.sb("vshb", [128, D], BF16)
    aqf = ctx.sb("aqf", [128, D], F32)
    aqb16 = ctx.sb("aqb16", [128, D], BF16)
    aqfm = ctx.sb("aqfm", [128, KB, TC], BF16)
    avsml = ctx.sb("avsml", [64, TC], BF16)
    wexp = ctx.sb("wexp", [128, 1024], BF16)
    sfix = ctx.sb("sfix", [128, 32], F32)
    cs_t = ctx.sb("cs_t", [128, NH], F32)
    ecs = ctx.sb("ecs", [128, NH], F32)
    ssum = ctx.sb("ssum", [128, NH], F32)
    rssum = ctx.sb("rssum", [128, NH], F32)
    wTb = ctx.sb("wTb", [128, KB, TC], BF16)
    fve = ctx.sb("fve", [128, D], F32)
    outk = ctx.sb("outk", [128, D], F32)
    outb16 = ctx.sb("outb16", [128, D], BF16)
    outfm = ctx.sb("outfm", [128, KB, TC], BF16)

    xin_r = xin.rearrange("(a p) t -> p a t", p=128)
    if xout is not None:
        xout_r = xout.rearrange("(a p) t -> p a t", p=128)

    def lnx(c):
        t0 = c * TC
        ctx.dma("sync", xch.ap(), xin_r[:, :, t0:t0 + TC], writes=("xch",))
        ctx.ln_fm(TC, KB, xch.ap(), "xch", xb.ap(), "xb", sq.ap(), "sq",
                  t1.ap(), "t1", xnb.ap(), "xnb", rows, "rows",
                  ctx.pb[0], "pb0", ctx.pb[1], "pb1")

    def proj_tok(dst_f32, dstk, wbt, wbtk, lora=None):
        for half in range(2):
            bank, bk = ctx.pb[half], f"pb{half}"
            pairs = [(xnb[:, kc, :], wbt[:, kc, half * 512:(half + 1) * 512], ("xnb", wbtk)) for kc in range(KB)]
            if lora is not None:
                ab16, abk, bb, bbk = lora
                pairs.append((ab16.ap(), bb[:, half * 512:(half + 1) * 512], (abk, bbk)))
            ctx.mm_group(bank[:, :], bk, pairs)
            p.step("scalar", lambda e, bank=bank, half=half: e.copy(dst_f32[:, half * 512:(half + 1) * 512], bank[:, :]),
                   reads=(f"pb{half}",), writes=(dstk,))

    def lora_a(awt, awtk, dstb, dstbk):
        bank, bk = ctx.pb[6], "pb6"
        ctx.mm_group(bank[0:64, 0:TC], bk,
                     [(awt[:, kc, :], xnb[:, kc, :], (awtk, "xnb")) for kc in range(KB)])
        p.step("scalar", lambda e: e.copy(dstb.ap(), bank[0:64, 0:TC]), reads=(bk,), writes=(dstbk,))

    def rope(dst, dstk, src, srck):
        s13 = s1.ap().rearrange("p (h d) -> p h d", d=HD)
        src3 = src.ap().rearrange("p (h d) -> p h d", d=HD)
        dst3 = dst.ap().rearrange("p (h d) -> p h d", d=HD)
        cosb = coss.ap().unsqueeze(1).broadcast_to([128, NH, HD])
        sinb = sins.ap().unsqueeze(1).broadcast_to([128, NH, HD])
        p.step("vector", lambda e: e.tensor_mul(s13[:, :, 0:32], src3[:, :, 32:64], sinb[:, :, 0:32]), reads=(srck, "sins"), writes=("s1",))
        p.step("vector", lambda e: e.tensor_mul(s13[:, :, 32:64], src3[:, :, 0:32], sinb[:, :, 32:64]), reads=(srck, "sins"), writes=("s1",))
        p.step("vector", lambda e: e.tensor_mul(s2.ap(), src.ap(), cosb.rearrange("p h d -> p (h d)") if False else cosb), reads=(srck, "coss"), writes=("s2",))
        p.step("vector", lambda e: e.tensor_sub(dst3[:, :, 0:32], s2.ap().rearrange("p (h d) -> p h d", d=HD)[:, :, 0:32], s13[:, :, 0:32]), reads=("s2", "s1"), writes=(dstk,))
        p.step("vector", lambda e: e.tensor_add(dst3[:, :, 32:64], s2.ap().rearrange("p (h d) -> p h d", d=HD)[:, :, 32:64], s13[:, :, 32:64]), reads=("s2", "s1"), writes=(dstk,))

    for c in (range(nch) if part in ("all", "kv") else ()):
        t0 = c * TC
        lnx(c)
        ctx.dma("sync", coss[:, :], costab[t0:t0 + TC, :], writes=("coss",))
        ctx.dma("sync", sins[:, :], sintab[t0:t0 + TC, :], writes=("sins",))
        lora_a(avb_w, "avb", avsml, "avsml")
        proj_tok(kf, "kf", wkb, "wkb")
        proj_tok(vf, "vf", wvb, "wvb", lora=(avsml, "avsml", bvb_w, "bvb"))
        rope(s3, "s3", kf, "kf")
        cdhb = cdh.ap().unsqueeze(1).broadcast_to([128, NH, HD])
        p.step("vector", lambda e: e.tensor_mul(kshb.ap().rearrange("p (h d) -> p h d", d=HD), s3.ap().rearrange("p (h d) -> p h d", d=HD), cdhb), reads=("s3", "cdh"), writes=("kshb",))
        p.step("vector", lambda e: e.tensor_mul(vshb.ap().rearrange("p (h d) -> p h d", d=HD), vf.ap().rearrange("p (h d) -> p h d", d=HD), cdhb), reads=("vf", "cdh"), writes=("vshb",))
        j = c % 4
        for half in range(2):
            kb_, kbk = ctx.pb[2 + half], f"pb{2 + half}"
            vb_, vbk = ctx.pb[0 + half] if False else ctx.pb[4 + half], f"pb{4 + half}"
            p.step("tensor", lambda e, kb_=kb_, half=half, j=j: e.matmul(kb_[32 * j:32 * (j + 1), :], s4[:, :], kshb[:, half * 512:(half + 1) * 512], start=True, stop=True, tile_position=(0, 32 * j)),
                   reads=("s4", "kshb"), writes=(kbk,))
            p.step("tensor", lambda e, vb_=vb_, half=half, j=j: e.matmul(vb_[32 * j:32 * (j + 1), :], s4[:, :], vshb[:, half * 512:(half + 1) * 512], start=True, stop=True, tile_position=(0, 32 * j)),
                   reads=("s4", "vshb"), writes=(vbk,))
        if debug and c == DBGC:
            ctx.dma("sync", dbg["d_kf"], kf.ap(), reads=("kf",), writes=("dkf",))
            ctx.dma("sync", dbg["d_ksh"], kshb.ap(), reads=("kshb",), writes=("dksh",))
            ctx.dma("sync", dbg["d_vf"], vf.ap(), reads=("vf",), writes=("dvf",))
        if j == 3:
            blk = c // 4
            for half in range(2):
                p.step("vector", lambda e, blk=blk, half=half: e.tensor_copy(fKs[:, blk, half * 512:(half + 1) * 512], ctx.pb[2 + half][:, :]), reads=(f"pb{2 + half}",), writes=("fKs",))
                p.step("vector", lambda e, blk=blk, half=half: e.tensor_copy(fVs[:, blk, half * 512:(half + 1) * 512], ctx.pb[4 + half][:, :]), reads=(f"pb{4 + half}",), writes=("fVs",))
    for sb_ in (range(KB) if part in ("all", "kv") else ()):
        for fb in range(KB):
            p.step("tensor", lambda e, sb_=sb_, fb=fb: e.transpose(ctx.pb[2][:, :].bitcast(BF16)[:, 0:128], fKs[:, sb_, fb * 128:(fb + 1) * 128], idb[:, :]),
                   reads=("fKs", "idb"), writes=("pb2",))
            p.step("vector", lambda e, sb_=sb_, fb=fb: e.tensor_copy(fKfm[:, fb, sb_ * 128:(sb_ + 1) * 128], ctx.pb[2][:, :].bitcast(BF16)[:, 0:128]),
                   reads=("pb2",), writes=("fKfm",))

    if debug:
        ctx.dma("sync", dbg["d_fKs"], fKs.ap().rearrange("p a t -> p (a t)"), reads=("fKs",), writes=("dfks",))
        ctx.dma("sync", dbg["d_fKfm"], fKfm.ap().rearrange("p a t -> p (a t)"), reads=("fKfm",), writes=("dfkfm",))
        ctx.dma("sync", dbg["d_fVs"], fVs.ap().rearrange("p a t -> p (a t)"), reads=("fVs",), writes=("dfvs",))

    if part == "kv":
        ctx.dma("sync", fks_out, fKs.ap().rearrange("p a t -> p (a t)"), reads=("fKs",), writes=("fko",))
        ctx.dma("sync", fvs_out, fVs.ap().rearrange("p a t -> p (a t)"), reads=("fVs",), writes=("fvo",))
        ctx.dma("sync", fkfm_out, fKfm.ap().rearrange("p a t -> p (a t)"), reads=("fKfm",), writes=("fkfo",))
        p.step("gpsimd", lambda e: e.nop(), reads=("fko", "fvo", "fkfo"), tag="final_wait")
        p.emit()
        return ctx
    if part != "all":
        ctx.dma("sync", fKs.ap().rearrange("p a t -> p (a t)"), fks_in, writes=("fKs",))
        ctx.dma("sync", fVs.ap().rearrange("p a t -> p (a t)"), fvs_in, writes=("fVs",))
        ctx.dma("sync", fKfm.ap().rearrange("p a t -> p (a t)"), fkfm_in, writes=("fKfm",))

    qrange = range(nch) if crange is None else range(*crange)
    for c in qrange:
        t0 = c * TC
        nslots = 32 * (c + 1)
        nsb = (nslots + 127) // 128
        lnx(c)
        ctx.dma("sync", coss[:, :], costab[t0:t0 + TC, :], writes=("coss",))
        ctx.dma("sync", sins[:, :], sintab[t0:t0 + TC, :], writes=("sins",))
        lora_a(aqb_w, "aqb", avsml, "avsml")
        proj_tok(qf, "qf", wqb, "wqb", lora=(avsml, "avsml", bqb_w, "bqb"))
        rope(s3, "s3", qf, "qf")
        cd1b = cd1.ap().unsqueeze(1).broadcast_to([128, NH, HD])
        p.step("vector", lambda e: e.tensor_mul(aqf.ap().rearrange("p (h d) -> p h d", d=HD), s3.ap().rearrange("p (h d) -> p h d", d=HD), cd1b), reads=("s3", "cd1"), writes=("aqf",))
        p.step("vector", lambda e: e.tensor_copy(aqb16.ap(), aqf.ap()), reads=("aqf",), writes=("aqb16",))
        for fb in range(KB):
            p.step("tensor", lambda e, fb=fb: e.transpose(ctx.pb[2][:, :].bitcast(BF16)[:, 0:128], aqb16[:, fb * 128:(fb + 1) * 128], idb[:, :]),
                   reads=("aqb16", "idb"), writes=("pb2",))
            p.step("vector", lambda e, fb=fb: e.tensor_copy(aqfm[:, fb, :], ctx.pb[2][:, :].bitcast(BF16)[:, 0:128]),
                   reads=("pb2",), writes=("aqfm",))
        lora_a(avb_w, "avb", avsml, "avsml")
        proj_tok(kf, "kf", wkb, "wkb")
        proj_tok(vf, "vf", wvb, "wvb", lora=(avsml, "avsml", bvb_w, "bvb"))
        rope(s3, "s3", kf, "kf")
        cdhb = cdh.ap().unsqueeze(1).broadcast_to([128, NH, HD])
        p.step("vector", lambda e: e.tensor_mul(kshb.ap().rearrange("p (h d) -> p h d", d=HD), s3.ap().rearrange("p (h d) -> p h d", d=HD), cdhb), reads=("s3", "cdh"), writes=("kshb",))
        p.step("vector", lambda e: e.tensor_mul(vshb.ap().rearrange("p (h d) -> p h d", d=HD), vf.ap().rearrange("p (h d) -> p h d", d=HD), cdhb), reads=("vf", "cdh"), writes=("vshb",))
        for half in range(2):
            p.step("tensor", lambda e, half=half: e.matmul(ctx.pb[4 + half][:, :], t4[:, :], kshb[:, half * 512:(half + 1) * 512], start=True, stop=True), reads=("t4", "kshb"), writes=(f"pb{4 + half}",))
            p.step("tensor", lambda e, half=half: e.matmul(ctx.pb[6 + half][:, :], t4[:, :], vshb[:, half * 512:(half + 1) * 512], start=True, stop=True), reads=("t4", "vshb"), writes=(f"pb{6 + half}",))
            p.step("tensor", lambda e, half=half, c=c: e.matmul(ctx.pb[2][:, :], e4[:, c % 4, :], fVs[:, c // 4, half * 512:(half + 1) * 512], start=True, stop=True),
                   reads=("e4", "fVs"), writes=("pb2",))
            p.step("scalar", lambda e, half=half: e.copy(fve[:, half * 512:(half + 1) * 512], ctx.pb[2][:, :]), reads=("pb2",), writes=("fve",))
        cpk3 = None
        p.step("vector", lambda e: e.tensor_mul(s1.ap()[:, 0:512], aqf[:, 0:512], ctx.pb[4][:, :]), reads=("aqf", "pb4"), writes=("s1",))
        p.step("vector", lambda e: e.tensor_mul(s1.ap()[:, 512:1024], aqf[:, 512:1024], ctx.pb[5][:, :]), reads=("aqf", "pb5"), writes=("s1",))
        p.step("vector", lambda e: e.tensor_reduce(cs_t.ap(), s1.ap().rearrange("p (h d) -> p h d", d=HD), mybir.AxisListType.X, ALU.add), reads=("s1",), writes=("cs_t",))
        p.step("scalar", lambda e: e.activation(ecs.ap(), cs_t.ap(), AF.Exp), reads=("cs_t",), writes=("ecs",))
        for h in range(NH):
            hb_, ho = h // 2, (h % 2) * 64
            aqs = aqfm[ho:ho + 64, hb_, :]
            for piece in range((nslots + 511) // 512):
                p0 = piece * 512
                pw = min(512, nslots - p0)
                bank, bk = ctx.pb[piece], f"pb{piece}"
                ctx.mm_group(bank[:, 0:pw], bk,
                             [(aqs, fKfm[ho:ho + 64, hb_, p0:p0 + pw], ("aqfm", "fKfm"))])
            lb, lo = (32 * c) // 512, (32 * c) % 512
            lbank, lbk = ctx.pb[lb], f"pb{lb}"
            p.step("vector", lambda e, lbank=lbank, lo=lo: e.tensor_mul(sfix.ap(), lbank[:, lo:lo + 32], mkp[:, :]), reads=(lbk, "mkp"), writes=("sfix",))
            p.step("vector", lambda e, h=h: e.scalar_tensor_tensor(s2.ap()[:, 0:32], meq[:, :], cs_t[:, h:h + 1], sfix.ap(), ALU.mult, ALU.add), reads=("meq", "cs_t", "sfix"), writes=("s2",))
            p.step("vector", lambda e: e.tensor_add(sfix.ap(), s2.ap()[:, 0:32], mfn[:, :]), reads=("s2", "mfn"), writes=("sfix",))
            for piece in range((nslots + 511) // 512):
                p0 = piece * 512
                pw = min(512, nslots - p0)
                bank, bk = ctx.pb[piece], f"pb{piece}"
                if p0 + pw > 32 * c:
                    pw2 = 32 * c - p0
                    if pw2 > 0:
                        p.step("scalar", lambda e, bank=bank, p0=p0, pw2=pw2: e.activation(wexp[:, p0:p0 + pw2], bank[:, 0:pw2], AF.Exp), reads=(bk,), writes=("wexp",))
                else:
                    p.step("scalar", lambda e, bank=bank, p0=p0, pw=pw: e.activation(wexp[:, p0:p0 + pw], bank[:, 0:pw], AF.Exp), reads=(bk,), writes=("wexp",))
            p.step("scalar", lambda e, c=c: e.activation(wexp[:, 32 * c:32 * c + 32], sfix.ap(), AF.Exp), reads=("sfix",), writes=("wexp",))
            p.step("vector", lambda e, h=h, ns=nslots: e.tensor_reduce(ssum[:, h:h + 1], wexp.ap()[:, 0:ns].unsqueeze(1), mybir.AxisListType.X, ALU.add), reads=("wexp",), writes=("ssum",))
            for sb_ in range(nsb):
                swidth = min(128, nslots - sb_ * 128)
                p.step("tensor", lambda e, sb_=sb_, sw=swidth: e.transpose(ctx.pb[2][:, :].bitcast(BF16)[0:sw, 0:128], wexp[:, sb_ * 128:sb_ * 128 + sw], idb[:, :]),
                       reads=("wexp", "idb"), writes=("pb2",))
                p.step("vector", lambda e, sb_=sb_, sw=swidth: e.tensor_copy(wTb[0:sw, sb_, :], ctx.pb[2][:, :].bitcast(BF16)[0:sw, 0:128]),
                       reads=("pb2",), writes=("wTb",))
            bank3, bk3 = ctx.pb[3], "pb3"
            ctx.mm_group(bank3[:, h * 64 - (h // 8) * 512:h * 64 - (h // 8) * 512 + 64] if False else bank3[:, (h % 8) * 64:(h % 8) * 64 + 64], bk3,
                         [(wTb[0:min(128, nslots - sb_ * 128), sb_, :], fVs[0:min(128, nslots - sb_ * 128), sb_, h * 64:(h + 1) * 64], ("wTb", "fVs")) for sb_ in range(nsb)])
            if h % 8 == 7:
                g = h // 8
                p.step("scalar", lambda e, g=g: e.copy(outk[:, g * 512:(g + 1) * 512], bank3[:, :]), reads=(bk3,), writes=("outk",))
        if debug and c == DBGC:
            ctx.dma("sync", dbg["d_outk0"], outk.ap(), reads=("outk",), writes=("doutk0",))
            ctx.dma("sync", dbg["d_fve"], fve.ap(), reads=("fve",), writes=("dfve",))
            p.step("scalar", lambda e: e.copy(s2.ap()[:, 0:512], ctx.pb[6][:, :]), reads=("pb6",), writes=("s2",))
            p.step("scalar", lambda e: e.copy(s2.ap()[:, 512:1024], ctx.pb[7][:, :]), reads=("pb7",), writes=("s2",))
            ctx.dma("sync", dbg["d_cpv"], s2.ap(), reads=("s2",), writes=("dcpv",))
            ctx.dma("sync", dbg["d_wT"], wTb.ap().rearrange("p a t -> p (a t)"), reads=("wTb",), writes=("dwT",))
            ctx.dma("sync", dbg["d_aqf"], aqf.ap(), reads=("aqf",), writes=("daqf",))
            ctx.dma("sync", dbg["d_cs"], cs_t.ap(), reads=("cs_t",), writes=("dcs",))
            ctx.dma("sync", dbg["d_wexp"], wexp.ap(), reads=("wexp",), writes=("dwexp",))
            ctx.dma("sync", dbg["d_ssum"], ssum.ap(), reads=("ssum",), writes=("dssum",))
            ctx.dma("sync", dbg["d_xnb"], xnb.ap().rearrange("p a t -> p (a t)"), reads=("xnb",), writes=("dxnb",))
        p.step("vector", lambda e: e.tensor_sub(s1.ap()[:, 0:512], ctx.pb[6][:, :], fve[:, 0:512]), reads=("pb6", "fve"), writes=("s1",))
        p.step("vector", lambda e: e.tensor_sub(s1.ap()[:, 512:1024], ctx.pb[7][:, :], fve[:, 512:1024]), reads=("pb7", "fve"), writes=("s1",))
        ecs3 = ecs.ap().unsqueeze(2).broadcast_to([128, NH, HD])
        p.step("vector", lambda e: e.tensor_mul(s1.ap().rearrange("p (h d) -> p h d", d=HD), s1.ap().rearrange("p (h d) -> p h d", d=HD), ecs3), reads=("s1", "ecs"), writes=("s1",))
        p.step("vector", lambda e: e.tensor_add(outk.ap(), outk.ap(), s1.ap()), reads=("outk", "s1"), writes=("outk",))
        p.step("vector", lambda e: e.reciprocal(rssum.ap(), ssum.ap()), reads=("ssum",), writes=("rssum",))
        rs3 = rssum.ap().unsqueeze(2).broadcast_to([128, NH, HD])
        p.step("vector", lambda e: e.tensor_mul(outk.ap().rearrange("p (h d) -> p h d", d=HD), outk.ap().rearrange("p (h d) -> p h d", d=HD), rs3), reads=("outk", "rssum"), writes=("outk",))
        cd2b = cd2.ap().unsqueeze(1).broadcast_to([128, NH, HD])
        p.step("vector", lambda e: e.tensor_mul(outb16.ap().rearrange("p (h d) -> p h d", d=HD), outk.ap().rearrange("p (h d) -> p h d", d=HD), cd2b), reads=("outk", "cd2"), writes=("outb16",))
        for fb in range(KB):
            p.step("tensor", lambda e, fb=fb: e.transpose(ctx.pb[2][:, :].bitcast(BF16)[:, 0:128], outb16[:, fb * 128:(fb + 1) * 128], idb[:, :]),
                   reads=("outb16", "idb"), writes=("pb2",))
            p.step("vector", lambda e, fb=fb: e.tensor_copy(outfm[:, fb, :], ctx.pb[2][:, :].bitcast(BF16)[:, 0:128]),
                   reads=("pb2",), writes=("outfm",))
        bank, bk = ctx.pb[3], "pb3"
        ctx.mm_group(bank[0:64, 0:TC], bk,
                     [(aob_w[:, kc, :], outfm[:, kc, :], ("aob", "outfm")) for kc in range(KB)])
        p.step("scalar", lambda e, bank=bank: e.copy(avsml.ap(), bank[0:64, 0:TC]), reads=(bk,), writes=("avsml",))
        for ob in range(KB):
            bank, bk = ctx.pb[ob % 2], f"pb{ob % 2}"
            pairs = [(wob[:, kc, ob * 128:(ob + 1) * 128], outfm[:, kc, :], ("wob", "outfm")) for kc in range(KB)]
            pairs.append((bob_w[:, ob * 128:(ob + 1) * 128], avsml.ap(), ("bob", "avsml")))
            ctx.mm_group(bank[:, 0:TC], bk, pairs)
            p.step("scalar", lambda e, ob=ob, bank=bank: e.copy(t1[:, ob, :], bank[:, 0:TC]), reads=(bk,), writes=("t1",))
        if debug and c == DBGC:
            ctx.dma("sync", dbg["d_outk"], outk.ap(), reads=("outk",), writes=("doutk",))
            ctx.dma("sync", dbg["d_t1"], t1.ap().rearrange("p a t -> p (a t)"), reads=("t1",), writes=("dt1",))
        ctx.ln_fm(TC, KB, t1.ap(), "t1", xb.ap(), "xb", sq.ap(), "sq",
                  s1.ap().rearrange("p (a t) -> p a t", t=TC), "s1", t1.ap(), "t1", rows, "rows",
                  ctx.pb[4], "pb4", ctx.pb[5], "pb5")
        p.step("vector", lambda e: e.tensor_add(xch.ap(), xch.ap(), t1.ap()), reads=("xch", "t1"), writes=("xch",))
        ctx.dma("sync", xout_r[:, :, t0:t0 + TC], xch.ap(), reads=("xch",), writes=(f"xo{c}",))

    p.step("gpsimd", lambda e: e.nop(), reads=tuple(f"xo{c}" for c in qrange), tag="final_wait")
    p.emit()
    return ctx


def l1_attn_consts():
    HAD = np.array([[1, 1, 1, 1], [1, -1, 1, -1], [1, 1, -1, -1], [1, -1, -1, 1]], np.float32)
    CDMA = np.tile(HAD, (1, 16))
    rowpat = CDMA[np.arange(128) % 4]
    inv_freq = 1.0 / (10000.0 ** (np.arange(0, 64, 2, dtype=np.float32) / 64))
    pos = np.arange(S, dtype=np.float32)
    freqs = pos[:, None] * inv_freq[None, :]
    emb = np.concatenate([freqs, freqs], axis=1)
    costab = np.cos(emb).astype(np.float32)
    sintab = np.sin(emb).astype(np.float32)
    pidx = np.arange(128)
    slot = pidx // 4
    csl = np.arange(32)
    meq = (csl[None, :] == slot[:, None]).astype(np.float32)
    mfut = (csl[None, :] > slot[:, None]).astype(np.float32)
    mkeep = 1.0 - meq - mfut
    mfutneg = (-1e30 * mfut).astype(np.float32)
    kin = np.arange(128)
    t4 = ((kin[:, None] // 4 == kin[None, :] // 4) & (kin[:, None] <= kin[None, :])).astype(bfloat16)
    s4 = (kin[:, None] // 4 == csl[None, :]).astype(bfloat16)
    e4 = np.zeros((128, 4, 128), np.float32)
    for j in range(4):
        e4[:, j, :] = (kin[:, None] == 32 * j + kin[None, :] // 4)
    e4 = e4.astype(bfloat16)
    return dict(
        costab=costab, sintab=sintab,
        cdmah=np.ascontiguousarray(rowpat / 2.0),
        cdma1=np.ascontiguousarray(rowpat / 8.0),
        cdma2=np.ascontiguousarray(rowpat * 2.0),
        meqt=meq, mkeept=mkeep, mfutnt=mfutneg,
        t4t=t4, s4t=s4, e4t=e4,
        ident_bf=np.eye(128, dtype=bfloat16),
    )


def l1_attn_inmap(x_fm, inp):
    m = dict(
        xin_fm=np.ascontiguousarray(x_fm),
        wq=np.ascontiguousarray(inp["L1_attn_Wq"]), wk=np.ascontiguousarray(inp["L1_attn_Wk"]),
        wv=np.ascontiguousarray(inp["L1_attn_Wv"]), wo=np.ascontiguousarray(inp["L1_attn_Wo"]),
        aq=np.ascontiguousarray(inp["L1_attn_Aq"]), bq=np.ascontiguousarray(inp["L1_attn_Bq"]),
        av=np.ascontiguousarray(inp["L1_attn_Av"]), bv=np.ascontiguousarray(inp["L1_attn_Bv"]),
        ao=np.ascontiguousarray(inp["L1_attn_Ao"]), bo=np.ascontiguousarray(inp["L1_attn_Bo"]),
    )
    m.update(l1_attn_consts())
    return m


_PROGS = {}


def _progs():
    if not _PROGS:
        _PROGS["pre"] = build_ln_pre()
        _PROGS["qkv"] = build_l0a_qkv()
        _PROGS["spec"] = build_l0a_spec()
        _PROGS["l0affn"] = build_l0a_ffn()
        _PROGS["post"] = build_addln()
        _PROGS["l0ffn"] = build_l0_ffn()
        _PROGS["l1attn"] = build_l1_attn()
        _PROGS["l1ffn"] = build_l1_ffn()
        _PROGS["l0c"] = l0_consts()
        _PROGS["l1c"] = l1_attn_consts()
    return _PROGS


def _run(ctx, in_maps):
    res = run_bass_kernel_spmd(ctx.nc, in_maps, core_ids=list(range(len(in_maps))))
    return res.results


def kernel(**inputs):
    P = _progs()
    mfwd, minv_stack, ffm = P["l0c"]
    l1c = P["l1c"]
    B = inputs["L0_attn_x"].shape[0]
    x = [np.ascontiguousarray(inputs["L0_attn_x"][b].T) for b in range(B)]

    xi = [r["xi_fm"] for r in _run(P["pre"], [{"xin_fm": x[b]} for b in range(B)])]
    for li in range(4):
        awp, abp = perm_qkv(inputs["L0_attn_attn_w"][li], inputs["L0_attn_attn_b"][li])
        lw = {
            "attn_w": awp, "attn_bf": fmvec(abp),
            "ln1wf": fmvec(inputs["L0_attn_ln1_w"][li]),
            "ln1bf": fmvec(inputs["L0_attn_ln1_b"][li]),
        }
        w = [r["w_fm"] for r in _run(P["qkv"], [dict(lw, xi_fm=xi[b]) for b in range(B)])]
        sw = {
            "mfwd": mfwd, "minv": minv_stack, "ffm": ffm,
            "aproj_w": np.ascontiguousarray(inputs["L0_attn_aproj_w"][li]),
            "aprojbf": fmvec(inputs["L0_attn_aproj_b"][li]),
        }
        xi = [r["xo_fm"] for r in _run(P["spec"], [dict(sw, xi_fm=xi[b], w_fm=w[b]) for b in range(B)])]
        fw = {
            "fc_w": np.ascontiguousarray(inputs["L0_attn_fc_w"][li]),
            "fcbf": fmvec(inputs["L0_attn_fc_b"][li]),
            "ln2wf": fmvec(inputs["L0_attn_ln2_w"][li]),
            "ln2bf": fmvec(inputs["L0_attn_ln2_b"][li]),
            "mproj_w": np.ascontiguousarray(inputs["L0_attn_mproj_w"][li]),
            "mprojbf": fmvec(inputs["L0_attn_mproj_b"][li]),
        }
        xi = [r["xo_fm"] for r in _run(P["l0affn"], [dict(fw, xi_fm=xi[b]) for b in range(B)])]
    x1 = [r["xout_fm"] for r in _run(P["post"], [{"xa_fm": x[b], "xb_fm": xi[b]} for b in range(B)])]

    x2 = [r["xout_fm"] for r in _run(
        P["l0ffn"],
        [l0_ffn_inmap(x1[b], inputs["L0_ffn_W1"], inputs["L0_ffn_b1"],
                      inputs["L0_ffn_W2"], inputs["L0_ffn_b2"], inputs["L0_ffn_scale"])
         for b in range(B)])]

    l1w = dict(
        wq=np.ascontiguousarray(inputs["L1_attn_Wq"]), wk=np.ascontiguousarray(inputs["L1_attn_Wk"]),
        wv=np.ascontiguousarray(inputs["L1_attn_Wv"]), wo=np.ascontiguousarray(inputs["L1_attn_Wo"]),
        aq=np.ascontiguousarray(inputs["L1_attn_Aq"]), bq=np.ascontiguousarray(inputs["L1_attn_Bq"]),
        av=np.ascontiguousarray(inputs["L1_attn_Av"]), bv=np.ascontiguousarray(inputs["L1_attn_Bv"]),
        ao=np.ascontiguousarray(inputs["L1_attn_Ao"]), bo=np.ascontiguousarray(inputs["L1_attn_Bo"]),
    )
    l1w.update(l1c)
    x3 = [r["xout_fm"] for r in _run(P["l1attn"], [dict(l1w, xin_fm=x2[b]) for b in range(B)])]

    x4 = [r["xout_fm"] for r in _run(
        P["l1ffn"],
        [l1_ffn_inmap(x3[b], inputs["L1_ffn_w1"], inputs["L1_ffn_b1"],
                      inputs["L1_ffn_w2"], inputs["L1_ffn_b2"]) for b in range(B)])]

    return np.stack([np.asarray(x4[b]).T for b in range(B)], axis=0).astype(np.float32)


# version 02
# speedup vs baseline: 1.0064x; 1.0064x over previous
import numpy as np
import ml_dtypes
import concourse.bass as bass
import concourse.mybir as mybir
from concourse.bass_utils import run_bass_kernel_spmd

F32 = mybir.dt.float32
BF16 = mybir.dt.bfloat16
I32 = mybir.dt.int32
AF = mybir.ActivationFunctionType
ALU = mybir.AluOpType
bfloat16 = ml_dtypes.bfloat16

NCORES = 8
D = 1024
S = 4096
KB = D // 128
EPS = 1e-5

SAME_ENGINE_EDGES = False
SAME_ENGINE_DRAIN = True


class Prog:

    def __init__(self, nc):
        self.nc = nc
        self.steps = []
        self.last_w = {}
        self.readers = {}

    def step(self, eng, fn, reads=(), writes=(), dma=False, tag=""):
        deps = set()
        for r in reads:
            w = self.last_w.get(r)
            if w is not None:
                deps.add(w)
        for r in writes:
            w = self.last_w.get(r)
            if w is not None:
                deps.add(w)
            deps.update(self.readers.get(r, ()))
        i = len(self.steps)
        self.steps.append(
            dict(i=i, eng=eng, fn=fn, deps=deps, dma=bool(dma), tag=tag, inc=False)
        )
        for r in reads:
            self.readers.setdefault(r, set()).add(i)
        for r in writes:
            self.last_w[r] = i
            self.readers[r] = set()
        return i

    def barrier(self):
        last = {}
        for st in self.steps:
            last[st["eng"] + (".d" if st["dma"] else "")] = st["i"]
        deps = set(last.values())
        for eng in ("tensor", "vector", "scalar", "gpsimd", "sync"):
            i = self.step(eng, lambda e: e.nop(), writes=(f"__bar{len(self.steps)}__",))
            self.steps[i]["deps"] |= deps

    def emit(self):
        nc = self.nc
        for st in self.steps:
            st["stream"] = st["eng"] + (".d" if st["dma"] else "")
        for st in self.steps:
            if st["dma"]:
                st["inc"] = True
            for d in st["deps"]:
                ds = self.steps[d]
                if ds["stream"] == st["stream"] and not ds["dma"]:
                    if st["eng"] == "tensor" or not SAME_ENGINE_EDGES:
                        continue
                ds["inc"] = True
        NLANES = 8
        streams = {}
        for st in self.steps:
            streams.setdefault(st["stream"], []).append(st)
        for s, lst in streams.items():
            if s.endswith(".d"):
                lane_rank = [0] * NLANES
                i = 0
                for st in lst:
                    st["lane"] = i % NLANES
                    lane_rank[st["lane"]] += 1
                    st["rank"] = lane_rank[st["lane"]]
                    i += 1
            else:
                r = 0
                for st in lst:
                    if st["inc"]:
                        r += 1
                    st["rank"] = r
        import contextlib

        with contextlib.ExitStack() as ex:
            sems = {}
            for s, lst in streams.items():
                if s.endswith(".d"):
                    sems[s] = (
                        [ex.enter_context(nc.semaphore(f"sem_{s.replace('.', '_')}_{l}"))
                         for l in range(NLANES)],
                        16,
                    )
                elif any(st["inc"] for st in lst):
                    sems[s] = (
                        ex.enter_context(nc.semaphore("sem_" + s.replace(".", "_"))),
                        1,
                    )
            by_eng = {}
            for st in self.steps:
                by_eng.setdefault(st["eng"], []).append(st)
            steps_all = self.steps
            with nc.Block() as block:
                for eng_name in list(by_eng.keys()):
                    lst = by_eng[eng_name]

                    def make_body(lst):
                        def body(eng):
                            watermark = {}
                            last_drain = -1
                            for st in lst:
                                if SAME_ENGINE_DRAIN and st["eng"] != "tensor" and not st["dma"]:
                                    need = any(
                                        steps_all[d]["stream"] == st["stream"]
                                        and not steps_all[d]["dma"]
                                        and d > last_drain
                                        for d in st["deps"]
                                    )
                                    if need:
                                        eng.drain(fusable=False)
                                        last_drain = st["i"]
                                for d in sorted(st["deps"]):
                                    ds = steps_all[d]
                                    if ds["stream"] == st["stream"] and not ds["dma"] and (st["eng"] == "tensor" or not SAME_ENGINE_EDGES):
                                        continue
                                    sem, unit = sems[ds["stream"]]
                                    if ds["dma"]:
                                        sem = sem[ds["lane"]]
                                        wkey = (ds["stream"], ds["lane"])
                                    else:
                                        wkey = ds["stream"]
                                    val = ds["rank"] * unit
                                    if watermark.get(wkey, -1) >= val:
                                        continue
                                    eng.wait_ge(sem, val)
                                    watermark[wkey] = val
                                if st["dma"] and st["rank"] > 1:
                                    sem, unit = sems[st["stream"]]
                                    sem = sem[st["lane"]]
                                    wkey = (st["stream"], st["lane"])
                                    val = (st["rank"] - 1) * unit
                                    if watermark.get(wkey, -1) < val:
                                        eng.wait_ge(sem, val)
                                        watermark[wkey] = val
                                ins = st["fn"](eng)
                                if st["inc"]:
                                    sem, unit = sems[st["stream"]]
                                    if st["dma"]:
                                        sem = sem[st["lane"]]
                                    ins.then_inc(sem, unit)
                            return None

                        return body

                    getattr(block, eng_name)(make_body(lst))


class Ctx:
    def __init__(self, name, bf16_banks=()):
        nc = bass.Bass("TRN2", target_bir_lowering=False)
        self.nc = nc
        self.p = Prog(nc)
        self.name = name
        self.pb = [nc.alloc_psum_tensor(f"pb{i}", [128, 1024] if i in bf16_banks else [128, 512],
                                        BF16 if i in bf16_banks else F32) for i in range(8)]
        self.ones_bf = nc.alloc_sbuf_tensor("ones_bf", [1, 128], BF16)
        self.ones_f = nc.alloc_sbuf_tensor("ones_f", [1, 128], F32)
        self.onescol_bf = nc.alloc_sbuf_tensor("onescol_bf", [128, 1], BF16)
        self.eps_col = nc.alloc_sbuf_tensor("eps_col", [128, 1], F32)
        self.p.step("vector", lambda e: e.memset(self.ones_bf[:, :], 1.0), writes=("ones_bf",))
        self.p.step("vector", lambda e: e.memset(self.ones_f[:, :], 1.0), writes=("ones_f",))
        self.p.step("vector", lambda e: e.memset(self.onescol_bf[:, :], 1.0), writes=("onescol_bf",))
        self.p.step("vector", lambda e: e.memset(self.eps_col[:, :], EPS), writes=("eps_col",))
        self._uid = 0
        self.arena_base = None
        self.arena_cur = None
        self.arena_top = None
        self.pfx = ""

    def enable_arena(self, base_bytes=16768, top_bytes=229000):
        self.arena_base = base_bytes
        self.arena_cur = base_bytes
        self.arena_top = top_bytes

    def uid(self):
        self._uid += 1
        return self._uid

    def sb(self, name, shape, dtype):
        if self.arena_base is None:
            return self.nc.alloc_sbuf_tensor(name, shape, dtype)
        import math as _m
        per_part = int(np.prod(shape[1:])) * mybir.dt.size(dtype)
        off = self.arena_cur
        self.arena_cur = (off + per_part + 31) // 32 * 32
        assert self.arena_cur <= self.arena_top, (
            f"arena overflow in phase {self.pfx}: {self.arena_cur} > {self.arena_top} ({name})")
        return self.nc.alloc_sbuf_tensor_at(self.pfx + name, shape, dtype, offset=off)

    def phase(self, name):
        self.p.barrier()
        self.pfx = name + "_"
        self.arena_cur = self.arena_base

    def din(self, name, shape, dtype=F32):
        return self.nc.dram_tensor(name, shape, dtype, kind="ExternalInput").ap()

    def dout(self, name, shape, dtype=F32):
        return self.nc.dram_tensor(name, shape, dtype, kind="ExternalOutput").ap()

    def dscratch(self, name, shape, dtype=F32):
        return self.nc.dram_tensor(name, shape, dtype, kind="Internal").ap()

    def dma(self, queue, out_ap, in_ap, reads=(), writes=(), tag=""):
        return self.p.step(
            queue, lambda e: e.dma_start(out_ap, in_ap), reads=reads, writes=writes,
            dma=True, tag=tag,
        )

    def mm_group(self, psum_ap, psum_key, pairs, tag=""):
        n = len(pairs)
        for j, (l, r, rk) in enumerate(pairs):
            def fn(e, l=l, r=r, j=j, n=n):
                return e.matmul(psum_ap, l, r, start=(j == 0), stop=(j == n - 1))
            self.p.step("tensor", fn, reads=tuple(rk), writes=(psum_key,), tag=tag)

    def ln_fm(self, T, nb, x3, xkey, xb3, xbk, sq3, sqk, t13, t1k, out3, outkey,
              rows, rowsk, bsA, bsAk, bsB, bsBk, bbR=None, bbRk=None, bbM=None, bbMk=None,
              cast_engine="gpsimd"):
        if bbR is None:
            bbR, bbRk, bbM, bbMk = bsA, bsAk, bsB, bsBk
        """LayerNorm over the feature axis, feature-major. x3/xb3/sq3/t13/out3:
        [128, nb, T] views (out3 may be bf16 or f32). Uses 2 stats banks (row 0)
        and 2 broadcast banks (full). T <= 512."""
        assert T <= 512
        p = self.p
        p.step(cast_engine, lambda e: e.tensor_copy(xb3, x3), reads=(xkey,), writes=(xbk,))
        p.step("scalar", lambda e: e.activation(sq3, xb3, AF.Square), reads=(xbk,), writes=(sqk,))
        self.mm_group(bsA[0:1, 0:T], bsAk,
                      [(self.onescol_bf[:, :], xb3[:, i, :], (xbk, "onescol_bf")) for i in range(nb)])
        self.mm_group(bsB[0:1, 0:T], bsBk,
                      [(self.onescol_bf[:, :], sq3[:, i, :], (sqk, "onescol_bf")) for i in range(nb)])
        Dn = float(nb * 128)
        m_r = rows[0:1, 0:T]
        var_r = rows[0:1, T:2 * T]
        rstd_r = rows[0:1, 2 * T:3 * T]
        mistd_r = rows[0:1, 3 * T:4 * T]
        p.step("vector", lambda e: e.tensor_scalar(m_r, bsA[0:1, 0:T], 1.0 / Dn, None, ALU.mult), reads=(bsAk,), writes=(rowsk,))
        p.step("vector", lambda e: e.tensor_mul(var_r, m_r, m_r), reads=(rowsk,), writes=(rowsk,))
        p.step("vector", lambda e: e.scalar_tensor_tensor(var_r, bsB[0:1, 0:T], 1.0 / Dn, var_r, ALU.mult, ALU.subtract), reads=(bsBk, rowsk), writes=(rowsk,))
        p.step("scalar", lambda e: e.activation(rstd_r, var_r, AF.Sqrt, bias=self.eps_col[0:1, 0:1]), reads=(rowsk, "eps_col"), writes=(rowsk,))
        p.step("vector", lambda e: e.reciprocal(rstd_r, rstd_r), reads=(rowsk,), writes=(rowsk,))
        p.step("vector", lambda e: e.tensor_mul(mistd_r, m_r, rstd_r), reads=(rowsk,), writes=(rowsk,))
        self.mm_group(bbR[:, 0:T], bbRk, [(self.ones_f[0:1, :], rstd_r, (rowsk, "ones_f"))])
        self.mm_group(bbM[:, 0:T], bbMk, [(self.ones_f[0:1, :], mistd_r, (rowsk, "ones_f"))])
        rstd_b = bbR[:, 0:T].unsqueeze(1).broadcast_to([128, nb, T])
        mistd_b = bbM[:, 0:T].unsqueeze(1).broadcast_to([128, nb, T])
        p.step("vector", lambda e: e.tensor_mul(t13, x3, rstd_b), reads=(xkey, bbRk), writes=(t1k,))
        p.step("vector", lambda e: e.tensor_sub(out3, t13, mistd_b), reads=(t1k, bbMk), writes=(outkey,))


def fmvec(v):
    return np.ascontiguousarray(np.asarray(v, dtype=np.float32).reshape(-1, 128).T)


def build_l1_ffn(Ttot=S, ctx=None, io=None):
    standalone = ctx is None
    if standalone:
        ctx = Ctx("l1ffnX")
        io = dict(xin_fm=ctx.din("xin_fm", [D, S]), w1=ctx.din("w1", [D, 4096]),
                  w2=ctx.din("w2", [4096, D]), b1f=ctx.din("b1f", [128, 32]),
                  b2f=ctx.din("b2f", [128, KB]), xout_fm=ctx.dout("xout_fm", [D, S]))
    else:
        ctx.phase("F1")
    nc, p = ctx.nc, ctx.p
    TC = 256
    nch = Ttot // TC
    H = 4096
    HB = H // 128

    xin = io["xin_fm"]
    w1 = io["w1"]
    w2 = io["w2"]
    b1f = io["b1f"]
    b2f = io["b2f"]
    xout = io["xout_fm"]

    stage = ctx.sb("stage", [128, 2048], F32)
    w1b = ctx.sb("w1b", [128, KB, H], BF16)
    w2b = ctx.sb("w2b", [128, HB, D], BF16)
    b1s = ctx.sb("b1s", [128, HB], F32)
    b2s = ctx.sb("b2s", [128, KB], F32)
    ctx.dma("sync", b1s[:, :], b1f, writes=("b1s",))
    ctx.dma("sync", b2s[:, :], b2f, writes=("b2s",))
    for kc in range(KB):
        for hh in range(2):
            ctx.dma("sync", stage[:, :], w1[kc * 128:(kc + 1) * 128, hh * 2048:(hh + 1) * 2048], writes=("stage",))
            p.step("gpsimd", lambda e, kc=kc, hh=hh: e.tensor_copy(w1b[:, kc, hh * 2048:(hh + 1) * 2048], stage[:, :]), reads=("stage",), writes=("w1b",))
    for hc in range(HB):
        ctx.dma("sync", stage[:, 0:D], w2[hc * 128:(hc + 1) * 128, :], writes=("stage",))
        p.step("gpsimd", lambda e, hc=hc: e.tensor_copy(w2b[:, hc, :], stage[:, 0:D]), reads=("stage",), writes=("w2b",))

    xch = ctx.sb("xch", [128, KB, TC], F32)
    xb = ctx.sb("xb", [128, KB, TC], BF16)
    sq = ctx.sb("sq", [128, KB, TC], BF16)
    t1 = ctx.sb("t1", [128, KB, TC], F32)
    xnb = ctx.sb("xnb", [128, KB, TC], BF16)
    hb = ctx.sb("hb", [128, HB, TC], BF16)
    yf = ctx.sb("yf", [128, KB, TC], F32)
    rows = ctx.sb("rows", [1, 2048], F32)

    xin_r = xin.rearrange("(a p) t -> p a t", p=128)
    xout_r = xout.rearrange("(a p) t -> p a t", p=128)

    for c in range(nch):
        t0 = c * TC
        ctx.dma("sync", xch.ap(), xin_r[:, :, t0:t0 + TC], writes=("xch",), tag=f"xld{c}")
        ctx.ln_fm(TC, KB, xch.ap(), "xch", xb.ap(), "xb", sq.ap(), "sq",
                  t1.ap(), "t1", xnb.ap(), "xnb", rows, "rows",
                  ctx.pb[6], "pb6", ctx.pb[7], "pb7")
        for mb in range(HB):
            bank, bk = ctx.pb[mb % 2], f"pb{mb % 2}"
            ctx.mm_group(bank[:, 0:TC], bk,
                         [(w1b[:, kc, mb * 128:(mb + 1) * 128], xnb[:, kc, :], ("w1b", "xnb")) for kc in range(KB)],
                         tag=f"fc{c}_{mb}")
            p.step("scalar", lambda e, mb=mb, bank=bank: e.activation(hb[:, mb, :], bank[:, 0:TC], AF.Gelu, bias=b1s[:, mb:mb + 1]),
                   reads=(bk, "b1s"), writes=("hb",), tag=f"gelu{c}_{mb}")
        for ob in range(KB):
            bank, bk = ctx.pb[2 + ob % 2], f"pb{2 + ob % 2}"
            ctx.mm_group(bank[:, 0:TC], bk,
                         [(w2b[:, hc, ob * 128:(ob + 1) * 128], hb[:, hc, :], ("w2b", "hb")) for hc in range(HB)],
                         tag=f"pj{c}_{ob}")
            p.step("scalar", lambda e, ob=ob, bank=bank: e.activation(yf[:, ob, :], bank[:, 0:TC], AF.Identity, bias=b2s[:, ob:ob + 1]),
                   reads=(bk, "b2s"), writes=("yf",), tag=f"bias{c}_{ob}")
        ctx.ln_fm(TC, KB, yf.ap(), "yf", xb.ap(), "xb", sq.ap(), "sq",
                  t1.ap(), "t1", t1.ap(), "t1", rows, "rows",
                  ctx.pb[6], "pb6", ctx.pb[7], "pb7")
        p.step("vector", lambda e: e.tensor_add(xch.ap(), xch.ap(), t1.ap()), reads=("xch", "t1"), writes=("xch",), tag=f"resadd{c}")
        ctx.dma("sync", xout_r[:, :, t0:t0 + TC], xch.ap(), reads=("xch",), writes=(f"xout{c}",), tag=f"xst{c}")

    if standalone:
        p.step("gpsimd", lambda e: e.nop(), reads=tuple(f"xout{c}" for c in range(nch)), tag="final_wait")
        ctx.p.emit()
    return ctx


def l1_ffn_inmap(x_fm, w1, b1, w2, b2):
    return {
        "xin_fm": np.ascontiguousarray(x_fm),
        "w1": np.ascontiguousarray(w1), "w2": np.ascontiguousarray(w2),
        "b1f": fmvec(b1), "b2f": fmvec(b2),
    }


def build_l0_ffn(Ttot=S, ctx=None, io=None):
    standalone = ctx is None
    if standalone:
        ctx = Ctx("l0ffnX")
        io = dict(xin_fm=ctx.din("xin_fm", [D, S]), w1t=ctx.din("w1t", [D, 4096]),
                  w2t=ctx.din("w2t", [4096, D]), b1f=ctx.din("b1f", [128, 32]),
                  b2f=ctx.din("b2f", [128, KB]), scale1=ctx.din("scale1", [1, 1]),
                  ident_bf=ctx.din("ident_bf", [128, 128], BF16),
                  xout_fm=ctx.dout("xout_fm", [D, S]))
    else:
        ctx.phase("F0")
    nc, p = ctx.nc, ctx.p
    TC = 128
    nch = Ttot // TC
    H = 4096
    HB = H // 128

    xin = io["xin_fm"]
    w1t = io["w1t"]
    w2t = io["w2t"]
    b1f = io["b1f"]
    b2f = io["b2f"]
    scale1 = io["scale1"]
    identb = io["ident_bf"]
    xout = io["xout_fm"]

    stage = ctx.sb("stage", [128, 1024], F32)
    w1b = ctx.sb("w1b", [128, KB, H], BF16)
    w2b = ctx.sb("w2b", [128, HB, D], BF16)
    b1s = ctx.sb("b1s", [128, HB], F32)
    b2s = ctx.sb("b2s", [128, KB], F32)
    idb = ctx.sb("idb", [128, 128], BF16)
    scs = ctx.sb("scs", [1, 1], F32)
    ctx.dma("sync", b1s[:, :], b1f, writes=("b1s",))
    ctx.dma("sync", b2s[:, :], b2f, writes=("b2s",))
    ctx.dma("sync", idb[:, :], identb, writes=("idb",))
    ctx.dma("sync", scs[:, :], scale1, writes=("scs",))
    for kc in range(KB):
        for hh in range(4):
            ctx.dma("sync", stage[:, :], w1t[kc * 128:(kc + 1) * 128, hh * 1024:(hh + 1) * 1024], writes=("stage",))
            p.step("gpsimd", lambda e, kc=kc, hh=hh: e.tensor_copy(w1b[:, kc, hh * 1024:(hh + 1) * 1024], stage[:, :]), reads=("stage",), writes=("w1b",))
    for hc in range(HB):
        ctx.dma("sync", stage[:, 0:D], w2t[hc * 128:(hc + 1) * 128, :], writes=("stage",), tag=f"w2dma{hc}")
        p.step("gpsimd", lambda e, hc=hc: e.tensor_copy(w2b[:, hc, :], stage[:, 0:D]), reads=("stage",), writes=("w2b",))

    xch = ctx.sb("xch", [128, KB, TC], F32)
    xb = ctx.sb("xb", [128, KB, TC], BF16)
    sq = ctx.sb("sq", [128, KB, TC], BF16)
    t1 = ctx.sb("t1", [128, KB, TC], F32)
    xnb = ctx.sb("xnb", [128, KB, TC], BF16)
    hb = ctx.sb("hb", [128, HB, TC], BF16)
    yf = ctx.sb("yf", [128, KB, TC], F32)
    rows = ctx.sb("rows", [1, 4 * TC], F32)
    odd1f = ctx.sb("odd1f", [128, KB, 4], BF16)
    even1T = ctx.sb("even1T", [1, 4 * D], BF16)
    even2T = ctx.sb("even2T", [1, 4 * D], BF16)
    odd2f = ctx.sb("odd2f", [128, KB, 4], BF16)
    tib = ctx.sb("tib", [1, 4 * TC], BF16)
    sub = ctx.sb("sub", [1, 16 * TC], BF16)

    xin_r = xin.rearrange("(a p) t -> p a t", p=128)
    xout_r = xout.rearrange("(a p) t -> p a t", p=128)

    trb = ctx.pb[3][:, :].bitcast(BF16)

    for c in range(nch):
        t0 = c * TC
        ctx.dma("sync", xch.ap(), xin_r[:, :, t0:t0 + TC], writes=("xch",), tag=f"xld{c}")
        ctx.ln_fm(TC, KB, xch.ap(), "xch", xb.ap(), "xb", sq.ap(), "sq",
                  t1.ap(), "t1", xnb.ap(), "xnb", rows, "rows",
                  ctx.pb[6], "pb6", ctx.pb[7], "pb7")
        if c == 0:
            p.step("vector", lambda e: e.tensor_copy(odd1f.ap(), xnb.ap()[:, :, 2:9:2]), reads=("xnb",), writes=("odd1f",))
            p.step("vector", lambda e: e.tensor_copy(odd2f.ap(), xnb.ap()[:, :, 10:17:2]), reads=("xnb",), writes=("odd2f",))
            for i in range(4):
                for kc in range(KB):
                    tok_e1 = 1 + 2 * i
                    tok_e2 = 9 + 2 * i
                    p.step("tensor", lambda e, kc=kc, tok=tok_e1: e.transpose(trb[0:1, 0:128], xnb[:, kc, tok:tok + 1], idb[:, :]),
                           reads=("xnb", "idb"), writes=("pb3",))
                    p.step("vector", lambda e, i=i, kc=kc: e.tensor_copy(even1T[0:1, i * D + kc * 128: i * D + (kc + 1) * 128], trb[0:1, 0:128]),
                           reads=("pb3",), writes=("even1T",))
                    p.step("tensor", lambda e, kc=kc, tok=tok_e2: e.transpose(trb[0:1, 0:128], xnb[:, kc, tok:tok + 1], idb[:, :]),
                           reads=("xnb", "idb"), writes=("pb3",))
                    p.step("vector", lambda e, i=i, kc=kc: e.tensor_copy(even2T[0:1, i * D + kc * 128: i * D + (kc + 1) * 128], trb[0:1, 0:128]),
                           reads=("pb3",), writes=("even2T",))
        for i in range(4):
            bank, bk = ctx.pb[2], "pb2"
            ctx.mm_group(bank[0:1, 0:TC], bk,
                         [(odd1f[:, kc, i:i + 1], xnb[:, kc, :], ("odd1f", "xnb")) for kc in range(KB)],
                         tag=f"t{c}_{i}")
            p.step("vector", lambda e, i=i, bank=bank: e.tensor_scalar(tib[0:1, i * TC:(i + 1) * TC], bank[0:1, 0:TC], scs[0:1, 0:1], None, ALU.mult),
                   reads=(bk, "scs"), writes=("tib",))
        for mb in range(HB):
            bank, bk = ctx.pb[mb % 2], f"pb{mb % 2}"
            i = mb // 8
            pairs = [(w1b[:, kc, mb * 128:(mb + 1) * 128], xnb[:, kc, :], ("w1b", "xnb")) for kc in range(KB)]
            pairs.append((even1T[0:1, i * D + (mb % 8) * 128: i * D + (mb % 8 + 1) * 128], tib[0:1, i * TC:(i + 1) * TC], ("even1T", "tib")))
            ctx.mm_group(bank[:, 0:TC], bk, pairs, tag=f"fc{c}_{mb}")
            p.step("scalar", lambda e, mb=mb, bank=bank: e.activation(hb[:, mb, :], bank[:, 0:TC], AF.Gelu, bias=b1s[:, mb:mb + 1]),
                   reads=(bk, "b1s"), writes=("hb",), tag=f"gelu{c}_{mb}")
        for i in range(4):
            for j in range(4):
                bank, bk = ctx.pb[2], "pb2"
                ctx.mm_group(bank[0:1, 0:TC], bk,
                             [(odd2f[:, kc, i:i + 1], hb[:, j * 8 + kc, :], ("odd2f", "hb")) for kc in range(KB)],
                             tag=f"u{c}_{i}{j}")
                p.step("vector", lambda e, i=i, j=j, bank=bank: e.tensor_scalar(sub[0:1, (i * 4 + j) * TC:(i * 4 + j + 1) * TC], bank[0:1, 0:TC], scs[0:1, 0:1], None, ALU.mult),
                       reads=(bk, "scs"), writes=("sub",))
        for ob in range(KB):
            bank, bk = ctx.pb[2 + ob % 2] if False else ctx.pb[ob % 2], f"pb{ob % 2}"
            i = ob // 2
            r0 = (ob % 2) * 128
            pairs = [(w2b[:, hc, ob * 128:(ob + 1) * 128], hb[:, hc, :], ("w2b", "hb")) for hc in range(HB)]
            for j in range(4):
                st0 = i * D + 4 * r0 + j
                eap = even2T[0:1, st0: st0 + 4 * 127 + 1: 4]
                pairs.append((eap, sub[0:1, (i * 4 + j) * TC:(i * 4 + j + 1) * TC], ("even2T", "sub")))
            ctx.mm_group(bank[:, 0:TC], bk, pairs, tag=f"pj{c}_{ob}")
            p.step("scalar", lambda e, ob=ob, bank=bank: e.activation(yf[:, ob, :], bank[:, 0:TC], AF.Identity, bias=b2s[:, ob:ob + 1]),
                   reads=(bk, "b2s"), writes=("yf",), tag=f"bias{c}_{ob}")
        ctx.ln_fm(TC, KB, yf.ap(), "yf", xb.ap(), "xb", sq.ap(), "sq",
                  t1.ap(), "t1", t1.ap(), "t1", rows, "rows",
                  ctx.pb[6], "pb6", ctx.pb[7], "pb7")
        p.step("vector", lambda e: e.tensor_add(xch.ap(), xch.ap(), t1.ap()), reads=("xch", "t1"), writes=("xch",), tag=f"resadd{c}")
        ctx.dma("sync", xout_r[:, :, t0:t0 + TC], xch.ap(), reads=("xch",), writes=(f"xout{c}",), tag=f"xst{c}")

    if standalone:
        p.step("gpsimd", lambda e: e.nop(), reads=tuple(f"xout{c}" for c in range(nch)), tag="final_wait")
        ctx.p.emit()
    return ctx


def l0_ffn_inmap(x_fm, W1, b1, W2, b2, scale):
    return {
        "xin_fm": np.ascontiguousarray(x_fm),
        "w1t": np.ascontiguousarray(W1.T), "w2t": np.ascontiguousarray(W2.T),
        "b1f": fmvec(b1), "b2f": fmvec(b2),
        "scale1": np.asarray(scale, np.float32).reshape(1, 1),
        "ident_bf": np.eye(128, dtype=bfloat16),
    }




def build_ln_pre(Ttot=S, ctx=None, io=None):
    standalone = ctx is None
    if standalone:
        ctx = Ctx("lnpreX")
        io = dict(xin_fm=ctx.din("xin_fm", [D, S]), xi_fm=ctx.dout("xi_fm", [D, S]))
    else:
        ctx.phase("PR")
    p = ctx.p
    TC = 256
    nch = Ttot // TC
    xin = io["xin_fm"]
    xout = io["xi_fm"]
    xch = ctx.sb("xch", [128, KB, TC], F32)
    xb = ctx.sb("xb", [128, KB, TC], BF16)
    sq = ctx.sb("sq", [128, KB, TC], BF16)
    t1 = ctx.sb("t1", [128, KB, TC], F32)
    rows = ctx.sb("rows", [1, 4 * TC], F32)
    xin_r = xin.rearrange("(a p) t -> p a t", p=128)
    xout_r = xout.rearrange("(a p) t -> p a t", p=128)
    for c in range(nch):
        t0 = c * TC
        ctx.dma("sync", xch.ap(), xin_r[:, :, t0:t0 + TC], writes=("xch",))
        ctx.ln_fm(TC, KB, xch.ap(), "xch", xb.ap(), "xb", sq.ap(), "sq",
                  t1.ap(), "t1", t1.ap(), "t1", rows, "rows",
                  ctx.pb[6], "pb6", ctx.pb[7], "pb7")
        ctx.dma("sync", xout_r[:, :, t0:t0 + TC], t1.ap(), reads=("t1",), writes=(f"xo{c}",))
    if standalone:
        p.step("gpsimd", lambda e: e.nop(), reads=tuple(f"xo{c}" for c in range(nch)), tag="final_wait")
        p.emit()
    return ctx


def build_addln(Ttot=S, ctx=None, io=None):
    standalone = ctx is None
    if standalone:
        ctx = Ctx("addlnX")
        io = dict(xa_fm=ctx.din("xa_fm", [D, S]), xb_fm=ctx.din("xb_fm", [D, S]),
                  xout_fm=ctx.dout("xout_fm", [D, S]))
    else:
        ctx.phase("PO")
    p = ctx.p
    TC = 256
    nch = Ttot // TC
    xa = io["xa_fm"]
    xbin = io["xb_fm"]
    xout = io["xout_fm"]
    ach = ctx.sb("ach", [128, KB, TC], F32)
    xch = ctx.sb("xch", [128, KB, TC], F32)
    xb = ctx.sb("xb", [128, KB, TC], BF16)
    sq = ctx.sb("sq", [128, KB, TC], BF16)
    t1 = ctx.sb("t1", [128, KB, TC], F32)
    rows = ctx.sb("rows", [1, 4 * TC], F32)
    xa_r = xa.rearrange("(a p) t -> p a t", p=128)
    xb_r = xbin.rearrange("(a p) t -> p a t", p=128)
    xout_r = xout.rearrange("(a p) t -> p a t", p=128)
    for c in range(nch):
        t0 = c * TC
        ctx.dma("sync", ach.ap(), xa_r[:, :, t0:t0 + TC], writes=("ach",))
        ctx.dma("sync", xch.ap(), xb_r[:, :, t0:t0 + TC], writes=("xch",))
        ctx.ln_fm(TC, KB, xch.ap(), "xch", xb.ap(), "xb", sq.ap(), "sq",
                  t1.ap(), "t1", t1.ap(), "t1", rows, "rows",
                  ctx.pb[6], "pb6", ctx.pb[7], "pb7")
        p.step("vector", lambda e: e.tensor_add(ach.ap(), ach.ap(), t1.ap()), reads=("ach", "t1"), writes=("ach",))
        ctx.dma("sync", xout_r[:, :, t0:t0 + TC], ach.ap(), reads=("ach",), writes=(f"xo{c}",))
    if standalone:
        p.step("gpsimd", lambda e: e.nop(), reads=tuple(f"xo{c}" for c in range(nch)), tag="final_wait")
        p.emit()
    return ctx


def _load_scaled_weight(ctx, wname, w_hbm, Kin, Nout, stage, stage_cols, wb, wbk,
                        lnwb=None, lnbb=None, biasf=None, beff=None, beffk=None,
                        bank=None, bankk=None):
    p = ctx.p
    nk = Kin // 128
    nm = Nout // 128
    for kc in range(nk):
        for h0 in range(0, Nout, stage_cols):
            hw = min(stage_cols, Nout - h0)
            ctx.dma("sync", stage[:, 0:hw], w_hbm[kc * 128:(kc + 1) * 128, h0:h0 + hw], writes=("stage",))
            p.step("gpsimd", lambda e, kc=kc, h0=h0, hw=hw: e.tensor_copy(wb[:, kc, h0:h0 + hw], stage[:, 0:hw]), reads=("stage",), writes=(wbk,))
    if lnbb is not None:
        for mb in range(nm):
            ctx.mm_group(bank[:, 0:1], bankk,
                         [(wb[:, kc, mb * 128:(mb + 1) * 128], lnbb[:, kc:kc + 1], (wbk, "lnbb")) for kc in range(nk)])
            p.step("vector", lambda e, mb=mb: e.scalar_tensor_tensor(beff[:, mb:mb + 1], bank[:, 0:1], 1.0, biasf[:, mb:mb + 1], ALU.mult, ALU.add),
                   reads=(bankk, "biasf"), writes=(beffk,))
    if lnwb is not None:
        for kc in range(nk):
            p.step("vector", lambda e, kc=kc: e.tensor_scalar(wb[:, kc, :], wb[:, kc, :], lnwb[:, kc:kc + 1], None, ALU.mult),
                   reads=(wbk, "lnwb"), writes=(wbk,))


def build_l0a_qkv(Ttot=S, ctx=None, io=None, pfx=""):
    standalone = ctx is None
    if standalone:
        ctx = Ctx("l0aqkvX")
        io = dict(xi_fm=ctx.din("xi_fm", [D, S]), attn_w=ctx.din("attn_w", [D, 3072]),
                  attn_bf=ctx.din("attn_bf", [128, 24]), ln1wf=ctx.din("ln1wf", [128, KB]),
                  ln1bf=ctx.din("ln1bf", [128, KB]), w_fm=ctx.dout("w_fm", [D, S], BF16))
    else:
        ctx.phase("Q" + pfx)
    nc, p = ctx.nc, ctx.p
    TC = 128
    nch = Ttot // TC
    NQ = 24

    xi = io["xi_fm"]
    aw = io["attn_w"]
    abf = io["attn_bf"]
    lnwf = io["ln1wf"]
    lnbf = io["ln1bf"]
    wout = io["w_fm"]

    stage = ctx.sb("stage", [128, 1536], F32)
    awb = ctx.sb("awb", [128, KB, 3072], BF16)
    lnw = ctx.sb("lnw", [128, KB], F32)
    lnb = ctx.sb("lnb", [128, KB], F32)
    lnbb = ctx.sb("lnbb", [128, KB], BF16)
    abs_ = ctx.sb("abs", [128, NQ], F32)
    beff = ctx.sb("beff", [128, NQ], F32)
    ctx.dma("sync", lnw[:, :], lnwf, writes=("lnwb",))
    ctx.dma("sync", lnb[:, :], lnbf, writes=("lnbq",))
    ctx.dma("sync", abs_[:, :], abf, writes=("biasf",))
    p.step("vector", lambda e: e.tensor_copy(lnbb[:, :], lnb[:, :]), reads=("lnbq",), writes=("lnbb",))
    _load_scaled_weight(ctx, "aw", aw, D, 3072, stage, 1536, awb, "awb",
                        lnwb=lnw, lnbb=lnbb, biasf=abs_, beff=beff, beffk="beff",
                        bank=ctx.pb[2], bankk="pb2")

    xch = ctx.sb("xch", [128, KB, TC], F32)
    xb = ctx.sb("xb", [128, KB, TC], BF16)
    sq = ctx.sb("sq", [128, KB, TC], BF16)
    t1 = ctx.sb("t1", [128, KB, TC], F32)
    xnb = ctx.sb("xnb", [128, KB, TC], BF16)
    rows = ctx.sb("rows", [1, 4 * TC], F32)
    qkv = ctx.sb("qkv", [128, NQ, TC], F32)
    qprev = ctx.sb("qprev", [128, NQ, 1], F32)
    dqkv = ctx.sb("dqkv", [128, NQ, TC], F32)
    rqkv = ctx.sb("rqkv", [128, NQ, TC], F32)
    cqkv = ctx.sb("cqkv", [128, NQ, TC], F32)
    s1 = ctx.sb("s1", [128, NQ, TC], F32)
    si = ctx.sb("si", [128, NQ, TC], I32)
    qk0 = ctx.sb("qk0", [128, KB, TC], F32)
    qk1 = ctx.sb("qk1", [128, KB, TC], F32)
    qk2 = ctx.sb("qk2", [128, KB, TC], F32)
    qk3 = ctx.sb("qk3", [128, KB, TC], F32)
    scr = ctx.sb("scr", [128, KB, TC], F32)
    wtmp = ctx.sb("wtmp", [128, KB, TC], F32)
    wb16 = ctx.sb("wb16", [128, KB, TC], BF16)

    xi_r = xi.rearrange("(a p) t -> p a t", p=128)
    wout_r = wout.rearrange("(a p) t -> p a t", p=128)

    INV2PI = float(1.0 / (2.0 * np.pi))
    TWOPI = float(2.0 * np.pi)

    for c in range(nch):
        t0 = c * TC
        ctx.dma("sync", xch.ap(), xi_r[:, :, t0:t0 + TC], writes=("xch",))
        ctx.ln_fm(TC, KB, xch.ap(), "xch", xb.ap(), "xb", sq.ap(), "sq",
                  t1.ap(), "t1", xnb.ap(), "xnb", rows, "rows",
                  ctx.pb[6], "pb6", ctx.pb[7], "pb7")
        for mb in range(NQ):
            bank, bk = ctx.pb[mb % 2], f"pb{mb % 2}"
            ctx.mm_group(bank[:, 0:TC], bk,
                         [(awb[:, kc, mb * 128:(mb + 1) * 128], xnb[:, kc, :], ("awb", "xnb")) for kc in range(KB)])
            p.step("scalar", lambda e, mb=mb, bank=bank: e.activation(qkv[:, mb, :], bank[:, 0:TC], AF.Identity, bias=beff[:, mb:mb + 1]),
                   reads=(bk, "beff"), writes=("qkv",))
        p.step("vector", lambda e: e.tensor_sub(dqkv.ap()[:, :, 1:TC], qkv.ap()[:, :, 1:TC], qkv.ap()[:, :, 0:TC - 1]), reads=("qkv",), writes=("dqkv",))
        if c == 0:
            p.step("vector", lambda e: e.memset(dqkv.ap()[:, :, 0:1], 0.0), writes=("dqkv",))
        else:
            p.step("vector", lambda e: e.tensor_sub(dqkv.ap()[:, :, 0:1], qkv.ap()[:, :, 0:1], qprev.ap()), reads=("qkv", "qprev"), writes=("dqkv",))
        p.step("vector", lambda e: e.tensor_copy(qprev.ap(), qkv.ap()[:, :, TC - 1:TC]), reads=("qkv",), writes=("qprev",))
        p.step("vector", lambda e: e.tensor_copy(rqkv.ap()[64:128, :, :], qkv.ap()[0:64, :, :]), reads=("qkv",), writes=("rqkv",))
        for sec in range(3):
            b0 = sec * 8
            p.step("vector", lambda e, b0=b0: e.tensor_copy(rqkv.ap()[0:64, b0 + 1:b0 + 8, :], qkv.ap()[64:128, b0:b0 + 7, :]), reads=("qkv",), writes=("rqkv",))
            p.step("vector", lambda e, b0=b0: e.tensor_copy(rqkv.ap()[0:64, b0:b0 + 1, :], qkv.ap()[64:128, b0 + 7:b0 + 8, :]), reads=("qkv",), writes=("rqkv",))
        p.step("vector", lambda e: e.tensor_scalar(s1.ap(), qkv.ap(), -INV2PI, 0.25, ALU.mult, ALU.add), reads=("qkv",), writes=("s1",))
        p.step("vector", lambda e: e.tensor_copy(si.ap(), s1.ap()), reads=("s1",), writes=("si",))
        p.step("vector", lambda e: e.tensor_sub(s1.ap(), s1.ap(), si.ap()), reads=("s1", "si"), writes=("s1",))
        p.step("scalar", lambda e: e.activation(cqkv.ap(), s1.ap(), AF.Sin, scale=TWOPI), reads=("s1",), writes=("cqkv",))
        q, k, v = qkv[:, 0:8, :], qkv[:, 8:16, :], qkv[:, 16:24, :]
        dq, dk, dv = dqkv[:, 0:8, :], dqkv[:, 8:16, :], dqkv[:, 16:24, :]
        rq, rk, rv = rqkv[:, 0:8, :], rqkv[:, 8:16, :], rqkv[:, 16:24, :]
        cq, ck, cv = cqkv[:, 0:8, :], cqkv[:, 8:16, :], cqkv[:, 16:24, :]
        def acc4(out, outk, terms):
            (s0, a0, a0k, b0, b0k) = terms[0]
            assert s0 == 1
            p.step("vector", lambda e: e.tensor_mul(out, a0, b0), reads=(a0k, b0k), writes=(outk,))
            for (sg, a, akk, b, bkk) in terms[1:]:
                p.step("vector", lambda e, a=a, b=b: e.tensor_mul(scr.ap(), a, b), reads=(akk, bkk), writes=("scr",))
                op = ALU.add if sg > 0 else ALU.subtract
                p.step("vector", lambda e, op=op: e.tensor_tensor(out, out, scr.ap(), op), reads=(outk, "scr"), writes=(outk,))
        Q, K_, V, DQ, DK, DV, RQ, RK, RV, CQ, CK, CV = ("qkv",) * 3 + ("dqkv",) * 3 + ("rqkv",) * 3 + ("cqkv",) * 3
        acc4(qk0.ap(), "qk0", [(1, q, Q, k, K_), (-1, dq, DQ, dk, DK), (-1, rq, RQ, rk, RK), (-1, cq, CQ, ck, CK)])
        acc4(qk1.ap(), "qk1", [(1, q, Q, dk, DK), (1, dq, DQ, k, K_), (1, rq, RQ, ck, CK), (-1, cq, CQ, rk, RK)])
        acc4(qk2.ap(), "qk2", [(1, q, Q, rk, RK), (-1, dq, DQ, ck, CK), (1, rq, RQ, k, K_), (1, cq, CQ, dk, DK)])
        acc4(qk3.ap(), "qk3", [(1, q, Q, ck, CK), (1, dq, DQ, rk, RK), (-1, rq, RQ, dk, DK), (1, cq, CQ, k, K_)])
        p.step("vector", lambda e: e.tensor_mul(wtmp.ap(), qk0.ap(), v), reads=("qk0", V), writes=("wtmp",))
        for (qq, qqk, bb, bbk) in [(qk1, "qk1", dv, DV), (qk2, "qk2", rv, RV)]:
            p.step("vector", lambda e, qq=qq, bb=bb: e.tensor_mul(scr.ap(), qq.ap(), bb), reads=(qqk, bbk), writes=("scr",))
            p.step("vector", lambda e: e.tensor_sub(wtmp.ap(), wtmp.ap(), scr.ap()), reads=("wtmp", "scr"), writes=("wtmp",))
        p.step("vector", lambda e: e.tensor_mul(scr.ap(), qk3.ap(), cv), reads=("qk3", CV), writes=("scr",))
        p.step("vector", lambda e: e.tensor_sub(wb16.ap(), wtmp.ap(), scr.ap()), reads=("wtmp", "scr"), writes=("wb16",))
        ctx.dma("sync", wout_r[:, :, t0:t0 + TC], wb16.ap(), reads=("wb16",), writes=(f"wo{c}",))

    if standalone:
        p.step("gpsimd", lambda e: e.nop(), reads=tuple(f"wo{c}" for c in range(nch)), tag="final_wait")
        p.emit()
    return ctx


def build_l0a_spec(Ttot=S, ctx=None, io=None, pfx=""):
    standalone = ctx is None
    if standalone:
        ctx = Ctx("l0aspecX")
        io = dict(xi_fm=ctx.din("xi_fm", [D, S]), w_fm=ctx.din("w_fm", [D, S], BF16),
                  mfwd=ctx.din("mfwd", [D, 2048], BF16), minv=ctx.din("minv", [2048, D], BF16),
                  ffm=ctx.din("ffm", [2048, S]), aproj_w=ctx.din("aproj_w", [D, D]),
                  aprojbf=ctx.din("aprojbf", [128, KB]), xo_fm=ctx.dout("xo_fm", [D, S]))
    else:
        ctx.phase("S" + pfx)
    nc, p = ctx.nc, ctx.p
    TC = 512
    nch = Ttot // TC

    xi = io["xi_fm"]
    wfm = io["w_fm"]
    mfwd = io["mfwd"]
    minv = io["minv"]
    ffm = io["ffm"]
    apw = io["aproj_w"]
    apbf = io["aprojbf"]
    xo = io["xo_fm"]

    mfwdb = ctx.sb("mfwdb", [128, KB, 2048], BF16)
    minvb = ctx.sb("minvb", [128, 16, D], BF16)
    apb16 = ctx.sb("apb16", [128, KB, D], BF16)
    apb = ctx.sb("apb", [128, KB], F32)
    stage = ctx.sb("stage", [128, 1024], F32)
    ctx.dma("sync", apb[:, :], apbf, writes=("apb",))
    mfwd_r = mfwd.rearrange("(a p) g -> p a g", p=128)
    minv_r = minv.rearrange("(a p) f -> p a f", p=128)
    ctx.dma("sync", mfwdb.ap(), mfwd_r, writes=("mfwdb",))
    ctx.dma("sync", minvb.ap(), minv_r, writes=("minvb",))
    _load_scaled_weight(ctx, "apw", apw, D, D, stage, 1024, apb16, "apb16")

    wch = ctx.sb("wch", [128, KB, TC], BF16)
    fch = ctx.sb("fch", [128, 16, TC], F32)
    wfb = ctx.sb("wfb", [128, 16, TC], BF16)
    out0b = ctx.sb("out0b", [128, KB, TC], BF16)
    tA = ctx.sb("tA", [128, TC], F32)
    tB = ctx.sb("tB", [128, TC], F32)
    xich = ctx.sb("xich", [128, KB, TC], F32)

    xi_r = xi.rearrange("(a p) t -> p a t", p=128)
    w_r = wfm.rearrange("(a p) t -> p a t", p=128)
    f_r = ffm.rearrange("(a p) t -> p a t", p=128)
    xo_r = xo.rearrange("(a p) t -> p a t", p=128)

    for c in range(nch):
        t0 = c * TC
        ctx.dma("sync", wch.ap(), w_r[:, :, t0:t0 + TC], writes=("wch",))
        ctx.dma("sync", fch.ap(), f_r[:, :, t0:t0 + TC], writes=("fch",))
        ctx.dma("sync", xich.ap(), xi_r[:, :, t0:t0 + TC], writes=("xich",))
        for gp in range(8):
            bA, bAk = ctx.pb[0], "pb0"
            bB, bBk = ctx.pb[1], "pb1"
            ctx.mm_group(bA[:, 0:TC], bAk,
                         [(mfwdb[:, kc, gp * 128:(gp + 1) * 128], wch[:, kc, :], ("mfwdb", "wch")) for kc in range(KB)])
            ctx.mm_group(bB[:, 0:TC], bBk,
                         [(mfwdb[:, kc, 1024 + gp * 128:1024 + (gp + 1) * 128], wch[:, kc, :], ("mfwdb", "wch")) for kc in range(KB)])
            Fr = fch[:, gp, :]
            Fi = fch[:, 8 + gp, :]
            p.step("vector", lambda e, bA=bA, Fr=Fr: e.tensor_mul(tA.ap(), bA[:, 0:TC], Fr), reads=(bAk, "fch"), writes=("tA",))
            p.step("vector", lambda e, bB=bB, Fi=Fi: e.tensor_mul(tB.ap(), bB[:, 0:TC], Fi), reads=(bBk, "fch"), writes=("tB",))
            p.step("vector", lambda e, gp=gp: e.tensor_sub(wfb[:, gp, :], tA.ap(), tB.ap()), reads=("tA", "tB"), writes=("wfb",))
            p.step("vector", lambda e, bA=bA, Fi=Fi: e.tensor_mul(tA.ap(), bA[:, 0:TC], Fi), reads=(bAk, "fch"), writes=("tA",))
            p.step("vector", lambda e, bB=bB, Fr=Fr: e.tensor_mul(tB.ap(), bB[:, 0:TC], Fr), reads=(bBk, "fch"), writes=("tB",))
            p.step("vector", lambda e, gp=gp: e.tensor_add(wfb[:, 8 + gp, :], tA.ap(), tB.ap()), reads=("tA", "tB"), writes=("wfb",))
        for fb in range(KB):
            bank, bk = ctx.pb[2 + fb % 2], f"pb{2 + fb % 2}"
            ctx.mm_group(bank[:, 0:TC], bk,
                         [(minvb[:, gb, fb * 128:(fb + 1) * 128], wfb[:, gb, :], ("minvb", "wfb")) for gb in range(16)])
            p.step("scalar", lambda e, fb=fb, bank=bank: e.copy(out0b[:, fb, :], bank[:, 0:TC]), reads=(bk,), writes=("out0b",))
        for ob in range(KB):
            bank, bk = ctx.pb[ob % 2], f"pb{ob % 2}"
            ctx.mm_group(bank[:, 0:TC], bk,
                         [(apb16[:, fb, ob * 128:(ob + 1) * 128], out0b[:, fb, :], ("apb16", "out0b")) for fb in range(KB)])
            p.step("vector", lambda e, ob=ob, bank=bank: e.scalar_tensor_tensor(xich[:, ob, :], bank[:, 0:TC], apb[:, ob:ob + 1], xich[:, ob, :], ALU.add, ALU.add),
                   reads=(bk, "apb", "xich"), writes=("xich",))
        ctx.dma("sync", xo_r[:, :, t0:t0 + TC], xich.ap(), reads=("xich",), writes=(f"xo{c}",))

    if standalone:
        p.step("gpsimd", lambda e: e.nop(), reads=tuple(f"xo{c}" for c in range(nch)), tag="final_wait")
        p.emit()
    return ctx


def build_l0a_ffn(Ttot=S, ctx=None, io=None, pfx=""):
    standalone = ctx is None
    if standalone:
        ctx = Ctx("l0affnX")
        io = dict(xi_fm=ctx.din("xi_fm", [D, S]), fc_w=ctx.din("fc_w", [D, 4096]),
                  fcbf=ctx.din("fcbf", [128, 32]), ln2wf=ctx.din("ln2wf", [128, KB]),
                  ln2bf=ctx.din("ln2bf", [128, KB]), mproj_w=ctx.din("mproj_w", [4096, D]),
                  mprojbf=ctx.din("mprojbf", [128, KB]), xo_fm=ctx.dout("xo_fm", [D, S]))
    else:
        ctx.phase("M" + pfx)
    nc, p = ctx.nc, ctx.p
    TC = 256
    nch = Ttot // TC
    H = 4096
    HB = 32

    xi = io["xi_fm"]
    fcw = io["fc_w"]
    fcbf = io["fcbf"]
    lnwf = io["ln2wf"]
    lnbf = io["ln2bf"]
    mpw = io["mproj_w"]
    mpbf = io["mprojbf"]
    xo = io["xo_fm"]

    stage = ctx.sb("stage", [128, 1024], F32)
    fcb16 = ctx.sb("fcb16", [128, KB, H], BF16)
    mpb16 = ctx.sb("mpb16", [128, HB, D], BF16)
    lnw = ctx.sb("lnw", [128, KB], F32)
    lnb = ctx.sb("lnb", [128, KB], F32)
    lnbb = ctx.sb("lnbb", [128, KB], BF16)
    fcb = ctx.sb("fcb", [128, HB], F32)
    beff = ctx.sb("beff", [128, HB], F32)
    mpb = ctx.sb("mpb", [128, KB], F32)
    ctx.dma("sync", lnw[:, :], lnwf, writes=("lnwb",))
    ctx.dma("sync", lnb[:, :], lnbf, writes=("lnbq",))
    ctx.dma("sync", fcb[:, :], fcbf, writes=("biasf",))
    ctx.dma("sync", mpb[:, :], mpbf, writes=("mpb",))
    p.step("vector", lambda e: e.tensor_copy(lnbb[:, :], lnb[:, :]), reads=("lnbq",), writes=("lnbb",))
    _load_scaled_weight(ctx, "fcw", fcw, D, H, stage, 1024, fcb16, "fcb16",
                        lnwb=lnw, lnbb=lnbb, biasf=fcb, beff=beff, beffk="beff",
                        bank=ctx.pb[2], bankk="pb2")
    _load_scaled_weight(ctx, "mpw", mpw, H, D, stage, 1024, mpb16, "mpb16")

    xch = ctx.sb("xch", [128, KB, TC], F32)
    xb = ctx.sb("xb", [128, KB, TC], BF16)
    sq = ctx.sb("sq", [128, KB, TC], BF16)
    t1 = ctx.sb("t1", [128, KB, TC], F32)
    xnb = ctx.sb("xnb", [128, KB, TC], BF16)
    hb = ctx.sb("hb", [128, HB, TC], BF16)
    rows = ctx.sb("rows", [1, 4 * TC], F32)

    xi_r = xi.rearrange("(a p) t -> p a t", p=128)
    xo_r = xo.rearrange("(a p) t -> p a t", p=128)

    for c in range(nch):
        t0 = c * TC
        ctx.dma("sync", xch.ap(), xi_r[:, :, t0:t0 + TC], writes=("xch",))
        ctx.ln_fm(TC, KB, xch.ap(), "xch", xb.ap(), "xb", sq.ap(), "sq",
                  t1.ap(), "t1", xnb.ap(), "xnb", rows, "rows",
                  ctx.pb[6], "pb6", ctx.pb[7], "pb7")
        for mb in range(HB):
            bank, bk = ctx.pb[mb % 2], f"pb{mb % 2}"
            ctx.mm_group(bank[:, 0:TC], bk,
                         [(fcb16[:, kc, mb * 128:(mb + 1) * 128], xnb[:, kc, :], ("fcb16", "xnb")) for kc in range(KB)])
            p.step("scalar", lambda e, mb=mb, bank=bank: e.activation(hb[:, mb, :], bank[:, 0:TC], AF.Gelu, bias=beff[:, mb:mb + 1]),
                   reads=(bk, "beff"), writes=("hb",))
        for ob in range(KB):
            bank, bk = ctx.pb[2 + ob % 2], f"pb{2 + ob % 2}"
            ctx.mm_group(bank[:, 0:TC], bk,
                         [(mpb16[:, hc, ob * 128:(ob + 1) * 128], hb[:, hc, :], ("mpb16", "hb")) for hc in range(HB)])
            p.step("vector", lambda e, ob=ob, bank=bank: e.scalar_tensor_tensor(xch[:, ob, :], bank[:, 0:TC], mpb[:, ob:ob + 1], xch[:, ob, :], ALU.add, ALU.add),
                   reads=(bk, "mpb", "xch"), writes=("xch",))
        ctx.dma("sync", xo_r[:, :, t0:t0 + TC], xch.ap(), reads=("xch",), writes=(f"xo{c}",))

    if standalone:
        p.step("gpsimd", lambda e: e.nop(), reads=tuple(f"xo{c}" for c in range(nch)), tag="final_wait")
        p.emit()
    return ctx


H16, HD64 = 16, 64


def l0_consts():
    h = np.arange(H16)
    hd = np.arange(HD64)
    ph = np.exp(-2j * np.pi * np.outer(h, np.arange(H16)) / H16)
    pd = np.exp(-2j * np.pi * np.outer(hd, np.arange(HD64)) / HD64)
    M2D = (ph[:, None, :, None] * pd[None, :, None, :]).reshape(D, D)
    Minv = np.conj(M2D).T / D
    mfwd = np.concatenate([M2D.real, M2D.imag], axis=1).astype(bfloat16)
    minv_stack = np.concatenate([Minv.real, -Minv.imag], axis=0).astype(bfloat16)
    ks = np.fft.fftfreq(S).astype(np.float32)[:, None, None]
    khf = np.fft.fftfreq(H16).astype(np.float32)[None, :, None]
    kdf = np.fft.fftfreq(HD64).astype(np.float32)[None, None, :]
    km = np.sqrt(ks**2 + khf**2 + kdf**2 + 1e-10)
    F = np.exp(1j * np.arctan(np.log(km + 1e-10)))
    lam = (np.arange(S) / S).astype(np.float64)
    t = np.arange(S).astype(np.float64) * 0.1
    mod = np.sin(t + lam) * np.cos(t - 2.0 * lam + 0.5 * lam**2)
    msc = 1.0 + 0.1 * mod
    Fsc = F.reshape(S, D) * msc[:, None]
    ffm = np.ascontiguousarray(np.concatenate([Fsc.real.T, Fsc.imag.T], axis=0).astype(np.float32))
    return mfwd, minv_stack, ffm


def perm_qkv(attn_w, attn_b):
    w = attn_w.reshape(D, H16, 3, HD64)
    wq = w[:, :, 0, :].reshape(D, D)
    wk = w[:, :, 1, :].reshape(D, D)
    wv = w[:, :, 2, :].reshape(D, D)
    wp = np.concatenate([wq, wk, wv], axis=1)
    b = attn_b.reshape(H16, 3, HD64)
    bp = np.concatenate([b[:, 0, :].ravel(), b[:, 1, :].ravel(), b[:, 2, :].ravel()])
    return np.ascontiguousarray(wp), bp


def build_l1_attn(Ttot=S, debug=False, part="all", crange=None, ctx=None, io=None):
    standalone = ctx is None
    if not standalone:
        assert part == "all" and not debug
        ctx.phase("A1")
    else:
        ctx = Ctx("l1attnX")
    nc, p = ctx.nc, ctx.p
    TC = 128
    TCDBG = 128
    DBGC = 1
    nch = Ttot // TC
    NH, HD = 16, 64

    if standalone:
        io = dict(xin_fm=ctx.din("xin_fm", [D, Ttot]),
                  wq=ctx.din("wq", [D, D]), wk=ctx.din("wk", [D, D]),
                  wv=ctx.din("wv", [D, D]), wo=ctx.din("wo", [D, D]),
                  aq=ctx.din("aq", [D, 64]), bq=ctx.din("bq", [64, D]),
                  av=ctx.din("av", [D, 64]), bv=ctx.din("bv", [64, D]),
                  ao=ctx.din("ao", [D, 64]), bo=ctx.din("bo", [64, D]),
                  costab=ctx.din("costab", [Ttot, HD]), sintab=ctx.din("sintab", [Ttot, HD]),
                  cdmah=ctx.din("cdmah", [128, HD]), cdma1=ctx.din("cdma1", [128, HD]),
                  cdma2=ctx.din("cdma2", [128, HD]), meqt=ctx.din("meqt", [128, 32]),
                  mkeept=ctx.din("mkeept", [128, 32]), mfutnt=ctx.din("mfutnt", [128, 32]),
                  t4t=ctx.din("t4t", [128, 128], BF16), s4t=ctx.din("s4t", [128, 32], BF16),
                  e4t=ctx.din("e4t", [128, 4, 128], BF16),
                  ident_bf=ctx.din("ident_bf", [128, 128], BF16))
    xin = io["xin_fm"]
    wq = io["wq"]; wk = io["wk"]; wv = io["wv"]; wo = io["wo"]
    aq = io["aq"]; bq = io["bq"]; av = io["av"]; bv = io["bv"]
    ao = io["ao"]; bo = io["bo"]
    costab = io["costab"]; sintab = io["sintab"]
    cdmah = io["cdmah"]; cdma1 = io["cdma1"]; cdma2 = io["cdma2"]
    meqt = io["meqt"]; mkeept = io["mkeept"]; mfutnt = io["mfutnt"]
    t4t = io["t4t"]; s4t = io["s4t"]; e4t = io["e4t"]
    identb = io["ident_bf"]
    nch0 = Ttot // 128
    if part == "kv":
        fks_out = ctx.dout("fks_hbm", [128, KB * D], BF16)
        fvs_out = ctx.dout("fvs_hbm", [128, KB * D], BF16)
        fkfm_out = ctx.dout("fkfm_hbm", [128, KB * D], BF16)
        xout = None
    else:
        xout = io["xout_fm"] if not standalone else ctx.dout("xout_fm", [D, Ttot])
        if part != "all":
            fks_in = ctx.din("fks_hbm", [128, KB * D], BF16)
            fvs_in = ctx.din("fvs_hbm", [128, KB * D], BF16)
            fkfm_in = ctx.din("fkfm_hbm", [128, KB * D], BF16)
    dbg = {}
    if debug:
        for nm, shp, dt_ in [("d_kf", [128, D], F32), ("d_ksh", [128, D], BF16),
                             ("d_fKs", [128, KB * D], BF16), ("d_fKfm", [128, KB * D], BF16),
                             ("d_aqf", [128, D], F32), ("d_cs", [128, NH], F32),
                             ("d_wexp", [128, 1024], BF16), ("d_ssum", [128, NH], F32),
                             ("d_outk", [128, D], F32), ("d_t1", [128, KB * TCDBG], F32),
                             ("d_outk0", [128, D], F32), ("d_fve", [128, D], F32),
                             ("d_fVs", [128, KB * D], BF16),
                             ("d_cpv", [128, D], F32), ("d_wT", [128, KB * TCDBG], BF16),
                             ("d_vf", [128, D], F32), ("d_xnb", [128, KB * TCDBG], BF16)]:
            dbg[nm] = ctx.dout(nm, shp, dt_)

    stage = ctx.sb("stage", [128, 1024], F32)
    wqb = ctx.sb("wqb", [128, KB, D], BF16)
    wkb = ctx.sb("wkb", [128, KB, D], BF16)
    wvb = ctx.sb("wvb", [128, KB, D], BF16)
    wob = ctx.sb("wob", [128, KB, D], BF16)
    aqb_w = ctx.sb("aqb_w", [128, KB, 64], BF16)
    avb_w = ctx.sb("avb_w", [128, KB, 64], BF16)
    aob_w = ctx.sb("aob_w", [128, KB, 64], BF16)
    bqb_w = ctx.sb("bqb_w", [64, D], BF16)
    bvb_w = ctx.sb("bvb_w", [64, D], BF16)
    bob_w = ctx.sb("bob_w", [64, D], BF16)
    for nm, hbmw, sb_ in [("wq", wq, wqb), ("wk", wk, wkb), ("wv", wv, wvb), ("wo", wo, wob)]:
        _load_scaled_weight(ctx, nm, hbmw, D, D, stage, 1024, sb_, nm + "b")
    for nm, hbmw, sb_ in [("aq", aq, aqb_w), ("av", av, avb_w), ("ao", ao, aob_w)]:
        for kc in range(KB):
            ctx.dma("sync", stage[:, 0:64], hbmw[kc * 128:(kc + 1) * 128, :], writes=("stage",))
            p.step("gpsimd", lambda e, kc=kc, sb_=sb_: e.tensor_copy(sb_[:, kc, :], stage[:, 0:64]), reads=("stage",), writes=(nm + "b",))
    for nm, hbmw, sb_ in [("bq", bq, bqb_w), ("bv", bv, bvb_w), ("bo", bo, bob_w)]:
        ctx.dma("sync", stage[0:64, 0:D], hbmw, writes=("stage",))
        p.step("gpsimd", lambda e, sb_=sb_: e.tensor_copy(sb_[:, :], stage[0:64, 0:D]), reads=("stage",), writes=(nm + "b",))

    cdh = ctx.sb("cdh", [128, HD], F32); ctx.dma("sync", cdh[:, :], cdmah, writes=("cdh",))
    cd1 = ctx.sb("cd1", [128, HD], F32); ctx.dma("sync", cd1[:, :], cdma1, writes=("cd1",))
    cd2 = ctx.sb("cd2", [128, HD], F32); ctx.dma("sync", cd2[:, :], cdma2, writes=("cd2",))
    meq = ctx.sb("meq", [128, 32], F32); ctx.dma("sync", meq[:, :], meqt, writes=("meq",))
    mkp = ctx.sb("mkp", [128, 32], F32); ctx.dma("sync", mkp[:, :], mkeept, writes=("mkp",))
    mfn = ctx.sb("mfn", [128, 32], F32); ctx.dma("sync", mfn[:, :], mfutnt, writes=("mfn",))
    t4 = ctx.sb("t4", [128, 128], BF16); ctx.dma("sync", t4[:, :], t4t, writes=("t4",))
    s4 = ctx.sb("s4", [128, 32], BF16); ctx.dma("sync", s4[:, :], s4t, writes=("s4",))
    e4 = ctx.sb("e4", [128, 4, 128], BF16); ctx.dma("sync", e4.ap(), e4t, writes=("e4",))
    idb = ctx.sb("idb", [128, 128], BF16); ctx.dma("sync", idb[:, :], identb, writes=("idb",))

    fKs = ctx.sb("fKs", [128, KB, D], BF16)
    fVs = ctx.sb("fVs", [128, KB, D], BF16)
    fKfm = ctx.sb("fKfm", [128, KB, D], BF16)
    if part == "kv":
        p.step("vector", lambda e: e.memset(fKs.ap(), 0.0), writes=("fKs",))
        p.step("vector", lambda e: e.memset(fVs.ap(), 0.0), writes=("fVs",))
        p.step("vector", lambda e: e.memset(fKfm.ap(), 0.0), writes=("fKfm",))

    xch = ctx.sb("xch", [128, KB, TC], F32)
    xb = ctx.sb("xb", [128, KB, TC], BF16)
    sq = ctx.sb("sq", [128, KB, TC], BF16)
    t1 = ctx.sb("t1", [128, KB, TC], F32)
    xnb = ctx.sb("xnb", [128, KB, TC], BF16)
    rows = ctx.sb("rows", [1, 4 * TC], F32)
    kf = ctx.sb("kf", [128, D], F32)
    vf = ctx.sb("vf", [128, D], F32)
    qf = ctx.sb("qf", [128, D], F32)
    s1 = ctx.sb("s1", [128, D], F32)
    s2 = ctx.sb("s2", [128, D], F32)
    s3 = ctx.sb("s3", [128, D], F32)
    coss = ctx.sb("coss", [128, HD], F32)
    sins = ctx.sb("sins", [128, HD], F32)
    kshb = ctx.sb("kshb", [128, D], BF16)
    vshb = ctx.sb("vshb", [128, D], BF16)
    aqf = ctx.sb("aqf", [128, D], F32)
    aqb16 = ctx.sb("aqb16", [128, D], BF16)
    aqfm = ctx.sb("aqfm", [128, KB, TC], BF16)
    avsml = ctx.sb("avsml", [64, TC], BF16)
    wexp = ctx.sb("wexp", [128, 1024], BF16)
    sfix = ctx.sb("sfix", [128, 32], F32)
    cs_t = ctx.sb("cs_t", [128, NH], F32)
    ecs = ctx.sb("ecs", [128, NH], F32)
    ssum = ctx.sb("ssum", [128, NH], F32)
    rssum = ctx.sb("rssum", [128, NH], F32)
    wTb = ctx.sb("wTb", [128, KB, TC], BF16)
    fve = ctx.sb("fve", [128, D], F32)
    outk = ctx.sb("outk", [128, D], F32)
    outb16 = ctx.sb("outb16", [128, D], BF16)
    outfm = ctx.sb("outfm", [128, KB, TC], BF16)

    xin_r = xin.rearrange("(a p) t -> p a t", p=128)
    if xout is not None:
        xout_r = xout.rearrange("(a p) t -> p a t", p=128)

    def lnx(c):
        t0 = c * TC
        ctx.dma("sync", xch.ap(), xin_r[:, :, t0:t0 + TC], writes=("xch",))
        ctx.ln_fm(TC, KB, xch.ap(), "xch", xb.ap(), "xb", sq.ap(), "sq",
                  t1.ap(), "t1", xnb.ap(), "xnb", rows, "rows",
                  ctx.pb[0], "pb0", ctx.pb[1], "pb1")

    def proj_tok(dst_f32, dstk, wbt, wbtk, lora=None):
        for half in range(2):
            bank, bk = ctx.pb[half], f"pb{half}"
            pairs = [(xnb[:, kc, :], wbt[:, kc, half * 512:(half + 1) * 512], ("xnb", wbtk)) for kc in range(KB)]
            if lora is not None:
                ab16, abk, bb, bbk = lora
                pairs.append((ab16.ap(), bb[:, half * 512:(half + 1) * 512], (abk, bbk)))
            ctx.mm_group(bank[:, :], bk, pairs)
            p.step("scalar", lambda e, bank=bank, half=half: e.copy(dst_f32[:, half * 512:(half + 1) * 512], bank[:, :]),
                   reads=(f"pb{half}",), writes=(dstk,))

    def lora_a(awt, awtk, dstb, dstbk):
        bank, bk = ctx.pb[6], "pb6"
        ctx.mm_group(bank[0:64, 0:TC], bk,
                     [(awt[:, kc, :], xnb[:, kc, :], (awtk, "xnb")) for kc in range(KB)])
        p.step("scalar", lambda e: e.copy(dstb.ap(), bank[0:64, 0:TC]), reads=(bk,), writes=(dstbk,))

    def rope(dst, dstk, src, srck):
        s13 = s1.ap().rearrange("p (h d) -> p h d", d=HD)
        src3 = src.ap().rearrange("p (h d) -> p h d", d=HD)
        dst3 = dst.ap().rearrange("p (h d) -> p h d", d=HD)
        cosb = coss.ap().unsqueeze(1).broadcast_to([128, NH, HD])
        sinb = sins.ap().unsqueeze(1).broadcast_to([128, NH, HD])
        p.step("vector", lambda e: e.tensor_mul(s13[:, :, 0:32], src3[:, :, 32:64], sinb[:, :, 0:32]), reads=(srck, "sins"), writes=("s1",))
        p.step("vector", lambda e: e.tensor_mul(s13[:, :, 32:64], src3[:, :, 0:32], sinb[:, :, 32:64]), reads=(srck, "sins"), writes=("s1",))
        p.step("vector", lambda e: e.tensor_mul(s2.ap(), src.ap(), cosb.rearrange("p h d -> p (h d)") if False else cosb), reads=(srck, "coss"), writes=("s2",))
        p.step("vector", lambda e: e.tensor_sub(dst3[:, :, 0:32], s2.ap().rearrange("p (h d) -> p h d", d=HD)[:, :, 0:32], s13[:, :, 0:32]), reads=("s2", "s1"), writes=(dstk,))
        p.step("vector", lambda e: e.tensor_add(dst3[:, :, 32:64], s2.ap().rearrange("p (h d) -> p h d", d=HD)[:, :, 32:64], s13[:, :, 32:64]), reads=("s2", "s1"), writes=(dstk,))

    for c in (range(nch) if part in ("all", "kv") else ()):
        t0 = c * TC
        lnx(c)
        ctx.dma("sync", coss[:, :], costab[t0:t0 + TC, :], writes=("coss",))
        ctx.dma("sync", sins[:, :], sintab[t0:t0 + TC, :], writes=("sins",))
        lora_a(avb_w, "avb", avsml, "avsml")
        proj_tok(kf, "kf", wkb, "wkb")
        proj_tok(vf, "vf", wvb, "wvb", lora=(avsml, "avsml", bvb_w, "bvb"))
        rope(s3, "s3", kf, "kf")
        cdhb = cdh.ap().unsqueeze(1).broadcast_to([128, NH, HD])
        p.step("vector", lambda e: e.tensor_mul(kshb.ap().rearrange("p (h d) -> p h d", d=HD), s3.ap().rearrange("p (h d) -> p h d", d=HD), cdhb), reads=("s3", "cdh"), writes=("kshb",))
        p.step("vector", lambda e: e.tensor_mul(vshb.ap().rearrange("p (h d) -> p h d", d=HD), vf.ap().rearrange("p (h d) -> p h d", d=HD), cdhb), reads=("vf", "cdh"), writes=("vshb",))
        j = c % 4
        for half in range(2):
            kb_, kbk = ctx.pb[2 + half], f"pb{2 + half}"
            vb_, vbk = ctx.pb[0 + half] if False else ctx.pb[4 + half], f"pb{4 + half}"
            p.step("tensor", lambda e, kb_=kb_, half=half, j=j: e.matmul(kb_[32 * j:32 * (j + 1), :], s4[:, :], kshb[:, half * 512:(half + 1) * 512], start=True, stop=True, tile_position=(0, 32 * j)),
                   reads=("s4", "kshb"), writes=(kbk,))
            p.step("tensor", lambda e, vb_=vb_, half=half, j=j: e.matmul(vb_[32 * j:32 * (j + 1), :], s4[:, :], vshb[:, half * 512:(half + 1) * 512], start=True, stop=True, tile_position=(0, 32 * j)),
                   reads=("s4", "vshb"), writes=(vbk,))
        if debug and c == DBGC:
            ctx.dma("sync", dbg["d_kf"], kf.ap(), reads=("kf",), writes=("dkf",))
            ctx.dma("sync", dbg["d_ksh"], kshb.ap(), reads=("kshb",), writes=("dksh",))
            ctx.dma("sync", dbg["d_vf"], vf.ap(), reads=("vf",), writes=("dvf",))
        if j == 3:
            blk = c // 4
            for half in range(2):
                p.step("vector", lambda e, blk=blk, half=half: e.tensor_copy(fKs[:, blk, half * 512:(half + 1) * 512], ctx.pb[2 + half][:, :]), reads=(f"pb{2 + half}",), writes=("fKs",))
                p.step("vector", lambda e, blk=blk, half=half: e.tensor_copy(fVs[:, blk, half * 512:(half + 1) * 512], ctx.pb[4 + half][:, :]), reads=(f"pb{4 + half}",), writes=("fVs",))
    for sb_ in (range(KB) if part in ("all", "kv") else ()):
        for fb in range(KB):
            p.step("tensor", lambda e, sb_=sb_, fb=fb: e.transpose(ctx.pb[2][:, :].bitcast(BF16)[:, 0:128], fKs[:, sb_, fb * 128:(fb + 1) * 128], idb[:, :]),
                   reads=("fKs", "idb"), writes=("pb2",))
            p.step("vector", lambda e, sb_=sb_, fb=fb: e.tensor_copy(fKfm[:, fb, sb_ * 128:(sb_ + 1) * 128], ctx.pb[2][:, :].bitcast(BF16)[:, 0:128]),
                   reads=("pb2",), writes=("fKfm",))

    if debug:
        ctx.dma("sync", dbg["d_fKs"], fKs.ap().rearrange("p a t -> p (a t)"), reads=("fKs",), writes=("dfks",))
        ctx.dma("sync", dbg["d_fKfm"], fKfm.ap().rearrange("p a t -> p (a t)"), reads=("fKfm",), writes=("dfkfm",))
        ctx.dma("sync", dbg["d_fVs"], fVs.ap().rearrange("p a t -> p (a t)"), reads=("fVs",), writes=("dfvs",))

    if part == "kv":
        ctx.dma("sync", fks_out, fKs.ap().rearrange("p a t -> p (a t)"), reads=("fKs",), writes=("fko",))
        ctx.dma("sync", fvs_out, fVs.ap().rearrange("p a t -> p (a t)"), reads=("fVs",), writes=("fvo",))
        ctx.dma("sync", fkfm_out, fKfm.ap().rearrange("p a t -> p (a t)"), reads=("fKfm",), writes=("fkfo",))
        p.step("gpsimd", lambda e: e.nop(), reads=("fko", "fvo", "fkfo"), tag="final_wait")
        p.emit()
        return ctx
    if part != "all":
        ctx.dma("sync", fKs.ap().rearrange("p a t -> p (a t)"), fks_in, writes=("fKs",))
        ctx.dma("sync", fVs.ap().rearrange("p a t -> p (a t)"), fvs_in, writes=("fVs",))
        ctx.dma("sync", fKfm.ap().rearrange("p a t -> p (a t)"), fkfm_in, writes=("fKfm",))

    qrange = range(nch) if crange is None else range(*crange)
    for c in qrange:
        t0 = c * TC
        nslots = 32 * (c + 1)
        nsb = (nslots + 127) // 128
        lnx(c)
        ctx.dma("sync", coss[:, :], costab[t0:t0 + TC, :], writes=("coss",))
        ctx.dma("sync", sins[:, :], sintab[t0:t0 + TC, :], writes=("sins",))
        lora_a(aqb_w, "aqb", avsml, "avsml")
        proj_tok(qf, "qf", wqb, "wqb", lora=(avsml, "avsml", bqb_w, "bqb"))
        rope(s3, "s3", qf, "qf")
        cd1b = cd1.ap().unsqueeze(1).broadcast_to([128, NH, HD])
        p.step("vector", lambda e: e.tensor_mul(aqf.ap().rearrange("p (h d) -> p h d", d=HD), s3.ap().rearrange("p (h d) -> p h d", d=HD), cd1b), reads=("s3", "cd1"), writes=("aqf",))
        p.step("vector", lambda e: e.tensor_copy(aqb16.ap(), aqf.ap()), reads=("aqf",), writes=("aqb16",))
        for fb in range(KB):
            p.step("tensor", lambda e, fb=fb: e.transpose(ctx.pb[2][:, :].bitcast(BF16)[:, 0:128], aqb16[:, fb * 128:(fb + 1) * 128], idb[:, :]),
                   reads=("aqb16", "idb"), writes=("pb2",))
            p.step("vector", lambda e, fb=fb: e.tensor_copy(aqfm[:, fb, :], ctx.pb[2][:, :].bitcast(BF16)[:, 0:128]),
                   reads=("pb2",), writes=("aqfm",))
        lora_a(avb_w, "avb", avsml, "avsml")
        proj_tok(kf, "kf", wkb, "wkb")
        proj_tok(vf, "vf", wvb, "wvb", lora=(avsml, "avsml", bvb_w, "bvb"))
        rope(s3, "s3", kf, "kf")
        cdhb = cdh.ap().unsqueeze(1).broadcast_to([128, NH, HD])
        p.step("vector", lambda e: e.tensor_mul(kshb.ap().rearrange("p (h d) -> p h d", d=HD), s3.ap().rearrange("p (h d) -> p h d", d=HD), cdhb), reads=("s3", "cdh"), writes=("kshb",))
        p.step("vector", lambda e: e.tensor_mul(vshb.ap().rearrange("p (h d) -> p h d", d=HD), vf.ap().rearrange("p (h d) -> p h d", d=HD), cdhb), reads=("vf", "cdh"), writes=("vshb",))
        for half in range(2):
            p.step("tensor", lambda e, half=half: e.matmul(ctx.pb[4 + half][:, :], t4[:, :], kshb[:, half * 512:(half + 1) * 512], start=True, stop=True), reads=("t4", "kshb"), writes=(f"pb{4 + half}",))
            p.step("tensor", lambda e, half=half: e.matmul(ctx.pb[6 + half][:, :], t4[:, :], vshb[:, half * 512:(half + 1) * 512], start=True, stop=True), reads=("t4", "vshb"), writes=(f"pb{6 + half}",))
            p.step("tensor", lambda e, half=half, c=c: e.matmul(ctx.pb[2][:, :], e4[:, c % 4, :], fVs[:, c // 4, half * 512:(half + 1) * 512], start=True, stop=True),
                   reads=("e4", "fVs"), writes=("pb2",))
            p.step("scalar", lambda e, half=half: e.copy(fve[:, half * 512:(half + 1) * 512], ctx.pb[2][:, :]), reads=("pb2",), writes=("fve",))
        cpk3 = None
        p.step("vector", lambda e: e.tensor_mul(s1.ap()[:, 0:512], aqf[:, 0:512], ctx.pb[4][:, :]), reads=("aqf", "pb4"), writes=("s1",))
        p.step("vector", lambda e: e.tensor_mul(s1.ap()[:, 512:1024], aqf[:, 512:1024], ctx.pb[5][:, :]), reads=("aqf", "pb5"), writes=("s1",))
        p.step("vector", lambda e: e.tensor_reduce(cs_t.ap(), s1.ap().rearrange("p (h d) -> p h d", d=HD), mybir.AxisListType.X, ALU.add), reads=("s1",), writes=("cs_t",))
        p.step("scalar", lambda e: e.activation(ecs.ap(), cs_t.ap(), AF.Exp), reads=("cs_t",), writes=("ecs",))
        for h in range(NH):
            hb_, ho = h // 2, (h % 2) * 64
            aqs = aqfm[ho:ho + 64, hb_, :]
            for piece in range((nslots + 511) // 512):
                p0 = piece * 512
                pw = min(512, nslots - p0)
                bank, bk = ctx.pb[piece], f"pb{piece}"
                ctx.mm_group(bank[:, 0:pw], bk,
                             [(aqs, fKfm[ho:ho + 64, hb_, p0:p0 + pw], ("aqfm", "fKfm"))])
            lb, lo = (32 * c) // 512, (32 * c) % 512
            lbank, lbk = ctx.pb[lb], f"pb{lb}"
            p.step("vector", lambda e, lbank=lbank, lo=lo: e.tensor_mul(sfix.ap(), lbank[:, lo:lo + 32], mkp[:, :]), reads=(lbk, "mkp"), writes=("sfix",))
            p.step("vector", lambda e, h=h: e.scalar_tensor_tensor(s2.ap()[:, 0:32], meq[:, :], cs_t[:, h:h + 1], sfix.ap(), ALU.mult, ALU.add), reads=("meq", "cs_t", "sfix"), writes=("s2",))
            p.step("vector", lambda e: e.tensor_add(sfix.ap(), s2.ap()[:, 0:32], mfn[:, :]), reads=("s2", "mfn"), writes=("sfix",))
            for piece in range((nslots + 511) // 512):
                p0 = piece * 512
                pw = min(512, nslots - p0)
                bank, bk = ctx.pb[piece], f"pb{piece}"
                if p0 + pw > 32 * c:
                    pw2 = 32 * c - p0
                    if pw2 > 0:
                        p.step("scalar", lambda e, bank=bank, p0=p0, pw2=pw2: e.activation(wexp[:, p0:p0 + pw2], bank[:, 0:pw2], AF.Exp), reads=(bk,), writes=("wexp",))
                else:
                    p.step("scalar", lambda e, bank=bank, p0=p0, pw=pw: e.activation(wexp[:, p0:p0 + pw], bank[:, 0:pw], AF.Exp), reads=(bk,), writes=("wexp",))
            p.step("scalar", lambda e, c=c: e.activation(wexp[:, 32 * c:32 * c + 32], sfix.ap(), AF.Exp), reads=("sfix",), writes=("wexp",))
            p.step("vector", lambda e, h=h, ns=nslots: e.tensor_reduce(ssum[:, h:h + 1], wexp.ap()[:, 0:ns].unsqueeze(1), mybir.AxisListType.X, ALU.add), reads=("wexp",), writes=("ssum",))
            for sb_ in range(nsb):
                swidth = min(128, nslots - sb_ * 128)
                p.step("tensor", lambda e, sb_=sb_, sw=swidth: e.transpose(ctx.pb[2][:, :].bitcast(BF16)[0:sw, 0:128], wexp[:, sb_ * 128:sb_ * 128 + sw], idb[:, :]),
                       reads=("wexp", "idb"), writes=("pb2",))
                p.step("vector", lambda e, sb_=sb_, sw=swidth: e.tensor_copy(wTb[0:sw, sb_, :], ctx.pb[2][:, :].bitcast(BF16)[0:sw, 0:128]),
                       reads=("pb2",), writes=("wTb",))
            bank3, bk3 = ctx.pb[3], "pb3"
            ctx.mm_group(bank3[:, h * 64 - (h // 8) * 512:h * 64 - (h // 8) * 512 + 64] if False else bank3[:, (h % 8) * 64:(h % 8) * 64 + 64], bk3,
                         [(wTb[0:min(128, nslots - sb_ * 128), sb_, :], fVs[0:min(128, nslots - sb_ * 128), sb_, h * 64:(h + 1) * 64], ("wTb", "fVs")) for sb_ in range(nsb)])
            if h % 8 == 7:
                g = h // 8
                p.step("scalar", lambda e, g=g: e.copy(outk[:, g * 512:(g + 1) * 512], bank3[:, :]), reads=(bk3,), writes=("outk",))
        if debug and c == DBGC:
            ctx.dma("sync", dbg["d_outk0"], outk.ap(), reads=("outk",), writes=("doutk0",))
            ctx.dma("sync", dbg["d_fve"], fve.ap(), reads=("fve",), writes=("dfve",))
            p.step("scalar", lambda e: e.copy(s2.ap()[:, 0:512], ctx.pb[6][:, :]), reads=("pb6",), writes=("s2",))
            p.step("scalar", lambda e: e.copy(s2.ap()[:, 512:1024], ctx.pb[7][:, :]), reads=("pb7",), writes=("s2",))
            ctx.dma("sync", dbg["d_cpv"], s2.ap(), reads=("s2",), writes=("dcpv",))
            ctx.dma("sync", dbg["d_wT"], wTb.ap().rearrange("p a t -> p (a t)"), reads=("wTb",), writes=("dwT",))
            ctx.dma("sync", dbg["d_aqf"], aqf.ap(), reads=("aqf",), writes=("daqf",))
            ctx.dma("sync", dbg["d_cs"], cs_t.ap(), reads=("cs_t",), writes=("dcs",))
            ctx.dma("sync", dbg["d_wexp"], wexp.ap(), reads=("wexp",), writes=("dwexp",))
            ctx.dma("sync", dbg["d_ssum"], ssum.ap(), reads=("ssum",), writes=("dssum",))
            ctx.dma("sync", dbg["d_xnb"], xnb.ap().rearrange("p a t -> p (a t)"), reads=("xnb",), writes=("dxnb",))
        p.step("vector", lambda e: e.tensor_sub(s1.ap()[:, 0:512], ctx.pb[6][:, :], fve[:, 0:512]), reads=("pb6", "fve"), writes=("s1",))
        p.step("vector", lambda e: e.tensor_sub(s1.ap()[:, 512:1024], ctx.pb[7][:, :], fve[:, 512:1024]), reads=("pb7", "fve"), writes=("s1",))
        ecs3 = ecs.ap().unsqueeze(2).broadcast_to([128, NH, HD])
        p.step("vector", lambda e: e.tensor_mul(s1.ap().rearrange("p (h d) -> p h d", d=HD), s1.ap().rearrange("p (h d) -> p h d", d=HD), ecs3), reads=("s1", "ecs"), writes=("s1",))
        p.step("vector", lambda e: e.tensor_add(outk.ap(), outk.ap(), s1.ap()), reads=("outk", "s1"), writes=("outk",))
        p.step("vector", lambda e: e.reciprocal(rssum.ap(), ssum.ap()), reads=("ssum",), writes=("rssum",))
        rs3 = rssum.ap().unsqueeze(2).broadcast_to([128, NH, HD])
        p.step("vector", lambda e: e.tensor_mul(outk.ap().rearrange("p (h d) -> p h d", d=HD), outk.ap().rearrange("p (h d) -> p h d", d=HD), rs3), reads=("outk", "rssum"), writes=("outk",))
        cd2b = cd2.ap().unsqueeze(1).broadcast_to([128, NH, HD])
        p.step("vector", lambda e: e.tensor_mul(outb16.ap().rearrange("p (h d) -> p h d", d=HD), outk.ap().rearrange("p (h d) -> p h d", d=HD), cd2b), reads=("outk", "cd2"), writes=("outb16",))
        for fb in range(KB):
            p.step("tensor", lambda e, fb=fb: e.transpose(ctx.pb[2][:, :].bitcast(BF16)[:, 0:128], outb16[:, fb * 128:(fb + 1) * 128], idb[:, :]),
                   reads=("outb16", "idb"), writes=("pb2",))
            p.step("vector", lambda e, fb=fb: e.tensor_copy(outfm[:, fb, :], ctx.pb[2][:, :].bitcast(BF16)[:, 0:128]),
                   reads=("pb2",), writes=("outfm",))
        bank, bk = ctx.pb[3], "pb3"
        ctx.mm_group(bank[0:64, 0:TC], bk,
                     [(aob_w[:, kc, :], outfm[:, kc, :], ("aob", "outfm")) for kc in range(KB)])
        p.step("scalar", lambda e, bank=bank: e.copy(avsml.ap(), bank[0:64, 0:TC]), reads=(bk,), writes=("avsml",))
        for ob in range(KB):
            bank, bk = ctx.pb[ob % 2], f"pb{ob % 2}"
            pairs = [(wob[:, kc, ob * 128:(ob + 1) * 128], outfm[:, kc, :], ("wob", "outfm")) for kc in range(KB)]
            pairs.append((bob_w[:, ob * 128:(ob + 1) * 128], avsml.ap(), ("bob", "avsml")))
            ctx.mm_group(bank[:, 0:TC], bk, pairs)
            p.step("scalar", lambda e, ob=ob, bank=bank: e.copy(t1[:, ob, :], bank[:, 0:TC]), reads=(bk,), writes=("t1",))
        if debug and c == DBGC:
            ctx.dma("sync", dbg["d_outk"], outk.ap(), reads=("outk",), writes=("doutk",))
            ctx.dma("sync", dbg["d_t1"], t1.ap().rearrange("p a t -> p (a t)"), reads=("t1",), writes=("dt1",))
        ctx.ln_fm(TC, KB, t1.ap(), "t1", xb.ap(), "xb", sq.ap(), "sq",
                  s1.ap().rearrange("p (a t) -> p a t", t=TC), "s1", t1.ap(), "t1", rows, "rows",
                  ctx.pb[4], "pb4", ctx.pb[5], "pb5")
        p.step("vector", lambda e: e.tensor_add(xch.ap(), xch.ap(), t1.ap()), reads=("xch", "t1"), writes=("xch",))
        ctx.dma("sync", xout_r[:, :, t0:t0 + TC], xch.ap(), reads=("xch",), writes=(f"xo{c}",))

    if standalone:
        p.step("gpsimd", lambda e: e.nop(), reads=tuple(f"xo{c}" for c in qrange), tag="final_wait")
        p.emit()
    return ctx


def l1_attn_consts():
    HAD = np.array([[1, 1, 1, 1], [1, -1, 1, -1], [1, 1, -1, -1], [1, -1, -1, 1]], np.float32)
    CDMA = np.tile(HAD, (1, 16))
    rowpat = CDMA[np.arange(128) % 4]
    inv_freq = 1.0 / (10000.0 ** (np.arange(0, 64, 2, dtype=np.float32) / 64))
    pos = np.arange(S, dtype=np.float32)
    freqs = pos[:, None] * inv_freq[None, :]
    emb = np.concatenate([freqs, freqs], axis=1)
    costab = np.cos(emb).astype(np.float32)
    sintab = np.sin(emb).astype(np.float32)
    pidx = np.arange(128)
    slot = pidx // 4
    csl = np.arange(32)
    meq = (csl[None, :] == slot[:, None]).astype(np.float32)
    mfut = (csl[None, :] > slot[:, None]).astype(np.float32)
    mkeep = 1.0 - meq - mfut
    mfutneg = (-1e30 * mfut).astype(np.float32)
    kin = np.arange(128)
    t4 = ((kin[:, None] // 4 == kin[None, :] // 4) & (kin[:, None] <= kin[None, :])).astype(bfloat16)
    s4 = (kin[:, None] // 4 == csl[None, :]).astype(bfloat16)
    e4 = np.zeros((128, 4, 128), np.float32)
    for j in range(4):
        e4[:, j, :] = (kin[:, None] == 32 * j + kin[None, :] // 4)
    e4 = e4.astype(bfloat16)
    return dict(
        costab=costab, sintab=sintab,
        cdmah=np.ascontiguousarray(rowpat / 2.0),
        cdma1=np.ascontiguousarray(rowpat / 8.0),
        cdma2=np.ascontiguousarray(rowpat * 2.0),
        meqt=meq, mkeept=mkeep, mfutnt=mfutneg,
        t4t=t4, s4t=s4, e4t=e4,
        ident_bf=np.eye(128, dtype=bfloat16),
    )


def l1_attn_inmap(x_fm, inp):
    m = dict(
        xin_fm=np.ascontiguousarray(x_fm),
        wq=np.ascontiguousarray(inp["L1_attn_Wq"]), wk=np.ascontiguousarray(inp["L1_attn_Wk"]),
        wv=np.ascontiguousarray(inp["L1_attn_Wv"]), wo=np.ascontiguousarray(inp["L1_attn_Wo"]),
        aq=np.ascontiguousarray(inp["L1_attn_Aq"]), bq=np.ascontiguousarray(inp["L1_attn_Bq"]),
        av=np.ascontiguousarray(inp["L1_attn_Av"]), bv=np.ascontiguousarray(inp["L1_attn_Bv"]),
        ao=np.ascontiguousarray(inp["L1_attn_Ao"]), bo=np.ascontiguousarray(inp["L1_attn_Bo"]),
    )
    m.update(l1_attn_consts())
    return m


def build_fused(nphases=99, x2_external=False):
    PH = [0]
    def tick():
        PH[0] += 1
        return PH[0] <= nphases
    ctx = Ctx("mega")
    ctx.enable_arena(base_bytes=20480, top_bytes=229344)
    nc = ctx.nc
    xin = ctx.din("x_fm", [D, S])
    out = ctx.dout("out_fm", [D, S])
    xi = ctx.dscratch("sc_xi", [D, S])
    wsc = ctx.dscratch("sc_w", [D, S], BF16)
    x1 = ctx.dscratch("sc_x1", [D, S])
    x2 = ctx.dscratch("sc_x2", [D, S])
    x3 = ctx.dscratch("sc_x3", [D, S])
    mfwd = ctx.din("mfwd", [D, 2048], BF16)
    minv = ctx.din("minv", [2048, D], BF16)
    ffm = ctx.din("ffm", [2048, S])
    identb = ctx.din("ident_bf", [128, 128], BF16)

    if tick():
        build_ln_pre(ctx=ctx, io=dict(xin_fm=xin, xi_fm=xi))
    for li in range(4):
        pf = f"l{li}"
        ioq = dict(
            xi_fm=xi, w_fm=wsc,
            attn_w=ctx.din(f"{pf}_attn_w", [D, 3072]),
            attn_bf=ctx.din(f"{pf}_attn_bf", [128, 24]),
            ln1wf=ctx.din(f"{pf}_ln1wf", [128, KB]),
            ln1bf=ctx.din(f"{pf}_ln1bf", [128, KB]),
        )
        if tick():
            build_l0a_qkv(ctx=ctx, io=ioq, pfx=pf)
        ios = dict(
            xi_fm=xi, w_fm=wsc, xo_fm=xi, mfwd=mfwd, minv=minv, ffm=ffm,
            aproj_w=ctx.din(f"{pf}_aproj_w", [D, D]),
            aprojbf=ctx.din(f"{pf}_aprojbf", [128, KB]),
        )
        if tick():
            build_l0a_spec(ctx=ctx, io=ios, pfx=pf)
        iof = dict(
            xi_fm=xi, xo_fm=xi,
            fc_w=ctx.din(f"{pf}_fc_w", [D, 4096]),
            fcbf=ctx.din(f"{pf}_fcbf", [128, 32]),
            ln2wf=ctx.din(f"{pf}_ln2wf", [128, KB]),
            ln2bf=ctx.din(f"{pf}_ln2bf", [128, KB]),
            mproj_w=ctx.din(f"{pf}_mproj_w", [4096, D]),
            mprojbf=ctx.din(f"{pf}_mprojbf", [128, KB]),
        )
        if tick():
            build_l0a_ffn(ctx=ctx, io=iof, pfx=pf)
    if tick():
        build_addln(ctx=ctx, io=dict(xa_fm=xin, xb_fm=xi, xout_fm=x1))
    if tick():
        build_l0_ffn(ctx=ctx, io=dict(
            xin_fm=x1, xout_fm=(out if x2_external else x2),
            w1t=ctx.din("f0_w1t", [D, 4096]), w2t=ctx.din("f0_w2t", [4096, D]),
            b1f=ctx.din("f0_b1f", [128, 32]), b2f=ctx.din("f0_b2f", [128, KB]),
            scale1=ctx.din("f0_scale1", [1, 1]), ident_bf=identb))
    if tick():
        build_l1_attn(ctx=ctx, io=dict(
            xin_fm=x2, xout_fm=x3,
            wq=ctx.din("a1_wq", [D, D]), wk=ctx.din("a1_wk", [D, D]),
            wv=ctx.din("a1_wv", [D, D]), wo=ctx.din("a1_wo", [D, D]),
            aq=ctx.din("a1_aq", [D, 64]), bq=ctx.din("a1_bq", [64, D]),
            av=ctx.din("a1_av", [D, 64]), bv=ctx.din("a1_bv", [64, D]),
            ao=ctx.din("a1_ao", [D, 64]), bo=ctx.din("a1_bo", [64, D]),
            costab=ctx.din("a1_costab", [S, 64]), sintab=ctx.din("a1_sintab", [S, 64]),
            cdmah=ctx.din("a1_cdmah", [128, 64]), cdma1=ctx.din("a1_cdma1", [128, 64]),
            cdma2=ctx.din("a1_cdma2", [128, 64]), meqt=ctx.din("a1_meqt", [128, 32]),
            mkeept=ctx.din("a1_mkeept", [128, 32]), mfutnt=ctx.din("a1_mfutnt", [128, 32]),
            t4t=ctx.din("a1_t4t", [128, 128], BF16), s4t=ctx.din("a1_s4t", [128, 32], BF16),
            e4t=ctx.din("a1_e4t", [128, 4, 128], BF16), ident_bf=identb))
    if tick():
        build_l1_ffn(ctx=ctx, io=dict(
            xin_fm=x3, xout_fm=out,
            w1=ctx.din("f1_w1", [D, 4096]), w2=ctx.din("f1_w2", [4096, D]),
            b1f=ctx.din("f1_b1f", [128, 32]), b2f=ctx.din("f1_b2f", [128, KB])))
    ctx.p.barrier()
    ctx.p.emit()
    return ctx


def fused_inmap(inputs, b, l0c, l1c):
    mfwd, minv_stack, ffm = l0c
    m = dict(
        x_fm=np.ascontiguousarray(inputs["L0_attn_x"][b].T),
        mfwd=mfwd, minv=minv_stack, ffm=ffm,
        ident_bf=np.eye(128, dtype=bfloat16),
    )
    for li in range(4):
        pf = f"l{li}"
        awp, abp = perm_qkv(inputs["L0_attn_attn_w"][li], inputs["L0_attn_attn_b"][li])
        m[f"{pf}_attn_w"] = awp
        m[f"{pf}_attn_bf"] = fmvec(abp)
        m[f"{pf}_ln1wf"] = fmvec(inputs["L0_attn_ln1_w"][li])
        m[f"{pf}_ln1bf"] = fmvec(inputs["L0_attn_ln1_b"][li])
        m[f"{pf}_aproj_w"] = np.ascontiguousarray(inputs["L0_attn_aproj_w"][li])
        m[f"{pf}_aprojbf"] = fmvec(inputs["L0_attn_aproj_b"][li])
        m[f"{pf}_fc_w"] = np.ascontiguousarray(inputs["L0_attn_fc_w"][li])
        m[f"{pf}_fcbf"] = fmvec(inputs["L0_attn_fc_b"][li])
        m[f"{pf}_ln2wf"] = fmvec(inputs["L0_attn_ln2_w"][li])
        m[f"{pf}_ln2bf"] = fmvec(inputs["L0_attn_ln2_b"][li])
        m[f"{pf}_mproj_w"] = np.ascontiguousarray(inputs["L0_attn_mproj_w"][li])
        m[f"{pf}_mprojbf"] = fmvec(inputs["L0_attn_mproj_b"][li])
    m["f0_w1t"] = np.ascontiguousarray(inputs["L0_ffn_W1"].T)
    m["f0_w2t"] = np.ascontiguousarray(inputs["L0_ffn_W2"].T)
    m["f0_b1f"] = fmvec(inputs["L0_ffn_b1"])
    m["f0_b2f"] = fmvec(inputs["L0_ffn_b2"])
    m["f0_scale1"] = np.asarray(inputs["L0_ffn_scale"], np.float32).reshape(1, 1)
    for k_, nm in [("wq", "Wq"), ("wk", "Wk"), ("wv", "Wv"), ("wo", "Wo"),
                   ("aq", "Aq"), ("bq", "Bq"), ("av", "Av"), ("bv", "Bv"),
                   ("ao", "Ao"), ("bo", "Bo")]:
        m[f"a1_{k_}"] = np.ascontiguousarray(inputs[f"L1_attn_{nm}"])
    for k_, v in l1c.items():
        if k_ == "ident_bf":
            continue
        m[f"a1_{k_[:-1] if False else k_}"] = v
    m["f1_w1"] = np.ascontiguousarray(inputs["L1_ffn_w1"])
    m["f1_w2"] = np.ascontiguousarray(inputs["L1_ffn_w2"])
    m["f1_b1f"] = fmvec(inputs["L1_ffn_b1"])
    m["f1_b2f"] = fmvec(inputs["L1_ffn_b2"])
    return m


_PROGS = {}


def _progs():
    if not _PROGS:
        _PROGS["pre"] = build_ln_pre()
        _PROGS["qkv"] = build_l0a_qkv()
        _PROGS["spec"] = build_l0a_spec()
        _PROGS["l0affn"] = build_l0a_ffn()
        _PROGS["post"] = build_addln()
        _PROGS["l0ffn"] = build_l0_ffn()
        _PROGS["l1attn"] = build_l1_attn()
        _PROGS["l1ffn"] = build_l1_ffn()
        _PROGS["l0c"] = l0_consts()
        _PROGS["l1c"] = l1_attn_consts()
    return _PROGS


def _run(ctx, in_maps):
    res = run_bass_kernel_spmd(ctx.nc, in_maps, core_ids=list(range(len(in_maps))))
    return res.results


def kernel(**inputs):
    if "fused15" not in _PROGS:
        _PROGS["fused15"] = build_fused(nphases=15, x2_external=True)
        _PROGS["l1attn"] = build_l1_attn()
        _PROGS["l1ffn"] = build_l1_ffn()
        _PROGS["l0c"] = l0_consts()
        _PROGS["l1c"] = l1_attn_consts()
    ctx = _PROGS["fused15"]
    l1c = _PROGS["l1c"]
    B = inputs["L0_attn_x"].shape[0]
    in_maps = []
    for b in range(B):
        m = fused_inmap(inputs, b, _PROGS["l0c"], l1c)
        for k_ in list(m.keys()):
            if k_.startswith("a1_") or k_.startswith("f1_"):
                del m[k_]
        in_maps.append(m)
    res = run_bass_kernel_spmd(ctx.nc, in_maps, core_ids=list(range(B)))
    x2 = [res.results[b]["out_fm"] for b in range(B)]
    l1w = dict(
        wq=np.ascontiguousarray(inputs["L1_attn_Wq"]), wk=np.ascontiguousarray(inputs["L1_attn_Wk"]),
        wv=np.ascontiguousarray(inputs["L1_attn_Wv"]), wo=np.ascontiguousarray(inputs["L1_attn_Wo"]),
        aq=np.ascontiguousarray(inputs["L1_attn_Aq"]), bq=np.ascontiguousarray(inputs["L1_attn_Bq"]),
        av=np.ascontiguousarray(inputs["L1_attn_Av"]), bv=np.ascontiguousarray(inputs["L1_attn_Bv"]),
        ao=np.ascontiguousarray(inputs["L1_attn_Ao"]), bo=np.ascontiguousarray(inputs["L1_attn_Bo"]),
    )
    l1w.update(l1c)
    res2 = run_bass_kernel_spmd(_PROGS["l1attn"].nc, [dict(l1w, xin_fm=x2[b]) for b in range(B)], core_ids=list(range(B)))
    x3 = [res2.results[b]["xout_fm"] for b in range(B)]
    res3 = run_bass_kernel_spmd(
        _PROGS["l1ffn"].nc,
        [l1_ffn_inmap(x3[b], inputs["L1_ffn_w1"], inputs["L1_ffn_b1"],
                      inputs["L1_ffn_w2"], inputs["L1_ffn_b2"]) for b in range(B)],
        core_ids=list(range(B)))
    return np.stack([np.asarray(res3.results[b]["xout_fm"]).T for b in range(B)], axis=0).astype(np.float32)


def _kernel_unfused(**inputs):
    P = _progs()
    mfwd, minv_stack, ffm = P["l0c"]
    l1c = P["l1c"]
    B = inputs["L0_attn_x"].shape[0]
    x = [np.ascontiguousarray(inputs["L0_attn_x"][b].T) for b in range(B)]

    xi = [r["xi_fm"] for r in _run(P["pre"], [{"xin_fm": x[b]} for b in range(B)])]
    for li in range(4):
        awp, abp = perm_qkv(inputs["L0_attn_attn_w"][li], inputs["L0_attn_attn_b"][li])
        lw = {
            "attn_w": awp, "attn_bf": fmvec(abp),
            "ln1wf": fmvec(inputs["L0_attn_ln1_w"][li]),
            "ln1bf": fmvec(inputs["L0_attn_ln1_b"][li]),
        }
        w = [r["w_fm"] for r in _run(P["qkv"], [dict(lw, xi_fm=xi[b]) for b in range(B)])]
        sw = {
            "mfwd": mfwd, "minv": minv_stack, "ffm": ffm,
            "aproj_w": np.ascontiguousarray(inputs["L0_attn_aproj_w"][li]),
            "aprojbf": fmvec(inputs["L0_attn_aproj_b"][li]),
        }
        xi = [r["xo_fm"] for r in _run(P["spec"], [dict(sw, xi_fm=xi[b], w_fm=w[b]) for b in range(B)])]
        fw = {
            "fc_w": np.ascontiguousarray(inputs["L0_attn_fc_w"][li]),
            "fcbf": fmvec(inputs["L0_attn_fc_b"][li]),
            "ln2wf": fmvec(inputs["L0_attn_ln2_w"][li]),
            "ln2bf": fmvec(inputs["L0_attn_ln2_b"][li]),
            "mproj_w": np.ascontiguousarray(inputs["L0_attn_mproj_w"][li]),
            "mprojbf": fmvec(inputs["L0_attn_mproj_b"][li]),
        }
        xi = [r["xo_fm"] for r in _run(P["l0affn"], [dict(fw, xi_fm=xi[b]) for b in range(B)])]
    x1 = [r["xout_fm"] for r in _run(P["post"], [{"xa_fm": x[b], "xb_fm": xi[b]} for b in range(B)])]

    x2 = [r["xout_fm"] for r in _run(
        P["l0ffn"],
        [l0_ffn_inmap(x1[b], inputs["L0_ffn_W1"], inputs["L0_ffn_b1"],
                      inputs["L0_ffn_W2"], inputs["L0_ffn_b2"], inputs["L0_ffn_scale"])
         for b in range(B)])]

    l1w = dict(
        wq=np.ascontiguousarray(inputs["L1_attn_Wq"]), wk=np.ascontiguousarray(inputs["L1_attn_Wk"]),
        wv=np.ascontiguousarray(inputs["L1_attn_Wv"]), wo=np.ascontiguousarray(inputs["L1_attn_Wo"]),
        aq=np.ascontiguousarray(inputs["L1_attn_Aq"]), bq=np.ascontiguousarray(inputs["L1_attn_Bq"]),
        av=np.ascontiguousarray(inputs["L1_attn_Av"]), bv=np.ascontiguousarray(inputs["L1_attn_Bv"]),
        ao=np.ascontiguousarray(inputs["L1_attn_Ao"]), bo=np.ascontiguousarray(inputs["L1_attn_Bo"]),
    )
    l1w.update(l1c)
    x3 = [r["xout_fm"] for r in _run(P["l1attn"], [dict(l1w, xin_fm=x2[b]) for b in range(B)])]

    x4 = [r["xout_fm"] for r in _run(
        P["l1ffn"],
        [l1_ffn_inmap(x3[b], inputs["L1_ffn_w1"], inputs["L1_ffn_b1"],
                      inputs["L1_ffn_w2"], inputs["L1_ffn_b2"]) for b in range(B)])]

    return np.stack([np.asarray(x4[b]).T for b in range(B)], axis=0).astype(np.float32)


# version 03 (final)
# speedup vs baseline: 1.0299x; 1.0234x over previous
import numpy as np
import ml_dtypes
import concourse.bass as bass
import concourse.mybir as mybir
from concourse.bass_utils import run_bass_kernel_spmd

F32 = mybir.dt.float32
BF16 = mybir.dt.bfloat16
I32 = mybir.dt.int32
AF = mybir.ActivationFunctionType
ALU = mybir.AluOpType
bfloat16 = ml_dtypes.bfloat16

NCORES = 8
D = 1024
S = 4096
KB = D // 128
EPS = 1e-5

SAME_ENGINE_EDGES = False
SAME_ENGINE_DRAIN = True


class Prog:

    def __init__(self, nc):
        self.nc = nc
        self.steps = []
        self.last_w = {}
        self.readers = {}

    def step(self, eng, fn, reads=(), writes=(), dma=False, tag=""):
        deps = set()
        for r in reads:
            w = self.last_w.get(r)
            if w is not None:
                deps.add(w)
        for r in writes:
            w = self.last_w.get(r)
            if w is not None:
                deps.add(w)
            deps.update(self.readers.get(r, ()))
        i = len(self.steps)
        self.steps.append(
            dict(i=i, eng=eng, fn=fn, deps=deps, dma=bool(dma), tag=tag, inc=False)
        )
        for r in reads:
            self.readers.setdefault(r, set()).add(i)
        for r in writes:
            self.last_w[r] = i
            self.readers[r] = set()
        return i

    def barrier(self):
        last = {}
        dma_tail = {}
        for st in self.steps:
            key = st["eng"] + (".d" if st["dma"] else "")
            last[key] = st["i"]
            if st["dma"]:
                dma_tail.setdefault(key, []).append(st["i"])
        deps = set(last.values())
        for key, lst in dma_tail.items():
            deps.update(lst[-8:])
        for eng in ("tensor", "vector", "scalar", "gpsimd", "sync"):
            i = self.step(eng, lambda e: e.nop(), writes=(f"__bar{len(self.steps)}__",))
            self.steps[i]["deps"] |= deps

    def emit(self):
        nc = self.nc
        NLANES0 = 8
        for st in self.steps:
            st["stream"] = st["eng"] + (".d" if st["dma"] else "")
        dlane = {}
        for st in self.steps:
            if st["dma"]:
                q = st["stream"]
                dlane[st["i"]] = dlane.setdefault("_n_" + q, 0) % NLANES0
                dlane["_n_" + q] += 1
        for st in self.steps:
            best = {}
            for d in st["deps"]:
                ds = self.steps[d]
                key = ds["stream"]
                if ds["dma"]:
                    key = (ds["stream"], dlane[d])
                if best.get(key, -1) < d:
                    best[key] = d
            st["deps"] = set(best.values())
        for st in self.steps:
            if st["dma"]:
                st["inc"] = True
            for d in st["deps"]:
                ds = self.steps[d]
                if ds["stream"] == st["stream"] and not ds["dma"]:
                    if st["eng"] == "tensor" or not SAME_ENGINE_EDGES:
                        continue
                ds["inc"] = True
        NLANES = 8
        streams = {}
        for st in self.steps:
            streams.setdefault(st["stream"], []).append(st)
        for s, lst in streams.items():
            if s.endswith(".d"):
                lane_rank = [0] * NLANES
                for st in lst:
                    st["lane"] = dlane[st["i"]]
                    lane_rank[st["lane"]] += 1
                    st["rank"] = lane_rank[st["lane"]]
            else:
                r = 0
                for st in lst:
                    if st["inc"]:
                        r += 1
                    st["rank"] = r
        import contextlib

        with contextlib.ExitStack() as ex:
            sems = {}
            for s, lst in streams.items():
                if s.endswith(".d"):
                    sems[s] = (
                        [ex.enter_context(nc.semaphore(f"sem_{s.replace('.', '_')}_{l}"))
                         for l in range(NLANES)],
                        16,
                    )
                elif any(st["inc"] for st in lst):
                    sems[s] = (
                        ex.enter_context(nc.semaphore("sem_" + s.replace(".", "_"))),
                        1,
                    )
            by_eng = {}
            for st in self.steps:
                by_eng.setdefault(st["eng"], []).append(st)
            steps_all = self.steps
            with nc.Block() as block:
                for eng_name in list(by_eng.keys()):
                    lst = by_eng[eng_name]

                    def make_body(lst):
                        def body(eng):
                            watermark = {}
                            last_drain = -1
                            for st in lst:
                                if SAME_ENGINE_DRAIN and st["eng"] != "tensor" and not st["dma"]:
                                    need = any(
                                        steps_all[d]["stream"] == st["stream"]
                                        and not steps_all[d]["dma"]
                                        and d > last_drain
                                        for d in st["deps"]
                                    )
                                    if need:
                                        eng.drain(fusable=False)
                                        last_drain = st["i"]
                                for d in sorted(st["deps"]):
                                    ds = steps_all[d]
                                    if ds["stream"] == st["stream"] and not ds["dma"] and (st["eng"] == "tensor" or not SAME_ENGINE_EDGES):
                                        continue
                                    sem, unit = sems[ds["stream"]]
                                    if ds["dma"]:
                                        sem = sem[ds["lane"]]
                                        wkey = (ds["stream"], ds["lane"])
                                    else:
                                        wkey = ds["stream"]
                                    val = ds["rank"] * unit
                                    if watermark.get(wkey, -1) >= val:
                                        continue
                                    eng.wait_ge(sem, val)
                                    watermark[wkey] = val
                                if st["dma"] and st["rank"] > 1:
                                    sem, unit = sems[st["stream"]]
                                    sem = sem[st["lane"]]
                                    wkey = (st["stream"], st["lane"])
                                    val = (st["rank"] - 1) * unit
                                    if watermark.get(wkey, -1) < val:
                                        eng.wait_ge(sem, val)
                                        watermark[wkey] = val
                                ins = st["fn"](eng)
                                if st["inc"]:
                                    sem, unit = sems[st["stream"]]
                                    if st["dma"]:
                                        sem = sem[st["lane"]]
                                    ins.then_inc(sem, unit)
                            return None

                        return body

                    getattr(block, eng_name)(make_body(lst))


class Ctx:
    def __init__(self, name, bf16_banks=()):
        nc = bass.Bass("TRN2", target_bir_lowering=False)
        self.nc = nc
        self.p = Prog(nc)
        self.name = name
        self.pb = [nc.alloc_psum_tensor(f"pb{i}", [128, 1024] if i in bf16_banks else [128, 512],
                                        BF16 if i in bf16_banks else F32) for i in range(8)]
        self.ones_bf = nc.alloc_sbuf_tensor("ones_bf", [1, 128], BF16)
        self.ones_f = nc.alloc_sbuf_tensor("ones_f", [1, 128], F32)
        self.onescol_bf = nc.alloc_sbuf_tensor("onescol_bf", [128, 1], BF16)
        self.eps_col = nc.alloc_sbuf_tensor("eps_col", [128, 1], F32)
        self.p.step("vector", lambda e: e.memset(self.ones_bf[:, :], 1.0), writes=("ones_bf",))
        self.p.step("vector", lambda e: e.memset(self.ones_f[:, :], 1.0), writes=("ones_f",))
        self.p.step("vector", lambda e: e.memset(self.onescol_bf[:, :], 1.0), writes=("onescol_bf",))
        self.p.step("vector", lambda e: e.memset(self.eps_col[:, :], EPS), writes=("eps_col",))
        self._uid = 0
        self.arena_base = None
        self.arena_cur = None
        self.arena_top = None
        self.pfx = ""

    def enable_arena(self, base_bytes=16768, top_bytes=229000):
        self.arena_base = base_bytes
        self.arena_cur = base_bytes
        self.arena_top = top_bytes

    def uid(self):
        self._uid += 1
        return self._uid

    def sb(self, name, shape, dtype):
        if self.arena_base is None:
            return self.nc.alloc_sbuf_tensor(self.pfx + name, shape, dtype)
        import math as _m
        per_part = int(np.prod(shape[1:])) * mybir.dt.size(dtype)
        off = self.arena_cur
        self.arena_cur = (off + per_part + 31) // 32 * 32
        assert self.arena_cur <= self.arena_top, (
            f"arena overflow in phase {self.pfx}: {self.arena_cur} > {self.arena_top} ({name})")
        return self.nc.alloc_sbuf_tensor_at(self.pfx + name, shape, dtype, offset=off)

    def phase(self, name):
        self.p.barrier()
        self.pfx = name + "_"
        if self.arena_base is not None:
            self.arena_cur = self.arena_base

    def din(self, name, shape, dtype=F32):
        return self.nc.dram_tensor(name, shape, dtype, kind="ExternalInput").ap()

    def dout(self, name, shape, dtype=F32):
        return self.nc.dram_tensor(name, shape, dtype, kind="ExternalOutput").ap()

    def dscratch(self, name, shape, dtype=F32):
        return self.nc.dram_tensor(name, shape, dtype, kind="Internal").ap()

    def dma(self, queue, out_ap, in_ap, reads=(), writes=(), tag=""):
        return self.p.step(
            queue, lambda e: e.dma_start(out_ap, in_ap), reads=reads, writes=writes,
            dma=True, tag=tag,
        )

    def mm_group(self, psum_ap, psum_key, pairs, tag=""):
        n = len(pairs)
        for j, (l, r, rk) in enumerate(pairs):
            def fn(e, l=l, r=r, j=j, n=n):
                return e.matmul(psum_ap, l, r, start=(j == 0), stop=(j == n - 1))
            self.p.step("tensor", fn, reads=tuple(rk), writes=(psum_key,), tag=tag)

    def ln_fm(self, T, nb, x3, xkey, xb3, xbk, sq3, sqk, t13, t1k, out3, outkey,
              rows, rowsk, bsA, bsAk, bsB, bsBk, bbR=None, bbRk=None, bbM=None, bbMk=None,
              cast_engine="gpsimd"):
        if bbR is None:
            bbR, bbRk, bbM, bbMk = bsA, bsAk, bsB, bsBk
        """LayerNorm over the feature axis, feature-major. x3/xb3/sq3/t13/out3:
        [128, nb, T] views (out3 may be bf16 or f32). Uses 2 stats banks (row 0)
        and 2 broadcast banks (full). T <= 512."""
        assert T <= 512
        p = self.p
        p.step(cast_engine, lambda e: e.tensor_copy(xb3, x3), reads=(xkey,), writes=(xbk,))
        p.step("scalar", lambda e: e.activation(sq3, xb3, AF.Square), reads=(xbk,), writes=(sqk,))
        self.mm_group(bsA[0:1, 0:T], bsAk,
                      [(self.onescol_bf[:, :], xb3[:, i, :], (xbk, "onescol_bf")) for i in range(nb)])
        self.mm_group(bsB[0:1, 0:T], bsBk,
                      [(self.onescol_bf[:, :], sq3[:, i, :], (sqk, "onescol_bf")) for i in range(nb)])
        Dn = float(nb * 128)
        m_r = rows[0:1, 0:T]
        var_r = rows[0:1, T:2 * T]
        rstd_r = rows[0:1, 2 * T:3 * T]
        mistd_r = rows[0:1, 3 * T:4 * T]
        p.step("vector", lambda e: e.tensor_scalar(m_r, bsA[0:1, 0:T], 1.0 / Dn, None, ALU.mult), reads=(bsAk,), writes=(rowsk,))
        p.step("vector", lambda e: e.tensor_mul(var_r, m_r, m_r), reads=(rowsk,), writes=(rowsk,))
        p.step("vector", lambda e: e.scalar_tensor_tensor(var_r, bsB[0:1, 0:T], 1.0 / Dn, var_r, ALU.mult, ALU.subtract), reads=(bsBk, rowsk), writes=(rowsk,))
        p.step("scalar", lambda e: e.activation(rstd_r, var_r, AF.Sqrt, bias=self.eps_col[0:1, 0:1]), reads=(rowsk, "eps_col"), writes=(rowsk,))
        p.step("vector", lambda e: e.reciprocal(rstd_r, rstd_r), reads=(rowsk,), writes=(rowsk,))
        p.step("vector", lambda e: e.tensor_mul(mistd_r, m_r, rstd_r), reads=(rowsk,), writes=(rowsk,))
        self.mm_group(bbR[:, 0:T], bbRk, [(self.ones_f[0:1, :], rstd_r, (rowsk, "ones_f"))])
        self.mm_group(bbM[:, 0:T], bbMk, [(self.ones_f[0:1, :], mistd_r, (rowsk, "ones_f"))])
        rstd_b = bbR[:, 0:T].unsqueeze(1).broadcast_to([128, nb, T])
        mistd_b = bbM[:, 0:T].unsqueeze(1).broadcast_to([128, nb, T])
        p.step("vector", lambda e: e.tensor_mul(t13, x3, rstd_b), reads=(xkey, bbRk), writes=(t1k,))
        p.step("vector", lambda e: e.tensor_sub(out3, t13, mistd_b), reads=(t1k, bbMk), writes=(outkey,))


def fmvec(v):
    return np.ascontiguousarray(np.asarray(v, dtype=np.float32).reshape(-1, 128).T)


def build_l1_ffn(Ttot=S, ctx=None, io=None):
    standalone = ctx is None
    if standalone:
        ctx = Ctx("l1ffnX")
        io = dict(xin_fm=ctx.din("xin_fm", [D, S]), w1=ctx.din("w1", [D, 4096]),
                  w2=ctx.din("w2", [4096, D]), b1f=ctx.din("b1f", [128, 32]),
                  b2f=ctx.din("b2f", [128, KB]), xout_fm=ctx.dout("xout_fm", [D, S]))
    else:
        ctx.phase("F1")
    nc, p = ctx.nc, ctx.p
    TC = 256
    nch = Ttot // TC
    H = 4096
    HB = H // 128

    xin = io["xin_fm"]
    w1 = io["w1"]
    w2 = io["w2"]
    b1f = io["b1f"]
    b2f = io["b2f"]
    xout = io["xout_fm"]

    stage = ctx.sb("stage", [128, 2048], F32)
    w1b = ctx.sb("w1b", [128, KB, H], BF16)
    w2b = ctx.sb("w2b", [128, HB, D], BF16)
    b1s = ctx.sb("b1s", [128, HB], F32)
    b2s = ctx.sb("b2s", [128, KB], F32)
    ctx.dma("sync", b1s[:, :], b1f, writes=("b1s",))
    ctx.dma("sync", b2s[:, :], b2f, writes=("b2s",))
    for kc in range(KB):
        for hh in range(2):
            ctx.dma("sync", stage[:, :], w1[kc * 128:(kc + 1) * 128, hh * 2048:(hh + 1) * 2048], writes=("stage",))
            p.step("gpsimd", lambda e, kc=kc, hh=hh: e.tensor_copy(w1b[:, kc, hh * 2048:(hh + 1) * 2048], stage[:, :]), reads=("stage",), writes=("w1b",))
    for hc in range(HB):
        ctx.dma("sync", stage[:, 0:D], w2[hc * 128:(hc + 1) * 128, :], writes=("stage",))
        p.step("gpsimd", lambda e, hc=hc: e.tensor_copy(w2b[:, hc, :], stage[:, 0:D]), reads=("stage",), writes=("w2b",))

    xch = ctx.sb("xch", [128, KB, TC], F32)
    xb = ctx.sb("xb", [128, KB, TC], BF16)
    sq = ctx.sb("sq", [128, KB, TC], BF16)
    t1 = ctx.sb("t1", [128, KB, TC], F32)
    xnb = ctx.sb("xnb", [128, KB, TC], BF16)
    hb = ctx.sb("hb", [128, HB, TC], BF16)
    yf = ctx.sb("yf", [128, KB, TC], F32)
    rows = ctx.sb("rows", [1, 2048], F32)

    xin_r = xin.rearrange("(a p) t -> p a t", p=128)
    xout_r = xout.rearrange("(a p) t -> p a t", p=128)

    for c in range(nch):
        t0 = c * TC
        ctx.dma("sync", xch.ap(), xin_r[:, :, t0:t0 + TC], writes=("xch",), tag=f"xld{c}")
        ctx.ln_fm(TC, KB, xch.ap(), "xch", xb.ap(), "xb", sq.ap(), "sq",
                  t1.ap(), "t1", xnb.ap(), "xnb", rows, "rows",
                  ctx.pb[6], "pb6", ctx.pb[7], "pb7")
        for mb in range(HB):
            bank, bk = ctx.pb[mb % 2], f"pb{mb % 2}"
            ctx.mm_group(bank[:, 0:TC], bk,
                         [(w1b[:, kc, mb * 128:(mb + 1) * 128], xnb[:, kc, :], ("w1b", "xnb")) for kc in range(KB)],
                         tag=f"fc{c}_{mb}")
            p.step("scalar", lambda e, mb=mb, bank=bank: e.activation(hb[:, mb, :], bank[:, 0:TC], AF.Gelu, bias=b1s[:, mb:mb + 1]),
                   reads=(bk, "b1s"), writes=("hb",), tag=f"gelu{c}_{mb}")
        for ob in range(KB):
            bank, bk = ctx.pb[2 + ob % 2], f"pb{2 + ob % 2}"
            ctx.mm_group(bank[:, 0:TC], bk,
                         [(w2b[:, hc, ob * 128:(ob + 1) * 128], hb[:, hc, :], ("w2b", "hb")) for hc in range(HB)],
                         tag=f"pj{c}_{ob}")
            p.step("scalar", lambda e, ob=ob, bank=bank: e.activation(yf[:, ob, :], bank[:, 0:TC], AF.Identity, bias=b2s[:, ob:ob + 1]),
                   reads=(bk, "b2s"), writes=("yf",), tag=f"bias{c}_{ob}")
        ctx.ln_fm(TC, KB, yf.ap(), "yf", xb.ap(), "xb", sq.ap(), "sq",
                  t1.ap(), "t1", t1.ap(), "t1", rows, "rows",
                  ctx.pb[6], "pb6", ctx.pb[7], "pb7")
        p.step("vector", lambda e: e.tensor_add(xch.ap(), xch.ap(), t1.ap()), reads=("xch", "t1"), writes=("xch",), tag=f"resadd{c}")
        ctx.dma("sync", xout_r[:, :, t0:t0 + TC], xch.ap(), reads=("xch",), writes=(f"xout{c}",), tag=f"xst{c}")

    if standalone:
        p.step("gpsimd", lambda e: e.nop(), reads=tuple(f"xout{c}" for c in range(nch)), tag="final_wait")
        ctx.p.emit()
    return ctx


def l1_ffn_inmap(x_fm, w1, b1, w2, b2):
    return {
        "xin_fm": np.ascontiguousarray(x_fm),
        "w1": np.ascontiguousarray(w1), "w2": np.ascontiguousarray(w2),
        "b1f": fmvec(b1), "b2f": fmvec(b2),
    }


def build_l0_ffn(Ttot=S, ctx=None, io=None):
    standalone = ctx is None
    if standalone:
        ctx = Ctx("l0ffnX")
        io = dict(xin_fm=ctx.din("xin_fm", [D, S]), w1t=ctx.din("w1t", [D, 4096]),
                  w2t=ctx.din("w2t", [4096, D]), b1f=ctx.din("b1f", [128, 32]),
                  b2f=ctx.din("b2f", [128, KB]), scale1=ctx.din("scale1", [1, 1]),
                  ident_bf=ctx.din("ident_bf", [128, 128], BF16),
                  xout_fm=ctx.dout("xout_fm", [D, S]))
    else:
        ctx.phase("F0")
    nc, p = ctx.nc, ctx.p
    TC = 128
    nch = Ttot // TC
    H = 4096
    HB = H // 128

    xin = io["xin_fm"]
    w1t = io["w1t"]
    w2t = io["w2t"]
    b1f = io["b1f"]
    b2f = io["b2f"]
    scale1 = io["scale1"]
    identb = io["ident_bf"]
    xout = io["xout_fm"]

    stage = ctx.sb("stage", [128, 1024], F32)
    w1b = ctx.sb("w1b", [128, KB, H], BF16)
    w2b = ctx.sb("w2b", [128, HB, D], BF16)
    b1s = ctx.sb("b1s", [128, HB], F32)
    b2s = ctx.sb("b2s", [128, KB], F32)
    idb = ctx.sb("idb", [128, 128], BF16)
    scs = ctx.sb("scs", [1, 1], F32)
    ctx.dma("sync", b1s[:, :], b1f, writes=("b1s",))
    ctx.dma("sync", b2s[:, :], b2f, writes=("b2s",))
    ctx.dma("sync", idb[:, :], identb, writes=("idb",))
    ctx.dma("sync", scs[:, :], scale1, writes=("scs",))
    for kc in range(KB):
        for hh in range(4):
            ctx.dma("sync", stage[:, :], w1t[kc * 128:(kc + 1) * 128, hh * 1024:(hh + 1) * 1024], writes=("stage",))
            p.step("gpsimd", lambda e, kc=kc, hh=hh: e.tensor_copy(w1b[:, kc, hh * 1024:(hh + 1) * 1024], stage[:, :]), reads=("stage",), writes=("w1b",))
    for hc in range(HB):
        ctx.dma("sync", stage[:, 0:D], w2t[hc * 128:(hc + 1) * 128, :], writes=("stage",), tag=f"w2dma{hc}")
        p.step("gpsimd", lambda e, hc=hc: e.tensor_copy(w2b[:, hc, :], stage[:, 0:D]), reads=("stage",), writes=("w2b",))

    xch = ctx.sb("xch", [128, KB, TC], F32)
    xb = ctx.sb("xb", [128, KB, TC], BF16)
    sq = ctx.sb("sq", [128, KB, TC], BF16)
    t1 = ctx.sb("t1", [128, KB, TC], F32)
    xnb = ctx.sb("xnb", [128, KB, TC], BF16)
    hb = ctx.sb("hb", [128, HB, TC], BF16)
    yf = ctx.sb("yf", [128, KB, TC], F32)
    rows = ctx.sb("rows", [1, 4 * TC], F32)
    odd1f = ctx.sb("odd1f", [128, KB, 4], BF16)
    even1T = ctx.sb("even1T", [1, 4 * D], BF16)
    even2T = ctx.sb("even2T", [1, 4 * D], BF16)
    odd2f = ctx.sb("odd2f", [128, KB, 4], BF16)
    tib = ctx.sb("tib", [1, 4 * TC], BF16)
    sub = ctx.sb("sub", [1, 16 * TC], BF16)

    xin_r = xin.rearrange("(a p) t -> p a t", p=128)
    xout_r = xout.rearrange("(a p) t -> p a t", p=128)

    trb = ctx.pb[3][:, :].bitcast(BF16)

    for c in range(nch):
        t0 = c * TC
        ctx.dma("sync", xch.ap(), xin_r[:, :, t0:t0 + TC], writes=("xch",), tag=f"xld{c}")
        ctx.ln_fm(TC, KB, xch.ap(), "xch", xb.ap(), "xb", sq.ap(), "sq",
                  t1.ap(), "t1", xnb.ap(), "xnb", rows, "rows",
                  ctx.pb[6], "pb6", ctx.pb[7], "pb7")
        if c == 0:
            p.step("vector", lambda e: e.tensor_copy(odd1f.ap(), xnb.ap()[:, :, 2:9:2]), reads=("xnb",), writes=("odd1f",))
            p.step("vector", lambda e: e.tensor_copy(odd2f.ap(), xnb.ap()[:, :, 10:17:2]), reads=("xnb",), writes=("odd2f",))
            for i in range(4):
                for kc in range(KB):
                    tok_e1 = 1 + 2 * i
                    tok_e2 = 9 + 2 * i
                    p.step("tensor", lambda e, kc=kc, tok=tok_e1: e.transpose(trb[0:1, 0:128], xnb[:, kc, tok:tok + 1], idb[:, :]),
                           reads=("xnb", "idb"), writes=("pb3",))
                    p.step("vector", lambda e, i=i, kc=kc: e.tensor_copy(even1T[0:1, i * D + kc * 128: i * D + (kc + 1) * 128], trb[0:1, 0:128]),
                           reads=("pb3",), writes=("even1T",))
                    p.step("tensor", lambda e, kc=kc, tok=tok_e2: e.transpose(trb[0:1, 0:128], xnb[:, kc, tok:tok + 1], idb[:, :]),
                           reads=("xnb", "idb"), writes=("pb3",))
                    p.step("vector", lambda e, i=i, kc=kc: e.tensor_copy(even2T[0:1, i * D + kc * 128: i * D + (kc + 1) * 128], trb[0:1, 0:128]),
                           reads=("pb3",), writes=("even2T",))
        for i in range(4):
            bank, bk = ctx.pb[2], "pb2"
            ctx.mm_group(bank[0:1, 0:TC], bk,
                         [(odd1f[:, kc, i:i + 1], xnb[:, kc, :], ("odd1f", "xnb")) for kc in range(KB)],
                         tag=f"t{c}_{i}")
            p.step("vector", lambda e, i=i, bank=bank: e.tensor_scalar(tib[0:1, i * TC:(i + 1) * TC], bank[0:1, 0:TC], scs[0:1, 0:1], None, ALU.mult),
                   reads=(bk, "scs"), writes=("tib",))
        for mb in range(HB):
            bank, bk = ctx.pb[mb % 2], f"pb{mb % 2}"
            i = mb // 8
            pairs = [(w1b[:, kc, mb * 128:(mb + 1) * 128], xnb[:, kc, :], ("w1b", "xnb")) for kc in range(KB)]
            pairs.append((even1T[0:1, i * D + (mb % 8) * 128: i * D + (mb % 8 + 1) * 128], tib[0:1, i * TC:(i + 1) * TC], ("even1T", "tib")))
            ctx.mm_group(bank[:, 0:TC], bk, pairs, tag=f"fc{c}_{mb}")
            p.step("scalar", lambda e, mb=mb, bank=bank: e.activation(hb[:, mb, :], bank[:, 0:TC], AF.Gelu, bias=b1s[:, mb:mb + 1]),
                   reads=(bk, "b1s"), writes=("hb",), tag=f"gelu{c}_{mb}")
        for i in range(4):
            for j in range(4):
                bank, bk = ctx.pb[2], "pb2"
                ctx.mm_group(bank[0:1, 0:TC], bk,
                             [(odd2f[:, kc, i:i + 1], hb[:, j * 8 + kc, :], ("odd2f", "hb")) for kc in range(KB)],
                             tag=f"u{c}_{i}{j}")
                p.step("vector", lambda e, i=i, j=j, bank=bank: e.tensor_scalar(sub[0:1, (i * 4 + j) * TC:(i * 4 + j + 1) * TC], bank[0:1, 0:TC], scs[0:1, 0:1], None, ALU.mult),
                       reads=(bk, "scs"), writes=("sub",))
        for ob in range(KB):
            bank, bk = ctx.pb[2 + ob % 2] if False else ctx.pb[ob % 2], f"pb{ob % 2}"
            i = ob // 2
            r0 = (ob % 2) * 128
            pairs = [(w2b[:, hc, ob * 128:(ob + 1) * 128], hb[:, hc, :], ("w2b", "hb")) for hc in range(HB)]
            for j in range(4):
                st0 = i * D + 4 * r0 + j
                eap = even2T[0:1, st0: st0 + 4 * 127 + 1: 4]
                pairs.append((eap, sub[0:1, (i * 4 + j) * TC:(i * 4 + j + 1) * TC], ("even2T", "sub")))
            ctx.mm_group(bank[:, 0:TC], bk, pairs, tag=f"pj{c}_{ob}")
            p.step("scalar", lambda e, ob=ob, bank=bank: e.activation(yf[:, ob, :], bank[:, 0:TC], AF.Identity, bias=b2s[:, ob:ob + 1]),
                   reads=(bk, "b2s"), writes=("yf",), tag=f"bias{c}_{ob}")
        ctx.ln_fm(TC, KB, yf.ap(), "yf", xb.ap(), "xb", sq.ap(), "sq",
                  t1.ap(), "t1", t1.ap(), "t1", rows, "rows",
                  ctx.pb[6], "pb6", ctx.pb[7], "pb7")
        p.step("vector", lambda e: e.tensor_add(xch.ap(), xch.ap(), t1.ap()), reads=("xch", "t1"), writes=("xch",), tag=f"resadd{c}")
        ctx.dma("sync", xout_r[:, :, t0:t0 + TC], xch.ap(), reads=("xch",), writes=(f"xout{c}",), tag=f"xst{c}")

    if standalone:
        p.step("gpsimd", lambda e: e.nop(), reads=tuple(f"xout{c}" for c in range(nch)), tag="final_wait")
        ctx.p.emit()
    return ctx


def l0_ffn_inmap(x_fm, W1, b1, W2, b2, scale):
    return {
        "xin_fm": np.ascontiguousarray(x_fm),
        "w1t": np.ascontiguousarray(W1.T), "w2t": np.ascontiguousarray(W2.T),
        "b1f": fmvec(b1), "b2f": fmvec(b2),
        "scale1": np.asarray(scale, np.float32).reshape(1, 1),
        "ident_bf": np.eye(128, dtype=bfloat16),
    }




def build_ln_pre(Ttot=S, ctx=None, io=None):
    standalone = ctx is None
    if standalone:
        ctx = Ctx("lnpreX")
        io = dict(xin_fm=ctx.din("xin_fm", [D, S]), xi_fm=ctx.dout("xi_fm", [D, S]))
    else:
        ctx.phase("PR")
    p = ctx.p
    TC = 256
    nch = Ttot // TC
    xin = io["xin_fm"]
    xout = io["xi_fm"]
    xch = ctx.sb("xch", [128, KB, TC], F32)
    xb = ctx.sb("xb", [128, KB, TC], BF16)
    sq = ctx.sb("sq", [128, KB, TC], BF16)
    t1 = ctx.sb("t1", [128, KB, TC], F32)
    rows = ctx.sb("rows", [1, 4 * TC], F32)
    xin_r = xin.rearrange("(a p) t -> p a t", p=128)
    xout_r = xout.rearrange("(a p) t -> p a t", p=128)
    for c in range(nch):
        t0 = c * TC
        ctx.dma("sync", xch.ap(), xin_r[:, :, t0:t0 + TC], writes=("xch",))
        ctx.ln_fm(TC, KB, xch.ap(), "xch", xb.ap(), "xb", sq.ap(), "sq",
                  t1.ap(), "t1", t1.ap(), "t1", rows, "rows",
                  ctx.pb[6], "pb6", ctx.pb[7], "pb7")
        ctx.dma("sync", xout_r[:, :, t0:t0 + TC], t1.ap(), reads=("t1",), writes=(f"xo{c}",))
    if standalone:
        p.step("gpsimd", lambda e: e.nop(), reads=tuple(f"xo{c}" for c in range(nch)), tag="final_wait")
        p.emit()
    return ctx


def build_addln(Ttot=S, ctx=None, io=None):
    standalone = ctx is None
    if standalone:
        ctx = Ctx("addlnX")
        io = dict(xa_fm=ctx.din("xa_fm", [D, S]), xb_fm=ctx.din("xb_fm", [D, S]),
                  xout_fm=ctx.dout("xout_fm", [D, S]))
    else:
        ctx.phase("PO")
    p = ctx.p
    TC = 256
    nch = Ttot // TC
    xa = io["xa_fm"]
    xbin = io["xb_fm"]
    xout = io["xout_fm"]
    ach = ctx.sb("ach", [128, KB, TC], F32)
    xch = ctx.sb("xch", [128, KB, TC], F32)
    xb = ctx.sb("xb", [128, KB, TC], BF16)
    sq = ctx.sb("sq", [128, KB, TC], BF16)
    t1 = ctx.sb("t1", [128, KB, TC], F32)
    rows = ctx.sb("rows", [1, 4 * TC], F32)
    xa_r = xa.rearrange("(a p) t -> p a t", p=128)
    xb_r = xbin.rearrange("(a p) t -> p a t", p=128)
    xout_r = xout.rearrange("(a p) t -> p a t", p=128)
    for c in range(nch):
        t0 = c * TC
        ctx.dma("sync", ach.ap(), xa_r[:, :, t0:t0 + TC], writes=("ach",))
        ctx.dma("sync", xch.ap(), xb_r[:, :, t0:t0 + TC], writes=("xch",))
        ctx.ln_fm(TC, KB, xch.ap(), "xch", xb.ap(), "xb", sq.ap(), "sq",
                  t1.ap(), "t1", t1.ap(), "t1", rows, "rows",
                  ctx.pb[6], "pb6", ctx.pb[7], "pb7")
        p.step("vector", lambda e: e.tensor_add(ach.ap(), ach.ap(), t1.ap()), reads=("ach", "t1"), writes=("ach",))
        ctx.dma("sync", xout_r[:, :, t0:t0 + TC], ach.ap(), reads=("ach",), writes=(f"xo{c}",))
    if standalone:
        p.step("gpsimd", lambda e: e.nop(), reads=tuple(f"xo{c}" for c in range(nch)), tag="final_wait")
        p.emit()
    return ctx


def _load_scaled_weight(ctx, wname, w_hbm, Kin, Nout, stage, stage_cols, wb, wbk,
                        lnwb=None, lnbb=None, biasf=None, beff=None, beffk=None,
                        bank=None, bankk=None):
    p = ctx.p
    nk = Kin // 128
    nm = Nout // 128
    for kc in range(nk):
        for h0 in range(0, Nout, stage_cols):
            hw = min(stage_cols, Nout - h0)
            ctx.dma("sync", stage[:, 0:hw], w_hbm[kc * 128:(kc + 1) * 128, h0:h0 + hw], writes=("stage",))
            p.step("gpsimd", lambda e, kc=kc, h0=h0, hw=hw: e.tensor_copy(wb[:, kc, h0:h0 + hw], stage[:, 0:hw]), reads=("stage",), writes=(wbk,))
    if lnbb is not None:
        for mb in range(nm):
            ctx.mm_group(bank[:, 0:1], bankk,
                         [(wb[:, kc, mb * 128:(mb + 1) * 128], lnbb[:, kc:kc + 1], (wbk, "lnbb")) for kc in range(nk)])
            p.step("vector", lambda e, mb=mb: e.scalar_tensor_tensor(beff[:, mb:mb + 1], bank[:, 0:1], 1.0, biasf[:, mb:mb + 1], ALU.mult, ALU.add),
                   reads=(bankk, "biasf"), writes=(beffk,))
    if lnwb is not None:
        for kc in range(nk):
            p.step("vector", lambda e, kc=kc: e.tensor_scalar(wb[:, kc, :], wb[:, kc, :], lnwb[:, kc:kc + 1], None, ALU.mult),
                   reads=(wbk, "lnwb"), writes=(wbk,))


def build_l0a_qkv(Ttot=S, ctx=None, io=None, pfx=""):
    standalone = ctx is None
    if standalone:
        ctx = Ctx("l0aqkvX")
        io = dict(xi_fm=ctx.din("xi_fm", [D, S]), attn_w=ctx.din("attn_w", [D, 3072]),
                  attn_bf=ctx.din("attn_bf", [128, 24]), ln1wf=ctx.din("ln1wf", [128, KB]),
                  ln1bf=ctx.din("ln1bf", [128, KB]), w_fm=ctx.dout("w_fm", [D, S], BF16))
    else:
        ctx.phase("Q" + pfx)
    nc, p = ctx.nc, ctx.p
    TC = 128
    nch = Ttot // TC
    NQ = 24

    xi = io["xi_fm"]
    aw = io["attn_w"]
    abf = io["attn_bf"]
    lnwf = io["ln1wf"]
    lnbf = io["ln1bf"]
    wout = io["w_fm"]

    stage = ctx.sb("stage", [128, 1536], F32)
    awb = ctx.sb("awb", [128, KB, 3072], BF16)
    lnw = ctx.sb("lnw", [128, KB], F32)
    lnb = ctx.sb("lnb", [128, KB], F32)
    lnbb = ctx.sb("lnbb", [128, KB], BF16)
    abs_ = ctx.sb("abs", [128, NQ], F32)
    beff = ctx.sb("beff", [128, NQ], F32)
    ctx.dma("sync", lnw[:, :], lnwf, writes=("lnwb",))
    ctx.dma("sync", lnb[:, :], lnbf, writes=("lnbq",))
    ctx.dma("sync", abs_[:, :], abf, writes=("biasf",))
    p.step("vector", lambda e: e.tensor_copy(lnbb[:, :], lnb[:, :]), reads=("lnbq",), writes=("lnbb",))
    _load_scaled_weight(ctx, "aw", aw, D, 3072, stage, 1536, awb, "awb",
                        lnwb=lnw, lnbb=lnbb, biasf=abs_, beff=beff, beffk="beff",
                        bank=ctx.pb[2], bankk="pb2")

    xch = ctx.sb("xch", [128, KB, TC], F32)
    xb = ctx.sb("xb", [128, KB, TC], BF16)
    sq = ctx.sb("sq", [128, KB, TC], BF16)
    t1 = ctx.sb("t1", [128, KB, TC], F32)
    xnb = ctx.sb("xnb", [128, KB, TC], BF16)
    rows = ctx.sb("rows", [1, 4 * TC], F32)
    qkv = ctx.sb("qkv", [128, NQ, TC], F32)
    qprev = ctx.sb("qprev", [128, NQ, 1], F32)
    dqkv = ctx.sb("dqkv", [128, NQ, TC], F32)
    rqkv = ctx.sb("rqkv", [128, NQ, TC], F32)
    cqkv = ctx.sb("cqkv", [128, NQ, TC], F32)
    s1 = ctx.sb("s1", [128, NQ, TC], F32)
    si = ctx.sb("si", [128, NQ, TC], I32)
    qk0 = ctx.sb("qk0", [128, KB, TC], F32)
    qk1 = ctx.sb("qk1", [128, KB, TC], F32)
    qk2 = ctx.sb("qk2", [128, KB, TC], F32)
    qk3 = ctx.sb("qk3", [128, KB, TC], F32)
    scr = ctx.sb("scr", [128, KB, TC], F32)
    wtmp = ctx.sb("wtmp", [128, KB, TC], F32)
    wb16 = ctx.sb("wb16", [128, KB, TC], BF16)

    xi_r = xi.rearrange("(a p) t -> p a t", p=128)
    wout_r = wout.rearrange("(a p) t -> p a t", p=128)

    INV2PI = float(1.0 / (2.0 * np.pi))
    TWOPI = float(2.0 * np.pi)

    for c in range(nch):
        t0 = c * TC
        ctx.dma("sync", xch.ap(), xi_r[:, :, t0:t0 + TC], writes=("xch",))
        ctx.ln_fm(TC, KB, xch.ap(), "xch", xb.ap(), "xb", sq.ap(), "sq",
                  t1.ap(), "t1", xnb.ap(), "xnb", rows, "rows",
                  ctx.pb[6], "pb6", ctx.pb[7], "pb7")
        for mb in range(NQ):
            bank, bk = ctx.pb[mb % 2], f"pb{mb % 2}"
            ctx.mm_group(bank[:, 0:TC], bk,
                         [(awb[:, kc, mb * 128:(mb + 1) * 128], xnb[:, kc, :], ("awb", "xnb")) for kc in range(KB)])
            p.step("scalar", lambda e, mb=mb, bank=bank: e.activation(qkv[:, mb, :], bank[:, 0:TC], AF.Identity, bias=beff[:, mb:mb + 1]),
                   reads=(bk, "beff"), writes=("qkv",))
        p.step("vector", lambda e: e.tensor_sub(dqkv.ap()[:, :, 1:TC], qkv.ap()[:, :, 1:TC], qkv.ap()[:, :, 0:TC - 1]), reads=("qkv",), writes=("dqkv",))
        if c == 0:
            p.step("vector", lambda e: e.memset(dqkv.ap()[:, :, 0:1], 0.0), writes=("dqkv",))
        else:
            p.step("vector", lambda e: e.tensor_sub(dqkv.ap()[:, :, 0:1], qkv.ap()[:, :, 0:1], qprev.ap()), reads=("qkv", "qprev"), writes=("dqkv",))
        p.step("vector", lambda e: e.tensor_copy(qprev.ap(), qkv.ap()[:, :, TC - 1:TC]), reads=("qkv",), writes=("qprev",))
        p.step("vector", lambda e: e.tensor_copy(rqkv.ap()[64:128, :, :], qkv.ap()[0:64, :, :]), reads=("qkv",), writes=("rqkv",))
        for sec in range(3):
            b0 = sec * 8
            p.step("vector", lambda e, b0=b0: e.tensor_copy(rqkv.ap()[0:64, b0 + 1:b0 + 8, :], qkv.ap()[64:128, b0:b0 + 7, :]), reads=("qkv",), writes=("rqkv",))
            p.step("vector", lambda e, b0=b0: e.tensor_copy(rqkv.ap()[0:64, b0:b0 + 1, :], qkv.ap()[64:128, b0 + 7:b0 + 8, :]), reads=("qkv",), writes=("rqkv",))
        p.step("vector", lambda e: e.tensor_scalar(s1.ap(), qkv.ap(), -INV2PI, 0.25, ALU.mult, ALU.add), reads=("qkv",), writes=("s1",))
        p.step("vector", lambda e: e.tensor_copy(si.ap(), s1.ap()), reads=("s1",), writes=("si",))
        p.step("vector", lambda e: e.tensor_sub(s1.ap(), s1.ap(), si.ap()), reads=("s1", "si"), writes=("s1",))
        p.step("scalar", lambda e: e.activation(cqkv.ap(), s1.ap(), AF.Sin, scale=TWOPI), reads=("s1",), writes=("cqkv",))
        q, k, v = qkv[:, 0:8, :], qkv[:, 8:16, :], qkv[:, 16:24, :]
        dq, dk, dv = dqkv[:, 0:8, :], dqkv[:, 8:16, :], dqkv[:, 16:24, :]
        rq, rk, rv = rqkv[:, 0:8, :], rqkv[:, 8:16, :], rqkv[:, 16:24, :]
        cq, ck, cv = cqkv[:, 0:8, :], cqkv[:, 8:16, :], cqkv[:, 16:24, :]
        def acc4(out, outk, terms):
            (s0, a0, a0k, b0, b0k) = terms[0]
            assert s0 == 1
            p.step("vector", lambda e: e.tensor_mul(out, a0, b0), reads=(a0k, b0k), writes=(outk,))
            for (sg, a, akk, b, bkk) in terms[1:]:
                p.step("vector", lambda e, a=a, b=b: e.tensor_mul(scr.ap(), a, b), reads=(akk, bkk), writes=("scr",))
                op = ALU.add if sg > 0 else ALU.subtract
                p.step("vector", lambda e, op=op: e.tensor_tensor(out, out, scr.ap(), op), reads=(outk, "scr"), writes=(outk,))
        Q, K_, V, DQ, DK, DV, RQ, RK, RV, CQ, CK, CV = ("qkv",) * 3 + ("dqkv",) * 3 + ("rqkv",) * 3 + ("cqkv",) * 3
        acc4(qk0.ap(), "qk0", [(1, q, Q, k, K_), (-1, dq, DQ, dk, DK), (-1, rq, RQ, rk, RK), (-1, cq, CQ, ck, CK)])
        acc4(qk1.ap(), "qk1", [(1, q, Q, dk, DK), (1, dq, DQ, k, K_), (1, rq, RQ, ck, CK), (-1, cq, CQ, rk, RK)])
        acc4(qk2.ap(), "qk2", [(1, q, Q, rk, RK), (-1, dq, DQ, ck, CK), (1, rq, RQ, k, K_), (1, cq, CQ, dk, DK)])
        acc4(qk3.ap(), "qk3", [(1, q, Q, ck, CK), (1, dq, DQ, rk, RK), (-1, rq, RQ, dk, DK), (1, cq, CQ, k, K_)])
        p.step("vector", lambda e: e.tensor_mul(wtmp.ap(), qk0.ap(), v), reads=("qk0", V), writes=("wtmp",))
        for (qq, qqk, bb, bbk) in [(qk1, "qk1", dv, DV), (qk2, "qk2", rv, RV)]:
            p.step("vector", lambda e, qq=qq, bb=bb: e.tensor_mul(scr.ap(), qq.ap(), bb), reads=(qqk, bbk), writes=("scr",))
            p.step("vector", lambda e: e.tensor_sub(wtmp.ap(), wtmp.ap(), scr.ap()), reads=("wtmp", "scr"), writes=("wtmp",))
        p.step("vector", lambda e: e.tensor_mul(scr.ap(), qk3.ap(), cv), reads=("qk3", CV), writes=("scr",))
        p.step("vector", lambda e: e.tensor_sub(wb16.ap(), wtmp.ap(), scr.ap()), reads=("wtmp", "scr"), writes=("wb16",))
        ctx.dma("sync", wout_r[:, :, t0:t0 + TC], wb16.ap(), reads=("wb16",), writes=(f"wo{c}",))

    if standalone:
        p.step("gpsimd", lambda e: e.nop(), reads=tuple(f"wo{c}" for c in range(nch)), tag="final_wait")
        p.emit()
    return ctx


def build_l0a_spec(Ttot=S, ctx=None, io=None, pfx=""):
    standalone = ctx is None
    if standalone:
        ctx = Ctx("l0aspecX")
        io = dict(xi_fm=ctx.din("xi_fm", [D, S]), w_fm=ctx.din("w_fm", [D, S], BF16),
                  mfwd=ctx.din("mfwd", [D, 2048], BF16), minv=ctx.din("minv", [2048, D], BF16),
                  ffm=ctx.din("ffm", [2048, S]), aproj_w=ctx.din("aproj_w", [D, D]),
                  aprojbf=ctx.din("aprojbf", [128, KB]), xo_fm=ctx.dout("xo_fm", [D, S]))
    else:
        ctx.phase("S" + pfx)
    nc, p = ctx.nc, ctx.p
    TC = 512
    nch = Ttot // TC

    xi = io["xi_fm"]
    wfm = io["w_fm"]
    mfwd = io["mfwd"]
    minv = io["minv"]
    ffm = io["ffm"]
    apw = io["aproj_w"]
    apbf = io["aprojbf"]
    xo = io["xo_fm"]

    mfwdb = ctx.sb("mfwdb", [128, KB, 2048], BF16)
    minvb = ctx.sb("minvb", [128, 16, D], BF16)
    apb16 = ctx.sb("apb16", [128, KB, D], BF16)
    apb = ctx.sb("apb", [128, KB], F32)
    stage = ctx.sb("stage", [128, 1024], F32)
    ctx.dma("sync", apb[:, :], apbf, writes=("apb",))
    mfwd_r = mfwd.rearrange("(a p) g -> p a g", p=128)
    minv_r = minv.rearrange("(a p) f -> p a f", p=128)
    ctx.dma("sync", mfwdb.ap(), mfwd_r, writes=("mfwdb",))
    ctx.dma("sync", minvb.ap(), minv_r, writes=("minvb",))
    _load_scaled_weight(ctx, "apw", apw, D, D, stage, 1024, apb16, "apb16")

    wch = ctx.sb("wch", [128, KB, TC], BF16)
    fch = ctx.sb("fch", [128, 16, TC], F32)
    wfb = ctx.sb("wfb", [128, 16, TC], BF16)
    out0b = ctx.sb("out0b", [128, KB, TC], BF16)
    tA = ctx.sb("tA", [128, TC], F32)
    tB = ctx.sb("tB", [128, TC], F32)
    xich = ctx.sb("xich", [128, KB, TC], F32)

    xi_r = xi.rearrange("(a p) t -> p a t", p=128)
    w_r = wfm.rearrange("(a p) t -> p a t", p=128)
    f_r = ffm.rearrange("(a p) t -> p a t", p=128)
    xo_r = xo.rearrange("(a p) t -> p a t", p=128)

    for c in range(nch):
        t0 = c * TC
        ctx.dma("sync", wch.ap(), w_r[:, :, t0:t0 + TC], writes=("wch",))
        ctx.dma("sync", fch.ap(), f_r[:, :, t0:t0 + TC], writes=("fch",))
        ctx.dma("sync", xich.ap(), xi_r[:, :, t0:t0 + TC], writes=("xich",))
        for gp in range(8):
            bA, bAk = ctx.pb[0], "pb0"
            bB, bBk = ctx.pb[1], "pb1"
            ctx.mm_group(bA[:, 0:TC], bAk,
                         [(mfwdb[:, kc, gp * 128:(gp + 1) * 128], wch[:, kc, :], ("mfwdb", "wch")) for kc in range(KB)])
            ctx.mm_group(bB[:, 0:TC], bBk,
                         [(mfwdb[:, kc, 1024 + gp * 128:1024 + (gp + 1) * 128], wch[:, kc, :], ("mfwdb", "wch")) for kc in range(KB)])
            Fr = fch[:, gp, :]
            Fi = fch[:, 8 + gp, :]
            p.step("vector", lambda e, bA=bA, Fr=Fr: e.tensor_mul(tA.ap(), bA[:, 0:TC], Fr), reads=(bAk, "fch"), writes=("tA",))
            p.step("vector", lambda e, bB=bB, Fi=Fi: e.tensor_mul(tB.ap(), bB[:, 0:TC], Fi), reads=(bBk, "fch"), writes=("tB",))
            p.step("vector", lambda e, gp=gp: e.tensor_sub(wfb[:, gp, :], tA.ap(), tB.ap()), reads=("tA", "tB"), writes=("wfb",))
            p.step("vector", lambda e, bA=bA, Fi=Fi: e.tensor_mul(tA.ap(), bA[:, 0:TC], Fi), reads=(bAk, "fch"), writes=("tA",))
            p.step("vector", lambda e, bB=bB, Fr=Fr: e.tensor_mul(tB.ap(), bB[:, 0:TC], Fr), reads=(bBk, "fch"), writes=("tB",))
            p.step("vector", lambda e, gp=gp: e.tensor_add(wfb[:, 8 + gp, :], tA.ap(), tB.ap()), reads=("tA", "tB"), writes=("wfb",))
        for fb in range(KB):
            bank, bk = ctx.pb[2 + fb % 2], f"pb{2 + fb % 2}"
            ctx.mm_group(bank[:, 0:TC], bk,
                         [(minvb[:, gb, fb * 128:(fb + 1) * 128], wfb[:, gb, :], ("minvb", "wfb")) for gb in range(16)])
            p.step("scalar", lambda e, fb=fb, bank=bank: e.copy(out0b[:, fb, :], bank[:, 0:TC]), reads=(bk,), writes=("out0b",))
        for ob in range(KB):
            bank, bk = ctx.pb[ob % 2], f"pb{ob % 2}"
            ctx.mm_group(bank[:, 0:TC], bk,
                         [(apb16[:, fb, ob * 128:(ob + 1) * 128], out0b[:, fb, :], ("apb16", "out0b")) for fb in range(KB)])
            p.step("vector", lambda e, ob=ob, bank=bank: e.scalar_tensor_tensor(xich[:, ob, :], bank[:, 0:TC], apb[:, ob:ob + 1], xich[:, ob, :], ALU.add, ALU.add),
                   reads=(bk, "apb", "xich"), writes=("xich",))
        ctx.dma("sync", xo_r[:, :, t0:t0 + TC], xich.ap(), reads=("xich",), writes=(f"xo{c}",))

    if standalone:
        p.step("gpsimd", lambda e: e.nop(), reads=tuple(f"xo{c}" for c in range(nch)), tag="final_wait")
        p.emit()
    return ctx


def build_l0a_ffn(Ttot=S, ctx=None, io=None, pfx=""):
    standalone = ctx is None
    if standalone:
        ctx = Ctx("l0affnX")
        io = dict(xi_fm=ctx.din("xi_fm", [D, S]), fc_w=ctx.din("fc_w", [D, 4096]),
                  fcbf=ctx.din("fcbf", [128, 32]), ln2wf=ctx.din("ln2wf", [128, KB]),
                  ln2bf=ctx.din("ln2bf", [128, KB]), mproj_w=ctx.din("mproj_w", [4096, D]),
                  mprojbf=ctx.din("mprojbf", [128, KB]), xo_fm=ctx.dout("xo_fm", [D, S]))
    else:
        ctx.phase("M" + pfx)
    nc, p = ctx.nc, ctx.p
    TC = 256
    nch = Ttot // TC
    H = 4096
    HB = 32

    xi = io["xi_fm"]
    fcw = io["fc_w"]
    fcbf = io["fcbf"]
    lnwf = io["ln2wf"]
    lnbf = io["ln2bf"]
    mpw = io["mproj_w"]
    mpbf = io["mprojbf"]
    xo = io["xo_fm"]

    stage = ctx.sb("stage", [128, 1024], F32)
    fcb16 = ctx.sb("fcb16", [128, KB, H], BF16)
    mpb16 = ctx.sb("mpb16", [128, HB, D], BF16)
    lnw = ctx.sb("lnw", [128, KB], F32)
    lnb = ctx.sb("lnb", [128, KB], F32)
    lnbb = ctx.sb("lnbb", [128, KB], BF16)
    fcb = ctx.sb("fcb", [128, HB], F32)
    beff = ctx.sb("beff", [128, HB], F32)
    mpb = ctx.sb("mpb", [128, KB], F32)
    ctx.dma("sync", lnw[:, :], lnwf, writes=("lnwb",))
    ctx.dma("sync", lnb[:, :], lnbf, writes=("lnbq",))
    ctx.dma("sync", fcb[:, :], fcbf, writes=("biasf",))
    ctx.dma("sync", mpb[:, :], mpbf, writes=("mpb",))
    p.step("vector", lambda e: e.tensor_copy(lnbb[:, :], lnb[:, :]), reads=("lnbq",), writes=("lnbb",))
    _load_scaled_weight(ctx, "fcw", fcw, D, H, stage, 1024, fcb16, "fcb16",
                        lnwb=lnw, lnbb=lnbb, biasf=fcb, beff=beff, beffk="beff",
                        bank=ctx.pb[2], bankk="pb2")
    _load_scaled_weight(ctx, "mpw", mpw, H, D, stage, 1024, mpb16, "mpb16")

    xch = ctx.sb("xch", [128, KB, TC], F32)
    xb = ctx.sb("xb", [128, KB, TC], BF16)
    sq = ctx.sb("sq", [128, KB, TC], BF16)
    t1 = ctx.sb("t1", [128, KB, TC], F32)
    xnb = ctx.sb("xnb", [128, KB, TC], BF16)
    hb = ctx.sb("hb", [128, HB, TC], BF16)
    rows = ctx.sb("rows", [1, 4 * TC], F32)

    xi_r = xi.rearrange("(a p) t -> p a t", p=128)
    xo_r = xo.rearrange("(a p) t -> p a t", p=128)

    for c in range(nch):
        t0 = c * TC
        ctx.dma("sync", xch.ap(), xi_r[:, :, t0:t0 + TC], writes=("xch",))
        ctx.ln_fm(TC, KB, xch.ap(), "xch", xb.ap(), "xb", sq.ap(), "sq",
                  t1.ap(), "t1", xnb.ap(), "xnb", rows, "rows",
                  ctx.pb[6], "pb6", ctx.pb[7], "pb7")
        for mb in range(HB):
            bank, bk = ctx.pb[mb % 2], f"pb{mb % 2}"
            ctx.mm_group(bank[:, 0:TC], bk,
                         [(fcb16[:, kc, mb * 128:(mb + 1) * 128], xnb[:, kc, :], ("fcb16", "xnb")) for kc in range(KB)])
            p.step("scalar", lambda e, mb=mb, bank=bank: e.activation(hb[:, mb, :], bank[:, 0:TC], AF.Gelu, bias=beff[:, mb:mb + 1]),
                   reads=(bk, "beff"), writes=("hb",))
        for ob in range(KB):
            bank, bk = ctx.pb[2 + ob % 2], f"pb{2 + ob % 2}"
            ctx.mm_group(bank[:, 0:TC], bk,
                         [(mpb16[:, hc, ob * 128:(ob + 1) * 128], hb[:, hc, :], ("mpb16", "hb")) for hc in range(HB)])
            p.step("vector", lambda e, ob=ob, bank=bank: e.scalar_tensor_tensor(xch[:, ob, :], bank[:, 0:TC], mpb[:, ob:ob + 1], xch[:, ob, :], ALU.add, ALU.add),
                   reads=(bk, "mpb", "xch"), writes=("xch",))
        ctx.dma("sync", xo_r[:, :, t0:t0 + TC], xch.ap(), reads=("xch",), writes=(f"xo{c}",))

    if standalone:
        p.step("gpsimd", lambda e: e.nop(), reads=tuple(f"xo{c}" for c in range(nch)), tag="final_wait")
        p.emit()
    return ctx


H16, HD64 = 16, 64


def l0_consts():
    h = np.arange(H16)
    hd = np.arange(HD64)
    ph = np.exp(-2j * np.pi * np.outer(h, np.arange(H16)) / H16)
    pd = np.exp(-2j * np.pi * np.outer(hd, np.arange(HD64)) / HD64)
    M2D = (ph[:, None, :, None] * pd[None, :, None, :]).reshape(D, D)
    Minv = np.conj(M2D).T / D
    mfwd = np.concatenate([M2D.real, M2D.imag], axis=1).astype(bfloat16)
    minv_stack = np.concatenate([Minv.real, -Minv.imag], axis=0).astype(bfloat16)
    ks = np.fft.fftfreq(S).astype(np.float32)[:, None, None]
    khf = np.fft.fftfreq(H16).astype(np.float32)[None, :, None]
    kdf = np.fft.fftfreq(HD64).astype(np.float32)[None, None, :]
    km = np.sqrt(ks**2 + khf**2 + kdf**2 + 1e-10)
    F = np.exp(1j * np.arctan(np.log(km + 1e-10)))
    lam = (np.arange(S) / S).astype(np.float64)
    t = np.arange(S).astype(np.float64) * 0.1
    mod = np.sin(t + lam) * np.cos(t - 2.0 * lam + 0.5 * lam**2)
    msc = 1.0 + 0.1 * mod
    Fsc = F.reshape(S, D) * msc[:, None]
    ffm = np.ascontiguousarray(np.concatenate([Fsc.real.T, Fsc.imag.T], axis=0).astype(np.float32))
    return mfwd, minv_stack, ffm


def perm_qkv(attn_w, attn_b):
    w = attn_w.reshape(D, H16, 3, HD64)
    wq = w[:, :, 0, :].reshape(D, D)
    wk = w[:, :, 1, :].reshape(D, D)
    wv = w[:, :, 2, :].reshape(D, D)
    wp = np.concatenate([wq, wk, wv], axis=1)
    b = attn_b.reshape(H16, 3, HD64)
    bp = np.concatenate([b[:, 0, :].ravel(), b[:, 1, :].ravel(), b[:, 2, :].ravel()])
    return np.ascontiguousarray(wp), bp


def build_l1_attn(Ttot=S, debug=False, part="all", crange=None, ctx=None, io=None):
    standalone = ctx is None
    if not standalone:
        assert part == "all" and not debug
        ctx.phase("A1")
    else:
        ctx = Ctx("l1attnX")
    nc, p = ctx.nc, ctx.p
    TC = 128
    TCDBG = 128
    DBGC = 1
    nch = Ttot // TC
    NH, HD = 16, 64

    if standalone:
        io = dict(xin_fm=ctx.din("xin_fm", [D, Ttot]),
                  wq=ctx.din("wq", [D, D]), wk=ctx.din("wk", [D, D]),
                  wv=ctx.din("wv", [D, D]), wo=ctx.din("wo", [D, D]),
                  aq=ctx.din("aq", [D, 64]), bq=ctx.din("bq", [64, D]),
                  av=ctx.din("av", [D, 64]), bv=ctx.din("bv", [64, D]),
                  ao=ctx.din("ao", [D, 64]), bo=ctx.din("bo", [64, D]),
                  costab=ctx.din("costab", [Ttot, HD]), sintab=ctx.din("sintab", [Ttot, HD]),
                  cdmah=ctx.din("cdmah", [128, HD]), cdma1=ctx.din("cdma1", [128, HD]),
                  cdma2=ctx.din("cdma2", [128, HD]), meqt=ctx.din("meqt", [128, 32]),
                  mkeept=ctx.din("mkeept", [128, 32]), mfutnt=ctx.din("mfutnt", [128, 32]),
                  t4t=ctx.din("t4t", [128, 128], BF16), s4t=ctx.din("s4t", [128, 32], BF16),
                  e4t=ctx.din("e4t", [128, 4, 128], BF16),
                  ident_bf=ctx.din("ident_bf", [128, 128], BF16))
    xin = io["xin_fm"]
    wq = io["wq"]; wk = io["wk"]; wv = io["wv"]; wo = io["wo"]
    aq = io["aq"]; bq = io["bq"]; av = io["av"]; bv = io["bv"]
    ao = io["ao"]; bo = io["bo"]
    costab = io["costab"]; sintab = io["sintab"]
    cdmah = io["cdmah"]; cdma1 = io["cdma1"]; cdma2 = io["cdma2"]
    meqt = io["meqt"]; mkeept = io["mkeept"]; mfutnt = io["mfutnt"]
    t4t = io["t4t"]; s4t = io["s4t"]; e4t = io["e4t"]
    identb = io["ident_bf"]
    nch0 = Ttot // 128
    if part == "kv":
        fks_out = ctx.dout("fks_hbm", [128, KB * D], BF16)
        fvs_out = ctx.dout("fvs_hbm", [128, KB * D], BF16)
        fkfm_out = ctx.dout("fkfm_hbm", [128, KB * D], BF16)
        xout = None
    else:
        xout = io["xout_fm"] if not standalone else ctx.dout("xout_fm", [D, Ttot])
        if part != "all":
            fks_in = ctx.din("fks_hbm", [128, KB * D], BF16)
            fvs_in = ctx.din("fvs_hbm", [128, KB * D], BF16)
            fkfm_in = ctx.din("fkfm_hbm", [128, KB * D], BF16)
    dbg = {}
    if debug:
        for nm, shp, dt_ in [("d_kf", [128, D], F32), ("d_ksh", [128, D], BF16),
                             ("d_fKs", [128, KB * D], BF16), ("d_fKfm", [128, KB * D], BF16),
                             ("d_aqf", [128, D], F32), ("d_cs", [128, NH], F32),
                             ("d_wexp", [128, 1024], BF16), ("d_ssum", [128, NH], F32),
                             ("d_outk", [128, D], F32), ("d_t1", [128, KB * TCDBG], F32),
                             ("d_outk0", [128, D], F32), ("d_fve", [128, D], F32),
                             ("d_fVs", [128, KB * D], BF16),
                             ("d_cpv", [128, D], F32), ("d_wT", [128, KB * TCDBG], BF16),
                             ("d_vf", [128, D], F32), ("d_xnb", [128, KB * TCDBG], BF16)]:
            dbg[nm] = ctx.dout(nm, shp, dt_)

    stage = ctx.sb("stage", [128, 1024], F32)
    wqb = ctx.sb("wqb", [128, KB, D], BF16)
    wkb = ctx.sb("wkb", [128, KB, D], BF16)
    wvb = ctx.sb("wvb", [128, KB, D], BF16)
    wob = ctx.sb("wob", [128, KB, D], BF16)
    aqb_w = ctx.sb("aqb_w", [128, KB, 64], BF16)
    avb_w = ctx.sb("avb_w", [128, KB, 64], BF16)
    aob_w = ctx.sb("aob_w", [128, KB, 64], BF16)
    bqb_w = ctx.sb("bqb_w", [128, D], BF16)[0:64, :]
    bvb_w = ctx.sb("bvb_w", [128, D], BF16)[0:64, :]
    bob_w = ctx.sb("bob_w", [128, D], BF16)[0:64, :]
    for nm, hbmw, sb_ in [("wq", wq, wqb), ("wk", wk, wkb), ("wv", wv, wvb), ("wo", wo, wob)]:
        _load_scaled_weight(ctx, nm, hbmw, D, D, stage, 1024, sb_, nm + "b")
    for nm, hbmw, sb_ in [("aq", aq, aqb_w), ("av", av, avb_w), ("ao", ao, aob_w)]:
        for kc in range(KB):
            ctx.dma("sync", stage[:, 0:64], hbmw[kc * 128:(kc + 1) * 128, :], writes=("stage",))
            p.step("gpsimd", lambda e, kc=kc, sb_=sb_: e.tensor_copy(sb_[:, kc, :], stage[:, 0:64]), reads=("stage",), writes=(nm + "b",))
    for nm, hbmw, sb_ in [("bq", bq, bqb_w), ("bv", bv, bvb_w), ("bo", bo, bob_w)]:
        ctx.dma("sync", stage[0:64, 0:D], hbmw, writes=("stage",))
        p.step("gpsimd", lambda e, sb_=sb_: e.tensor_copy(sb_[:, :], stage[0:64, 0:D]), reads=("stage",), writes=(nm + "b",))

    cdh = ctx.sb("cdh", [128, HD], F32); ctx.dma("sync", cdh[:, :], cdmah, writes=("cdh",))
    cd1 = ctx.sb("cd1", [128, HD], F32); ctx.dma("sync", cd1[:, :], cdma1, writes=("cd1",))
    cd2 = ctx.sb("cd2", [128, HD], F32); ctx.dma("sync", cd2[:, :], cdma2, writes=("cd2",))
    meq = ctx.sb("meq", [128, 32], F32); ctx.dma("sync", meq[:, :], meqt, writes=("meq",))
    mkp = ctx.sb("mkp", [128, 32], F32); ctx.dma("sync", mkp[:, :], mkeept, writes=("mkp",))
    mfn = ctx.sb("mfn", [128, 32], F32); ctx.dma("sync", mfn[:, :], mfutnt, writes=("mfn",))
    t4 = ctx.sb("t4", [128, 128], BF16); ctx.dma("sync", t4[:, :], t4t, writes=("t4",))
    s4 = ctx.sb("s4", [128, 32], BF16); ctx.dma("sync", s4[:, :], s4t, writes=("s4",))
    e4 = ctx.sb("e4", [128, 4, 128], BF16); ctx.dma("sync", e4.ap(), e4t, writes=("e4",))
    idb = ctx.sb("idb", [128, 128], BF16); ctx.dma("sync", idb[:, :], identb, writes=("idb",))

    fKs = ctx.sb("fKs", [128, KB, D], BF16)
    fVs = ctx.sb("fVs", [128, KB, D], BF16)
    fKfm = ctx.sb("fKfm", [128, KB, D], BF16)
    if part == "kv":
        p.step("vector", lambda e: e.memset(fKs.ap(), 0.0), writes=("fKs",))
        p.step("vector", lambda e: e.memset(fVs.ap(), 0.0), writes=("fVs",))
        p.step("vector", lambda e: e.memset(fKfm.ap(), 0.0), writes=("fKfm",))

    xch = ctx.sb("xch", [128, KB, TC], F32)
    xb = ctx.sb("xb", [128, KB, TC], BF16)
    sq = ctx.sb("sq", [128, KB, TC], BF16)
    t1 = ctx.sb("t1", [128, KB, TC], F32)
    xnb = ctx.sb("xnb", [128, KB, TC], BF16)
    rows = ctx.sb("rows", [1, 4 * TC], F32)
    kf = ctx.sb("kf", [128, D], F32)
    vf = ctx.sb("vf", [128, D], F32)
    qf = ctx.sb("qf", [128, D], F32)
    s1 = ctx.sb("s1", [128, D], F32)
    s2 = ctx.sb("s2", [128, D], F32)
    s3 = ctx.sb("s3", [128, D], F32)
    coss = ctx.sb("coss", [128, HD], F32)
    sins = ctx.sb("sins", [128, HD], F32)
    kshb = ctx.sb("kshb", [128, D], BF16)
    vshb = ctx.sb("vshb", [128, D], BF16)
    aqf = ctx.sb("aqf", [128, D], F32)
    aqb16 = ctx.sb("aqb16", [128, D], BF16)
    aqfm = ctx.sb("aqfm", [128, KB, TC], BF16)
    avsml = ctx.sb("avsml", [128, TC], BF16)[0:64, :]
    wexp = ctx.sb("wexp", [128, 1024], BF16)
    sfix = ctx.sb("sfix", [128, 32], F32)
    cs_t = ctx.sb("cs_t", [128, NH], F32)
    ecs = ctx.sb("ecs", [128, NH], F32)
    ssum = ctx.sb("ssum", [128, NH], F32)
    rssum = ctx.sb("rssum", [128, NH], F32)
    wTb = ctx.sb("wTb", [128, KB, TC], BF16)
    fve = ctx.sb("fve", [128, D], F32)
    outk = ctx.sb("outk", [128, D], F32)
    outb16 = ctx.sb("outb16", [128, D], BF16)
    outfm = ctx.sb("outfm", [128, KB, TC], BF16)

    xin_r = xin.rearrange("(a p) t -> p a t", p=128)
    if xout is not None:
        xout_r = xout.rearrange("(a p) t -> p a t", p=128)

    def lnx(c):
        t0 = c * TC
        ctx.dma("sync", xch.ap(), xin_r[:, :, t0:t0 + TC], writes=("xch",))
        ctx.ln_fm(TC, KB, xch.ap(), "xch", xb.ap(), "xb", sq.ap(), "sq",
                  t1.ap(), "t1", xnb.ap(), "xnb", rows, "rows",
                  ctx.pb[0], "pb0", ctx.pb[1], "pb1")

    def proj_tok(dst_f32, dstk, wbt, wbtk, lora=None):
        for half in range(2):
            bank, bk = ctx.pb[half], f"pb{half}"
            pairs = [(xnb[:, kc, :], wbt[:, kc, half * 512:(half + 1) * 512], ("xnb", wbtk)) for kc in range(KB)]
            if lora is not None:
                ab16, abk, bb, bbk = lora
                pairs.append((ab16[:, :], bb[:, half * 512:(half + 1) * 512], (abk, bbk)))
            ctx.mm_group(bank[:, :], bk, pairs)
            p.step("scalar", lambda e, bank=bank, half=half: e.copy(dst_f32[:, half * 512:(half + 1) * 512], bank[:, :]),
                   reads=(f"pb{half}",), writes=(dstk,))

    def lora_a(awt, awtk, dstb, dstbk):
        bank, bk = ctx.pb[6], "pb6"
        ctx.mm_group(bank[0:64, 0:TC], bk,
                     [(awt[:, kc, :], xnb[:, kc, :], (awtk, "xnb")) for kc in range(KB)])
        p.step("scalar", lambda e: e.copy(dstb[:, :], bank[0:64, 0:TC]), reads=(bk,), writes=(dstbk,))

    def rope(dst, dstk, src, srck):
        s13 = s1.ap().rearrange("p (h d) -> p h d", d=HD)
        src3 = src.ap().rearrange("p (h d) -> p h d", d=HD)
        dst3 = dst.ap().rearrange("p (h d) -> p h d", d=HD)
        cosb = coss.ap().unsqueeze(1).broadcast_to([128, NH, HD])
        sinb = sins.ap().unsqueeze(1).broadcast_to([128, NH, HD])
        p.step("vector", lambda e: e.tensor_mul(s13[:, :, 0:32], src3[:, :, 32:64], sinb[:, :, 0:32]), reads=(srck, "sins"), writes=("s1",))
        p.step("vector", lambda e: e.tensor_mul(s13[:, :, 32:64], src3[:, :, 0:32], sinb[:, :, 32:64]), reads=(srck, "sins"), writes=("s1",))
        p.step("vector", lambda e: e.tensor_mul(s2.ap(), src.ap(), cosb.rearrange("p h d -> p (h d)") if False else cosb), reads=(srck, "coss"), writes=("s2",))
        p.step("vector", lambda e: e.tensor_sub(dst3[:, :, 0:32], s2.ap().rearrange("p (h d) -> p h d", d=HD)[:, :, 0:32], s13[:, :, 0:32]), reads=("s2", "s1"), writes=(dstk,))
        p.step("vector", lambda e: e.tensor_add(dst3[:, :, 32:64], s2.ap().rearrange("p (h d) -> p h d", d=HD)[:, :, 32:64], s13[:, :, 32:64]), reads=("s2", "s1"), writes=(dstk,))

    for c in (range(nch) if part in ("all", "kv") else ()):
        t0 = c * TC
        lnx(c)
        ctx.dma("sync", coss[:, :], costab[t0:t0 + TC, :], writes=("coss",))
        ctx.dma("sync", sins[:, :], sintab[t0:t0 + TC, :], writes=("sins",))
        lora_a(avb_w, "avb", avsml, "avsml")
        proj_tok(kf, "kf", wkb, "wkb")
        proj_tok(vf, "vf", wvb, "wvb", lora=(avsml, "avsml", bvb_w, "bvb"))
        rope(s3, "s3", kf, "kf")
        cdhb = cdh.ap().unsqueeze(1).broadcast_to([128, NH, HD])
        p.step("vector", lambda e: e.tensor_mul(kshb.ap().rearrange("p (h d) -> p h d", d=HD), s3.ap().rearrange("p (h d) -> p h d", d=HD), cdhb), reads=("s3", "cdh"), writes=("kshb",))
        p.step("vector", lambda e: e.tensor_mul(vshb.ap().rearrange("p (h d) -> p h d", d=HD), vf.ap().rearrange("p (h d) -> p h d", d=HD), cdhb), reads=("vf", "cdh"), writes=("vshb",))
        j = c % 4
        for half in range(2):
            kb_, kbk = ctx.pb[2 + half], f"pb{2 + half}"
            vb_, vbk = ctx.pb[0 + half] if False else ctx.pb[4 + half], f"pb{4 + half}"
            p.step("tensor", lambda e, kb_=kb_, half=half, j=j: e.matmul(kb_[32 * j:32 * (j + 1), :], s4[:, :], kshb[:, half * 512:(half + 1) * 512], start=True, stop=True, tile_position=(0, 32 * j)),
                   reads=("s4", "kshb"), writes=(kbk,))
            p.step("tensor", lambda e, vb_=vb_, half=half, j=j: e.matmul(vb_[32 * j:32 * (j + 1), :], s4[:, :], vshb[:, half * 512:(half + 1) * 512], start=True, stop=True, tile_position=(0, 32 * j)),
                   reads=("s4", "vshb"), writes=(vbk,))
        if debug and c == DBGC:
            ctx.dma("sync", dbg["d_kf"], kf.ap(), reads=("kf",), writes=("dkf",))
            ctx.dma("sync", dbg["d_ksh"], kshb.ap(), reads=("kshb",), writes=("dksh",))
            ctx.dma("sync", dbg["d_vf"], vf.ap(), reads=("vf",), writes=("dvf",))
        if j == 3:
            blk = c // 4
            for half in range(2):
                p.step("vector", lambda e, blk=blk, half=half: e.tensor_copy(fKs[:, blk, half * 512:(half + 1) * 512], ctx.pb[2 + half][:, :]), reads=(f"pb{2 + half}",), writes=("fKs",))
                p.step("vector", lambda e, blk=blk, half=half: e.tensor_copy(fVs[:, blk, half * 512:(half + 1) * 512], ctx.pb[4 + half][:, :]), reads=(f"pb{4 + half}",), writes=("fVs",))
    for sb_ in (range(KB) if part in ("all", "kv") else ()):
        for fb in range(KB):
            p.step("tensor", lambda e, sb_=sb_, fb=fb: e.transpose(ctx.pb[2][:, :].bitcast(BF16)[:, 0:128], fKs[:, sb_, fb * 128:(fb + 1) * 128], idb[:, :]),
                   reads=("fKs", "idb"), writes=("pb2",))
            p.step("vector", lambda e, sb_=sb_, fb=fb: e.tensor_copy(fKfm[:, fb, sb_ * 128:(sb_ + 1) * 128], ctx.pb[2][:, :].bitcast(BF16)[:, 0:128]),
                   reads=("pb2",), writes=("fKfm",))

    if debug:
        ctx.dma("sync", dbg["d_fKs"], fKs.ap().rearrange("p a t -> p (a t)"), reads=("fKs",), writes=("dfks",))
        ctx.dma("sync", dbg["d_fKfm"], fKfm.ap().rearrange("p a t -> p (a t)"), reads=("fKfm",), writes=("dfkfm",))
        ctx.dma("sync", dbg["d_fVs"], fVs.ap().rearrange("p a t -> p (a t)"), reads=("fVs",), writes=("dfvs",))

    if part == "kv":
        ctx.dma("sync", fks_out, fKs.ap().rearrange("p a t -> p (a t)"), reads=("fKs",), writes=("fko",))
        ctx.dma("sync", fvs_out, fVs.ap().rearrange("p a t -> p (a t)"), reads=("fVs",), writes=("fvo",))
        ctx.dma("sync", fkfm_out, fKfm.ap().rearrange("p a t -> p (a t)"), reads=("fKfm",), writes=("fkfo",))
        p.step("gpsimd", lambda e: e.nop(), reads=("fko", "fvo", "fkfo"), tag="final_wait")
        p.emit()
        return ctx
    if part != "all":
        ctx.dma("sync", fKs.ap().rearrange("p a t -> p (a t)"), fks_in, writes=("fKs",))
        ctx.dma("sync", fVs.ap().rearrange("p a t -> p (a t)"), fvs_in, writes=("fVs",))
        ctx.dma("sync", fKfm.ap().rearrange("p a t -> p (a t)"), fkfm_in, writes=("fKfm",))

    qrange = range(nch) if crange is None else range(*crange)
    for c in qrange:
        t0 = c * TC
        nslots = 32 * (c + 1)
        nsb = (nslots + 127) // 128
        lnx(c)
        ctx.dma("sync", coss[:, :], costab[t0:t0 + TC, :], writes=("coss",))
        ctx.dma("sync", sins[:, :], sintab[t0:t0 + TC, :], writes=("sins",))
        lora_a(aqb_w, "aqb", avsml, "avsml")
        proj_tok(qf, "qf", wqb, "wqb", lora=(avsml, "avsml", bqb_w, "bqb"))
        rope(s3, "s3", qf, "qf")
        cd1b = cd1.ap().unsqueeze(1).broadcast_to([128, NH, HD])
        p.step("vector", lambda e: e.tensor_mul(aqf.ap().rearrange("p (h d) -> p h d", d=HD), s3.ap().rearrange("p (h d) -> p h d", d=HD), cd1b), reads=("s3", "cd1"), writes=("aqf",))
        p.step("vector", lambda e: e.tensor_copy(aqb16.ap(), aqf.ap()), reads=("aqf",), writes=("aqb16",))
        for fb in range(KB):
            p.step("tensor", lambda e, fb=fb: e.transpose(ctx.pb[2][:, :].bitcast(BF16)[:, 0:128], aqb16[:, fb * 128:(fb + 1) * 128], idb[:, :]),
                   reads=("aqb16", "idb"), writes=("pb2",))
            p.step("vector", lambda e, fb=fb: e.tensor_copy(aqfm[:, fb, :], ctx.pb[2][:, :].bitcast(BF16)[:, 0:128]),
                   reads=("pb2",), writes=("aqfm",))
        lora_a(avb_w, "avb", avsml, "avsml")
        proj_tok(kf, "kf", wkb, "wkb")
        proj_tok(vf, "vf", wvb, "wvb", lora=(avsml, "avsml", bvb_w, "bvb"))
        rope(s3, "s3", kf, "kf")
        cdhb = cdh.ap().unsqueeze(1).broadcast_to([128, NH, HD])
        p.step("vector", lambda e: e.tensor_mul(kshb.ap().rearrange("p (h d) -> p h d", d=HD), s3.ap().rearrange("p (h d) -> p h d", d=HD), cdhb), reads=("s3", "cdh"), writes=("kshb",))
        p.step("vector", lambda e: e.tensor_mul(vshb.ap().rearrange("p (h d) -> p h d", d=HD), vf.ap().rearrange("p (h d) -> p h d", d=HD), cdhb), reads=("vf", "cdh"), writes=("vshb",))
        for half in range(2):
            p.step("tensor", lambda e, half=half: e.matmul(ctx.pb[4 + half][:, :], t4[:, :], kshb[:, half * 512:(half + 1) * 512], start=True, stop=True), reads=("t4", "kshb"), writes=(f"pb{4 + half}",))
            p.step("tensor", lambda e, half=half: e.matmul(ctx.pb[6 + half][:, :], t4[:, :], vshb[:, half * 512:(half + 1) * 512], start=True, stop=True), reads=("t4", "vshb"), writes=(f"pb{6 + half}",))
            p.step("tensor", lambda e, half=half, c=c: e.matmul(ctx.pb[2][:, :], e4[:, c % 4, :], fVs[:, c // 4, half * 512:(half + 1) * 512], start=True, stop=True),
                   reads=("e4", "fVs"), writes=("pb2",))
            p.step("scalar", lambda e, half=half: e.copy(fve[:, half * 512:(half + 1) * 512], ctx.pb[2][:, :]), reads=("pb2",), writes=("fve",))
        cpk3 = None
        p.step("vector", lambda e: e.tensor_mul(s1.ap()[:, 0:512], aqf[:, 0:512], ctx.pb[4][:, :]), reads=("aqf", "pb4"), writes=("s1",))
        p.step("vector", lambda e: e.tensor_mul(s1.ap()[:, 512:1024], aqf[:, 512:1024], ctx.pb[5][:, :]), reads=("aqf", "pb5"), writes=("s1",))
        p.step("vector", lambda e: e.tensor_reduce(cs_t.ap(), s1.ap().rearrange("p (h d) -> p h d", d=HD), mybir.AxisListType.X, ALU.add), reads=("s1",), writes=("cs_t",))
        p.step("scalar", lambda e: e.activation(ecs.ap(), cs_t.ap(), AF.Exp), reads=("cs_t",), writes=("ecs",))
        for h in range(NH):
            hb_, ho = h // 2, (h % 2) * 64
            aqs = aqfm[ho:ho + 64, hb_, :]
            for piece in range((nslots + 511) // 512):
                p0 = piece * 512
                pw = min(512, nslots - p0)
                bank, bk = ctx.pb[piece], f"pb{piece}"
                ctx.mm_group(bank[:, 0:pw], bk,
                             [(aqs, fKfm[ho:ho + 64, hb_, p0:p0 + pw], ("aqfm", "fKfm"))])
            lb, lo = (32 * c) // 512, (32 * c) % 512
            lbank, lbk = ctx.pb[lb], f"pb{lb}"
            p.step("vector", lambda e, lbank=lbank, lo=lo: e.tensor_mul(sfix.ap(), lbank[:, lo:lo + 32], mkp[:, :]), reads=(lbk, "mkp"), writes=("sfix",))
            p.step("vector", lambda e, h=h: e.scalar_tensor_tensor(s2.ap()[:, 0:32], meq[:, :], cs_t[:, h:h + 1], sfix.ap(), ALU.mult, ALU.add), reads=("meq", "cs_t", "sfix"), writes=("s2",))
            p.step("vector", lambda e: e.tensor_add(sfix.ap(), s2.ap()[:, 0:32], mfn[:, :]), reads=("s2", "mfn"), writes=("sfix",))
            for piece in range((nslots + 511) // 512):
                p0 = piece * 512
                pw = min(512, nslots - p0)
                bank, bk = ctx.pb[piece], f"pb{piece}"
                if p0 + pw > 32 * c:
                    pw2 = 32 * c - p0
                    if pw2 > 0:
                        p.step("scalar", lambda e, bank=bank, p0=p0, pw2=pw2: e.activation(wexp[:, p0:p0 + pw2], bank[:, 0:pw2], AF.Exp), reads=(bk,), writes=("wexp",))
                else:
                    p.step("scalar", lambda e, bank=bank, p0=p0, pw=pw: e.activation(wexp[:, p0:p0 + pw], bank[:, 0:pw], AF.Exp), reads=(bk,), writes=("wexp",))
            p.step("scalar", lambda e, c=c: e.activation(wexp[:, 32 * c:32 * c + 32], sfix.ap(), AF.Exp), reads=("sfix",), writes=("wexp",))
            p.step("vector", lambda e, h=h, ns=nslots: e.tensor_reduce(ssum[:, h:h + 1], wexp.ap()[:, 0:ns].unsqueeze(1), mybir.AxisListType.X, ALU.add), reads=("wexp",), writes=("ssum",))
            for sb_ in range(nsb):
                swidth = min(128, nslots - sb_ * 128)
                p.step("tensor", lambda e, sb_=sb_, sw=swidth: e.transpose(ctx.pb[2][:, :].bitcast(BF16)[0:sw, 0:128], wexp[:, sb_ * 128:sb_ * 128 + sw], idb[:, :]),
                       reads=("wexp", "idb"), writes=("pb2",))
                p.step("vector", lambda e, sb_=sb_, sw=swidth: e.tensor_copy(wTb[0:sw, sb_, :], ctx.pb[2][:, :].bitcast(BF16)[0:sw, 0:128]),
                       reads=("pb2",), writes=("wTb",))
            bank3, bk3 = ctx.pb[3], "pb3"
            ctx.mm_group(bank3[:, h * 64 - (h // 8) * 512:h * 64 - (h // 8) * 512 + 64] if False else bank3[:, (h % 8) * 64:(h % 8) * 64 + 64], bk3,
                         [(wTb[0:min(128, nslots - sb_ * 128), sb_, :], fVs[0:min(128, nslots - sb_ * 128), sb_, h * 64:(h + 1) * 64], ("wTb", "fVs")) for sb_ in range(nsb)])
            if h % 8 == 7:
                g = h // 8
                p.step("scalar", lambda e, g=g: e.copy(outk[:, g * 512:(g + 1) * 512], bank3[:, :]), reads=(bk3,), writes=("outk",))
        if debug and c == DBGC:
            ctx.dma("sync", dbg["d_outk0"], outk.ap(), reads=("outk",), writes=("doutk0",))
            ctx.dma("sync", dbg["d_fve"], fve.ap(), reads=("fve",), writes=("dfve",))
            p.step("scalar", lambda e: e.copy(s2.ap()[:, 0:512], ctx.pb[6][:, :]), reads=("pb6",), writes=("s2",))
            p.step("scalar", lambda e: e.copy(s2.ap()[:, 512:1024], ctx.pb[7][:, :]), reads=("pb7",), writes=("s2",))
            ctx.dma("sync", dbg["d_cpv"], s2.ap(), reads=("s2",), writes=("dcpv",))
            ctx.dma("sync", dbg["d_wT"], wTb.ap().rearrange("p a t -> p (a t)"), reads=("wTb",), writes=("dwT",))
            ctx.dma("sync", dbg["d_aqf"], aqf.ap(), reads=("aqf",), writes=("daqf",))
            ctx.dma("sync", dbg["d_cs"], cs_t.ap(), reads=("cs_t",), writes=("dcs",))
            ctx.dma("sync", dbg["d_wexp"], wexp.ap(), reads=("wexp",), writes=("dwexp",))
            ctx.dma("sync", dbg["d_ssum"], ssum.ap(), reads=("ssum",), writes=("dssum",))
            ctx.dma("sync", dbg["d_xnb"], xnb.ap().rearrange("p a t -> p (a t)"), reads=("xnb",), writes=("dxnb",))
        p.step("vector", lambda e: e.tensor_sub(s1.ap()[:, 0:512], ctx.pb[6][:, :], fve[:, 0:512]), reads=("pb6", "fve"), writes=("s1",))
        p.step("vector", lambda e: e.tensor_sub(s1.ap()[:, 512:1024], ctx.pb[7][:, :], fve[:, 512:1024]), reads=("pb7", "fve"), writes=("s1",))
        ecs3 = ecs.ap().unsqueeze(2).broadcast_to([128, NH, HD])
        p.step("vector", lambda e: e.tensor_mul(s1.ap().rearrange("p (h d) -> p h d", d=HD), s1.ap().rearrange("p (h d) -> p h d", d=HD), ecs3), reads=("s1", "ecs"), writes=("s1",))
        p.step("vector", lambda e: e.tensor_add(outk.ap(), outk.ap(), s1.ap()), reads=("outk", "s1"), writes=("outk",))
        p.step("vector", lambda e: e.reciprocal(rssum.ap(), ssum.ap()), reads=("ssum",), writes=("rssum",))
        rs3 = rssum.ap().unsqueeze(2).broadcast_to([128, NH, HD])
        p.step("vector", lambda e: e.tensor_mul(outk.ap().rearrange("p (h d) -> p h d", d=HD), outk.ap().rearrange("p (h d) -> p h d", d=HD), rs3), reads=("outk", "rssum"), writes=("outk",))
        cd2b = cd2.ap().unsqueeze(1).broadcast_to([128, NH, HD])
        p.step("vector", lambda e: e.tensor_mul(outb16.ap().rearrange("p (h d) -> p h d", d=HD), outk.ap().rearrange("p (h d) -> p h d", d=HD), cd2b), reads=("outk", "cd2"), writes=("outb16",))
        for fb in range(KB):
            p.step("tensor", lambda e, fb=fb: e.transpose(ctx.pb[2][:, :].bitcast(BF16)[:, 0:128], outb16[:, fb * 128:(fb + 1) * 128], idb[:, :]),
                   reads=("outb16", "idb"), writes=("pb2",))
            p.step("vector", lambda e, fb=fb: e.tensor_copy(outfm[:, fb, :], ctx.pb[2][:, :].bitcast(BF16)[:, 0:128]),
                   reads=("pb2",), writes=("outfm",))
        bank, bk = ctx.pb[3], "pb3"
        ctx.mm_group(bank[0:64, 0:TC], bk,
                     [(aob_w[:, kc, :], outfm[:, kc, :], ("aob", "outfm")) for kc in range(KB)])
        p.step("scalar", lambda e, bank=bank: e.copy(avsml[:, :], bank[0:64, 0:TC]), reads=(bk,), writes=("avsml",))
        for ob in range(KB):
            bank, bk = ctx.pb[ob % 2], f"pb{ob % 2}"
            pairs = [(wob[:, kc, ob * 128:(ob + 1) * 128], outfm[:, kc, :], ("wob", "outfm")) for kc in range(KB)]
            pairs.append((bob_w[:, ob * 128:(ob + 1) * 128], avsml[:, :], ("bob", "avsml")))
            ctx.mm_group(bank[:, 0:TC], bk, pairs)
            p.step("scalar", lambda e, ob=ob, bank=bank: e.copy(t1[:, ob, :], bank[:, 0:TC]), reads=(bk,), writes=("t1",))
        if debug and c == DBGC:
            ctx.dma("sync", dbg["d_outk"], outk.ap(), reads=("outk",), writes=("doutk",))
            ctx.dma("sync", dbg["d_t1"], t1.ap().rearrange("p a t -> p (a t)"), reads=("t1",), writes=("dt1",))
        ctx.ln_fm(TC, KB, t1.ap(), "t1", xb.ap(), "xb", sq.ap(), "sq",
                  s1.ap().rearrange("p (a t) -> p a t", t=TC), "s1", t1.ap(), "t1", rows, "rows",
                  ctx.pb[4], "pb4", ctx.pb[5], "pb5")
        p.step("vector", lambda e: e.tensor_add(xch.ap(), xch.ap(), t1.ap()), reads=("xch", "t1"), writes=("xch",))
        ctx.dma("sync", xout_r[:, :, t0:t0 + TC], xch.ap(), reads=("xch",), writes=(f"xo{c}",))

    if standalone:
        p.step("gpsimd", lambda e: e.nop(), reads=tuple(f"xo{c}" for c in qrange), tag="final_wait")
        p.emit()
    return ctx


def l1_attn_consts():
    HAD = np.array([[1, 1, 1, 1], [1, -1, 1, -1], [1, 1, -1, -1], [1, -1, -1, 1]], np.float32)
    CDMA = np.tile(HAD, (1, 16))
    rowpat = CDMA[np.arange(128) % 4]
    inv_freq = 1.0 / (10000.0 ** (np.arange(0, 64, 2, dtype=np.float32) / 64))
    pos = np.arange(S, dtype=np.float32)
    freqs = pos[:, None] * inv_freq[None, :]
    emb = np.concatenate([freqs, freqs], axis=1)
    costab = np.cos(emb).astype(np.float32)
    sintab = np.sin(emb).astype(np.float32)
    pidx = np.arange(128)
    slot = pidx // 4
    csl = np.arange(32)
    meq = (csl[None, :] == slot[:, None]).astype(np.float32)
    mfut = (csl[None, :] > slot[:, None]).astype(np.float32)
    mkeep = 1.0 - meq - mfut
    mfutneg = (-1e30 * mfut).astype(np.float32)
    kin = np.arange(128)
    t4 = ((kin[:, None] // 4 == kin[None, :] // 4) & (kin[:, None] <= kin[None, :])).astype(bfloat16)
    s4 = (kin[:, None] // 4 == csl[None, :]).astype(bfloat16)
    e4 = np.zeros((128, 4, 128), np.float32)
    for j in range(4):
        e4[:, j, :] = (kin[:, None] == 32 * j + kin[None, :] // 4)
    e4 = e4.astype(bfloat16)
    return dict(
        costab=costab, sintab=sintab,
        cdmah=np.ascontiguousarray(rowpat / 2.0),
        cdma1=np.ascontiguousarray(rowpat / 8.0),
        cdma2=np.ascontiguousarray(rowpat * 2.0),
        meqt=meq, mkeept=mkeep, mfutnt=mfutneg,
        t4t=t4, s4t=s4, e4t=e4,
        ident_bf=np.eye(128, dtype=bfloat16),
    )


def l1_attn_inmap(x_fm, inp):
    m = dict(
        xin_fm=np.ascontiguousarray(x_fm),
        wq=np.ascontiguousarray(inp["L1_attn_Wq"]), wk=np.ascontiguousarray(inp["L1_attn_Wk"]),
        wv=np.ascontiguousarray(inp["L1_attn_Wv"]), wo=np.ascontiguousarray(inp["L1_attn_Wo"]),
        aq=np.ascontiguousarray(inp["L1_attn_Aq"]), bq=np.ascontiguousarray(inp["L1_attn_Bq"]),
        av=np.ascontiguousarray(inp["L1_attn_Av"]), bv=np.ascontiguousarray(inp["L1_attn_Bv"]),
        ao=np.ascontiguousarray(inp["L1_attn_Ao"]), bo=np.ascontiguousarray(inp["L1_attn_Bo"]),
    )
    m.update(l1_attn_consts())
    return m


def build_fused(nphases=99, x2_external=False):
    PH = [0]
    def tick():
        PH[0] += 1
        return PH[0] <= nphases
    ctx = Ctx("mega")
    ctx.enable_arena(base_bytes=20480, top_bytes=229344)
    nc = ctx.nc
    xin = ctx.din("x_fm", [D, S])
    out = ctx.dout("out_fm", [D, S])
    xi = ctx.dscratch("sc_xi", [D, S])
    wsc = ctx.dscratch("sc_w", [D, S], BF16)
    x1 = ctx.dscratch("sc_x1", [D, S])
    x2 = ctx.dscratch("sc_x2", [D, S])
    x3 = ctx.dscratch("sc_x3", [D, S])
    mfwd = ctx.din("mfwd", [D, 2048], BF16)
    minv = ctx.din("minv", [2048, D], BF16)
    ffm = ctx.din("ffm", [2048, S])
    identb = ctx.din("ident_bf", [128, 128], BF16)

    if tick():
        build_ln_pre(ctx=ctx, io=dict(xin_fm=xin, xi_fm=xi))
    for li in range(4):
        pf = f"l{li}"
        ioq = dict(
            xi_fm=xi, w_fm=wsc,
            attn_w=ctx.din(f"{pf}_attn_w", [D, 3072]),
            attn_bf=ctx.din(f"{pf}_attn_bf", [128, 24]),
            ln1wf=ctx.din(f"{pf}_ln1wf", [128, KB]),
            ln1bf=ctx.din(f"{pf}_ln1bf", [128, KB]),
        )
        if tick():
            build_l0a_qkv(ctx=ctx, io=ioq, pfx=pf)
        ios = dict(
            xi_fm=xi, w_fm=wsc, xo_fm=xi, mfwd=mfwd, minv=minv, ffm=ffm,
            aproj_w=ctx.din(f"{pf}_aproj_w", [D, D]),
            aprojbf=ctx.din(f"{pf}_aprojbf", [128, KB]),
        )
        if tick():
            build_l0a_spec(ctx=ctx, io=ios, pfx=pf)
        iof = dict(
            xi_fm=xi, xo_fm=xi,
            fc_w=ctx.din(f"{pf}_fc_w", [D, 4096]),
            fcbf=ctx.din(f"{pf}_fcbf", [128, 32]),
            ln2wf=ctx.din(f"{pf}_ln2wf", [128, KB]),
            ln2bf=ctx.din(f"{pf}_ln2bf", [128, KB]),
            mproj_w=ctx.din(f"{pf}_mproj_w", [4096, D]),
            mprojbf=ctx.din(f"{pf}_mprojbf", [128, KB]),
        )
        if tick():
            build_l0a_ffn(ctx=ctx, io=iof, pfx=pf)
    if tick():
        build_addln(ctx=ctx, io=dict(xa_fm=xin, xb_fm=xi, xout_fm=x1))
    if tick():
        build_l0_ffn(ctx=ctx, io=dict(
            xin_fm=x1, xout_fm=(out if x2_external else x2),
            w1t=ctx.din("f0_w1t", [D, 4096]), w2t=ctx.din("f0_w2t", [4096, D]),
            b1f=ctx.din("f0_b1f", [128, 32]), b2f=ctx.din("f0_b2f", [128, KB]),
            scale1=ctx.din("f0_scale1", [1, 1]), ident_bf=identb))
    if tick():
        build_l1_attn(ctx=ctx, io=dict(
            xin_fm=x2, xout_fm=x3,
            wq=ctx.din("a1_wq", [D, D]), wk=ctx.din("a1_wk", [D, D]),
            wv=ctx.din("a1_wv", [D, D]), wo=ctx.din("a1_wo", [D, D]),
            aq=ctx.din("a1_aq", [D, 64]), bq=ctx.din("a1_bq", [64, D]),
            av=ctx.din("a1_av", [D, 64]), bv=ctx.din("a1_bv", [64, D]),
            ao=ctx.din("a1_ao", [D, 64]), bo=ctx.din("a1_bo", [64, D]),
            costab=ctx.din("a1_costab", [S, 64]), sintab=ctx.din("a1_sintab", [S, 64]),
            cdmah=ctx.din("a1_cdmah", [128, 64]), cdma1=ctx.din("a1_cdma1", [128, 64]),
            cdma2=ctx.din("a1_cdma2", [128, 64]), meqt=ctx.din("a1_meqt", [128, 32]),
            mkeept=ctx.din("a1_mkeept", [128, 32]), mfutnt=ctx.din("a1_mfutnt", [128, 32]),
            t4t=ctx.din("a1_t4t", [128, 128], BF16), s4t=ctx.din("a1_s4t", [128, 32], BF16),
            e4t=ctx.din("a1_e4t", [128, 4, 128], BF16), ident_bf=identb))
    if tick():
        build_l1_ffn(ctx=ctx, io=dict(
            xin_fm=x3, xout_fm=out,
            w1=ctx.din("f1_w1", [D, 4096]), w2=ctx.din("f1_w2", [4096, D]),
            b1f=ctx.din("f1_b1f", [128, 32]), b2f=ctx.din("f1_b2f", [128, KB])))
    ctx.p.barrier()
    ctx.p.emit()
    return ctx


def fused_inmap(inputs, b, l0c, l1c):
    mfwd, minv_stack, ffm = l0c
    m = dict(
        x_fm=np.ascontiguousarray(inputs["L0_attn_x"][b].T),
        mfwd=mfwd, minv=minv_stack, ffm=ffm,
        ident_bf=np.eye(128, dtype=bfloat16),
    )
    for li in range(4):
        pf = f"l{li}"
        awp, abp = perm_qkv(inputs["L0_attn_attn_w"][li], inputs["L0_attn_attn_b"][li])
        m[f"{pf}_attn_w"] = awp
        m[f"{pf}_attn_bf"] = fmvec(abp)
        m[f"{pf}_ln1wf"] = fmvec(inputs["L0_attn_ln1_w"][li])
        m[f"{pf}_ln1bf"] = fmvec(inputs["L0_attn_ln1_b"][li])
        m[f"{pf}_aproj_w"] = np.ascontiguousarray(inputs["L0_attn_aproj_w"][li])
        m[f"{pf}_aprojbf"] = fmvec(inputs["L0_attn_aproj_b"][li])
        m[f"{pf}_fc_w"] = np.ascontiguousarray(inputs["L0_attn_fc_w"][li])
        m[f"{pf}_fcbf"] = fmvec(inputs["L0_attn_fc_b"][li])
        m[f"{pf}_ln2wf"] = fmvec(inputs["L0_attn_ln2_w"][li])
        m[f"{pf}_ln2bf"] = fmvec(inputs["L0_attn_ln2_b"][li])
        m[f"{pf}_mproj_w"] = np.ascontiguousarray(inputs["L0_attn_mproj_w"][li])
        m[f"{pf}_mprojbf"] = fmvec(inputs["L0_attn_mproj_b"][li])
    m["f0_w1t"] = np.ascontiguousarray(inputs["L0_ffn_W1"].T)
    m["f0_w2t"] = np.ascontiguousarray(inputs["L0_ffn_W2"].T)
    m["f0_b1f"] = fmvec(inputs["L0_ffn_b1"])
    m["f0_b2f"] = fmvec(inputs["L0_ffn_b2"])
    m["f0_scale1"] = np.asarray(inputs["L0_ffn_scale"], np.float32).reshape(1, 1)
    for k_, nm in [("wq", "Wq"), ("wk", "Wk"), ("wv", "Wv"), ("wo", "Wo"),
                   ("aq", "Aq"), ("bq", "Bq"), ("av", "Av"), ("bv", "Bv"),
                   ("ao", "Ao"), ("bo", "Bo")]:
        m[f"a1_{k_}"] = np.ascontiguousarray(inputs[f"L1_attn_{nm}"])
    for k_, v in l1c.items():
        if k_ == "ident_bf":
            continue
        m[f"a1_{k_[:-1] if False else k_}"] = v
    m["f1_w1"] = np.ascontiguousarray(inputs["L1_ffn_w1"])
    m["f1_w2"] = np.ascontiguousarray(inputs["L1_ffn_w2"])
    m["f1_b1f"] = fmvec(inputs["L1_ffn_b1"])
    m["f1_b2f"] = fmvec(inputs["L1_ffn_b2"])
    return m


_PROGS = {}


def _progs():
    if not _PROGS:
        _PROGS["pre"] = build_ln_pre()
        _PROGS["qkv"] = build_l0a_qkv()
        _PROGS["spec"] = build_l0a_spec()
        _PROGS["l0affn"] = build_l0a_ffn()
        _PROGS["post"] = build_addln()
        _PROGS["l0ffn"] = build_l0_ffn()
        _PROGS["l1attn"] = build_l1_attn()
        _PROGS["l1ffn"] = build_l1_ffn()
        _PROGS["l0c"] = l0_consts()
        _PROGS["l1c"] = l1_attn_consts()
    return _PROGS


def _run(ctx, in_maps):
    res = run_bass_kernel_spmd(ctx.nc, in_maps, core_ids=list(range(len(in_maps))))
    return res.results


def kernel(**inputs):
    if "fused15" not in _PROGS:
        _PROGS["fused15"] = build_fused(nphases=15, x2_external=True)
        _PROGS["l1attn"] = build_l1_attn()
        _PROGS["l1ffn"] = build_l1_ffn()
        _PROGS["l0c"] = l0_consts()
        _PROGS["l1c"] = l1_attn_consts()
    ctx = _PROGS["fused15"]
    l1c = _PROGS["l1c"]
    B = inputs["L0_attn_x"].shape[0]
    in_maps = []
    for b in range(B):
        m = fused_inmap(inputs, b, _PROGS["l0c"], l1c)
        for k_ in list(m.keys()):
            if k_.startswith("a1_") or k_.startswith("f1_"):
                del m[k_]
        in_maps.append(m)
    res = run_bass_kernel_spmd(ctx.nc, in_maps, core_ids=list(range(B)))
    x2 = [res.results[b]["out_fm"] for b in range(B)]
    l1w = dict(
        wq=np.ascontiguousarray(inputs["L1_attn_Wq"]), wk=np.ascontiguousarray(inputs["L1_attn_Wk"]),
        wv=np.ascontiguousarray(inputs["L1_attn_Wv"]), wo=np.ascontiguousarray(inputs["L1_attn_Wo"]),
        aq=np.ascontiguousarray(inputs["L1_attn_Aq"]), bq=np.ascontiguousarray(inputs["L1_attn_Bq"]),
        av=np.ascontiguousarray(inputs["L1_attn_Av"]), bv=np.ascontiguousarray(inputs["L1_attn_Bv"]),
        ao=np.ascontiguousarray(inputs["L1_attn_Ao"]), bo=np.ascontiguousarray(inputs["L1_attn_Bo"]),
    )
    l1w.update(l1c)
    res2 = run_bass_kernel_spmd(_PROGS["l1attn"].nc, [dict(l1w, xin_fm=x2[b]) for b in range(B)], core_ids=list(range(B)))
    x3 = [res2.results[b]["xout_fm"] for b in range(B)]
    res3 = run_bass_kernel_spmd(
        _PROGS["l1ffn"].nc,
        [l1_ffn_inmap(x3[b], inputs["L1_ffn_w1"], inputs["L1_ffn_b1"],
                      inputs["L1_ffn_w2"], inputs["L1_ffn_b2"]) for b in range(B)],
        core_ids=list(range(B)))
    return np.stack([np.asarray(res3.results[b]["xout_fm"]).T for b in range(B)], axis=0).astype(np.float32)


def _kernel_unfused(**inputs):
    P = _progs()
    mfwd, minv_stack, ffm = P["l0c"]
    l1c = P["l1c"]
    B = inputs["L0_attn_x"].shape[0]
    x = [np.ascontiguousarray(inputs["L0_attn_x"][b].T) for b in range(B)]

    xi = [r["xi_fm"] for r in _run(P["pre"], [{"xin_fm": x[b]} for b in range(B)])]
    for li in range(4):
        awp, abp = perm_qkv(inputs["L0_attn_attn_w"][li], inputs["L0_attn_attn_b"][li])
        lw = {
            "attn_w": awp, "attn_bf": fmvec(abp),
            "ln1wf": fmvec(inputs["L0_attn_ln1_w"][li]),
            "ln1bf": fmvec(inputs["L0_attn_ln1_b"][li]),
        }
        w = [r["w_fm"] for r in _run(P["qkv"], [dict(lw, xi_fm=xi[b]) for b in range(B)])]
        sw = {
            "mfwd": mfwd, "minv": minv_stack, "ffm": ffm,
            "aproj_w": np.ascontiguousarray(inputs["L0_attn_aproj_w"][li]),
            "aprojbf": fmvec(inputs["L0_attn_aproj_b"][li]),
        }
        xi = [r["xo_fm"] for r in _run(P["spec"], [dict(sw, xi_fm=xi[b], w_fm=w[b]) for b in range(B)])]
        fw = {
            "fc_w": np.ascontiguousarray(inputs["L0_attn_fc_w"][li]),
            "fcbf": fmvec(inputs["L0_attn_fc_b"][li]),
            "ln2wf": fmvec(inputs["L0_attn_ln2_w"][li]),
            "ln2bf": fmvec(inputs["L0_attn_ln2_b"][li]),
            "mproj_w": np.ascontiguousarray(inputs["L0_attn_mproj_w"][li]),
            "mprojbf": fmvec(inputs["L0_attn_mproj_b"][li]),
        }
        xi = [r["xo_fm"] for r in _run(P["l0affn"], [dict(fw, xi_fm=xi[b]) for b in range(B)])]
    x1 = [r["xout_fm"] for r in _run(P["post"], [{"xa_fm": x[b], "xb_fm": xi[b]} for b in range(B)])]

    x2 = [r["xout_fm"] for r in _run(
        P["l0ffn"],
        [l0_ffn_inmap(x1[b], inputs["L0_ffn_W1"], inputs["L0_ffn_b1"],
                      inputs["L0_ffn_W2"], inputs["L0_ffn_b2"], inputs["L0_ffn_scale"])
         for b in range(B)])]

    l1w = dict(
        wq=np.ascontiguousarray(inputs["L1_attn_Wq"]), wk=np.ascontiguousarray(inputs["L1_attn_Wk"]),
        wv=np.ascontiguousarray(inputs["L1_attn_Wv"]), wo=np.ascontiguousarray(inputs["L1_attn_Wo"]),
        aq=np.ascontiguousarray(inputs["L1_attn_Aq"]), bq=np.ascontiguousarray(inputs["L1_attn_Bq"]),
        av=np.ascontiguousarray(inputs["L1_attn_Av"]), bv=np.ascontiguousarray(inputs["L1_attn_Bv"]),
        ao=np.ascontiguousarray(inputs["L1_attn_Ao"]), bo=np.ascontiguousarray(inputs["L1_attn_Bo"]),
    )
    l1w.update(l1c)
    x3 = [r["xout_fm"] for r in _run(P["l1attn"], [dict(l1w, xin_fm=x2[b]) for b in range(B)])]

    x4 = [r["xout_fm"] for r in _run(
        P["l1ffn"],
        [l1_ffn_inmap(x3[b], inputs["L1_ffn_w1"], inputs["L1_ffn_b1"],
                      inputs["L1_ffn_w2"], inputs["L1_ffn_b2"]) for b in range(B)])]

    return np.stack([np.asarray(x4[b]).T for b in range(B)], axis=0).astype(np.float32)
